# version 11
# baseline (speedup 1.0000x reference)
import numpy as np
import ml_dtypes
from contextlib import ExitStack

import concourse.bass as bass
import concourse.mybir as mybir
from concourse.bass_utils import run_bass_kernel_spmd
from concourse.alu_op_type import AluOpType as ALU

F32 = mybir.dt.float32
BF16 = mybir.dt.bfloat16
AF = mybir.ActivationFunctionType
AX = mybir.AxisListType.X
BF16NP = ml_dtypes.bfloat16

S = 4096
DM = 1024
NCH = 8
CH = 512
MT = 4
NG = 4
GT = 1024
PIF = float(np.pi)
EPS = 1e-5
DBG = False
DBG2 = False


class _Op:
    __slots__ = ("i", "eng", "fn", "r", "w", "dma", "dmai", "inc", "incv",
                 "deps", "tag", "flushed", "fend", "waits", "swb", "early")

    def __init__(self, i, eng, fn, r, w, dma, dmai, tag, swb=False, early=False):
        self.i, self.eng, self.fn = i, eng, fn
        self.r, self.w = tuple(r), tuple(w)
        self.dma, self.dmai = dma, dmai
        self.inc, self.incv = False, None
        self.deps, self.tag = [], tag
        self.flushed, self.fend, self.waits = False, None, []
        self.swb, self.early = swb, early


class Prog:
    CENG = ("pe", "act", "dve", "pool", "sp")
    QENG = ("sp", "act", "pool")

    NLANE = 8

    def __init__(self, nc, ctx):
        self.nc = nc
        self.ops = []
        self.pending = []
        self.last_w = {}
        self.readers = {}
        self.inc_cnt = {e: 0 for e in self.CENG}
        self.dma_cnt = {e: 0 for e in self.QENG}
        self.lane_cnt = {e: [0] * self.NLANE for e in self.QENG}
        self.sems = {e: ctx.enter_context(nc.semaphore(f"s_{e}")) for e in self.CENG}
        self.dsems = {e: [ctx.enter_context(nc.semaphore(f"d_{e}{i}"))
                          for i in range(self.NLANE)] for e in self.QENG}
        self.waited = {e: {} for e in self.CENG}
        self.nflush = 0
        self.barrier_req = {}

    def _track(self, o):
        deps = set()
        for k in o.r:
            j = self.last_w.get(k)
            if j is not None:
                deps.add(j)
            self.readers.setdefault(k, []).append(o.i)
        for k in o.w:
            j = self.last_w.get(k)
            if j is not None:
                deps.add(j)
            for rj in self.readers.get(k, ()):
                deps.add(rj)
            self.last_w[k] = o.i
            self.readers[k] = []
        deps.discard(o.i)
        o.deps = sorted(deps)

    def op(self, eng, fn, r=(), w=(), tag="", swb=False, early=False):
        o = _Op(len(self.ops), eng, fn, r, w, False, None, tag, swb, early)
        self._track(o)
        self.ops.append(o)
        self.pending.append(o)
        return o

    def dma(self, eng, fn, r=(), w=(), tag=""):
        assert eng in self.QENG
        lane = self.dma_cnt[eng] % self.NLANE
        self.dma_cnt[eng] += 1
        self.lane_cnt[eng][lane] += 1
        o = _Op(len(self.ops), eng, fn, r, w, True,
                (lane, self.lane_cnt[eng][lane]), tag)
        self._track(o)
        self.ops.append(o)
        self.pending.append(o)
        return o

    def barrier(self):
        assert not self.pending
        snap = {}
        for e2 in self.CENG:
            if self.inc_cnt[e2]:
                snap[("e", e2)] = self.inc_cnt[e2]
        for q in self.QENG:
            for lane in range(self.NLANE):
                if self.lane_cnt[q][lane]:
                    snap[("d", q, lane)] = 16 * self.lane_cnt[q][lane]
        for e in self.CENG:
            self.barrier_req[e] = dict(snap)

    def flush(self, name):
        for o in self.pending:
            for d in o.deps:
                dop = self.ops[d]
                if dop.dma or dop.flushed:
                    continue
                if dop.eng == o.eng and (dop.eng in ("pe", "sp")):
                    continue
                dop.inc = True
        for o in self.pending:
            if o.dma:
                continue
            if o.inc:
                self.inc_cnt[o.eng] += 1
                o.incv = self.inc_cnt[o.eng]
        last = {}
        for o in self.pending:
            if not o.dma:
                last[o.eng] = o
        for o in last.values():
            if o.incv is None:
                o.inc = True
                self.inc_cnt[o.eng] += 1
                o.incv = self.inc_cnt[o.eng]
        first_seen = set()
        for o in self.pending:
            wd = self.waited[o.eng]
            ws = {}
            if o.eng not in first_seen:
                first_seen.add(o.eng)
                breq = self.barrier_req.pop(o.eng, None)
                if breq:
                    for sem, val in breq.items():
                        if val > ws.get(sem, 0):
                            ws[sem] = val
            for d in o.deps:
                dop = self.ops[d]
                if dop.dma:
                    lane, cnt = dop.dmai
                    sem, val = ("d", dop.eng, lane), 16 * cnt
                elif dop.eng == o.eng and (dop.eng in ("pe", "sp")):
                    continue
                elif dop.flushed:
                    sem, val = ("e", dop.eng), dop.fend
                else:
                    assert dop.incv is not None, (dop.tag, o.tag)
                    sem, val = ("e", dop.eng), dop.incv
                if val > ws.get(sem, 0):
                    ws[sem] = val
            o.waits = [(sem, val) for sem, val in sorted(ws.items())
                       if wd.get(sem, 0) < val]
            for sem, val in o.waits:
                wd[sem] = val
        for o in self.pending:
            o.flushed = True
            o.fend = self.inc_cnt[o.eng] if not o.dma else None
        pend = self.pending
        self.pending = []
        self.nflush += 1
        secmap = {"sp": "sync", "pe": "tensor", "act": "scalar",
                  "dve": "vector", "pool": "gpsimd"}
        with self.nc.Block(f"f{self.nflush}_{name}") as blk:
            for engname in self.CENG:
                ops_e = [o for o in pend if o.eng == engname]
                if not ops_e:
                    continue

                def body(eng, ops_e=ops_e, engname=engname):
                    for o in ops_e:
                        for key, val in o.waits:
                            if key[0] == "d":
                                sem = self.dsems[key[1]][key[2]]
                            else:
                                sem = self.sems[key[1]]
                            eng.wait_ge(sem, val)
                        inst = o.fn(eng)
                        if o.dma:
                            inst.then_inc(self.dsems[o.eng][o.dmai[0]], 16)
                        elif o.inc:
                            assert inst is not None, o.tag
                            inst.then_inc(self.sems[o.eng], 1)

                getattr(blk, secmap[engname])(body)


def _prep_common(I):
    a = lambda x: np.ascontiguousarray(np.asarray(x, dtype=np.float32))
    c = {}
    c["wqT"] = a(np.asarray(I["L0_attn_Wq"]).T)
    c["wkT"] = a(np.asarray(I["L0_attn_Wk"]).T)
    c["wvT"] = a(np.asarray(I["L0_attn_Wv"]).T)
    c["woT"] = a(np.asarray(I["L0_attn_Wo"]).T)
    c["bq_c"] = a(np.asarray(I["L0_attn_bq"]).reshape(8, 128).T)
    c["bk_c"] = a(np.asarray(I["L0_attn_bk"]).reshape(8, 128).T)
    c["bv_row"] = a(np.asarray(I["L0_attn_bv"])[None, :])
    c["bo_row"] = a(np.asarray(I["L0_attn_bo"])[None, :])
    c["a_rt"] = a(np.transpose(np.asarray(I["L0_ffn_G1_R"])[0], (1, 2, 0)))
    c["a_it"] = a(np.transpose(np.asarray(I["L0_ffn_G1_I"])[0], (1, 2, 0)))
    c["b_rt"] = a(np.transpose(np.asarray(I["L0_ffn_G2_R"])[:, :, 0, :], (0, 2, 1)))
    c["b_it"] = a(np.transpose(np.asarray(I["L0_ffn_G2_I"])[:, :, 0, :], (0, 2, 1)))
    c["brl_row"] = a(np.asarray(I["L0_ffn_bias_real"])[None, :])
    c["bil_row"] = a(np.asarray(I["L0_ffn_bias_imag"])[None, :])
    c["nulog_row"] = a(np.asarray(I["L1_attn_nu_log"])[None, :])
    c["thetalog_row"] = a(np.asarray(I["L1_attn_theta_log"])[None, :])
    c["breT"] = a(np.asarray(I["L1_attn_B_re"]).T)
    c["bimT"] = a(np.asarray(I["L1_attn_B_im"]).T)
    c["creT"] = a(np.asarray(I["L1_attn_C_re"]).T)
    c["cimT"] = a(np.asarray(I["L1_attn_C_im"]).T)
    c["dskip_row"] = a(np.asarray(I["L1_attn_D_skip"])[None, :])
    c["rw"] = a(I["L1_ffn_router_w"])
    c["ebias_row"] = a(np.asarray(I["L1_ffn_expert_bias"])[None, :])
    c["w1"] = a(I["L1_ffn_w1"])
    c["w2"] = a(I["L1_ffn_w2"])
    c["w3"] = a(I["L1_ffn_w3"])
    c["sw1"] = a(I["L1_ffn_sw1"])
    c["sw2"] = a(I["L1_ffn_sw2"])
    c["sw3"] = a(I["L1_ffn_sw3"])
    c["id128"] = np.eye(128, dtype=BF16NP)
    c["id64d"] = np.vstack([np.eye(64), np.eye(64)]).astype(BF16NP)
    c["ut128"] = np.triu(np.ones((128, 128), dtype=np.float32))
    c["ut128b"] = np.triu(np.ones((128, 128), dtype=np.float32)).astype(BF16NP)
    c["ones_bf"] = np.ones((128, 1), dtype=BF16NP)
    c["onesrow_f"] = np.ones((1, 128), dtype=np.float32)
    c["onesrow_bf"] = np.ones((1, 128), dtype=BF16NP)
    c["iota1_row"] = np.arange(1, 129, dtype=np.float32)[None, :]
    e127 = np.zeros((128, 1), dtype=np.float32)
    e127[127, 0] = 1.0
    c["e127_f"] = e127
    c["e127_bf"] = e127.astype(BF16NP)
    c["eps_col"] = np.full((128, 1), EPS, dtype=np.float32)
    ecq = np.zeros((32, 512), dtype=np.float32)
    for q in range(4):
        for p in range(128):
            ecq[8 * q + p // 16, q * 128 + p] = 1.0
    c["ecq"] = ecq
    ecg = np.zeros((16, 128), dtype=np.float32)
    for p in range(128):
        ecg[p % 16, p] = 1.0
    c["ecg"] = ecg
    return c


_IN_SPECS = {
    "x0": ([S, DM], F32),
    "wqT": ([DM, DM], F32), "wkT": ([DM, DM], F32),
    "wvT": ([DM, DM], F32), "woT": ([DM, DM], F32),
    "bq_c": ([128, 8], F32), "bk_c": ([128, 8], F32),
    "bv_row": ([1, DM], F32), "bo_row": ([1, DM], F32),
    "a_rt": ([4, 32, 32], F32), "a_it": ([4, 32, 32], F32),
    "b_rt": ([4, 16, 16], F32), "b_it": ([4, 16, 16], F32),
    "brl_row": ([1, 512], F32), "bil_row": ([1, 512], F32),
    "nulog_row": ([1, DM], F32), "thetalog_row": ([1, DM], F32),
    "breT": ([DM, DM], F32), "bimT": ([DM, DM], F32),
    "creT": ([DM, DM], F32), "cimT": ([DM, DM], F32),
    "dskip_row": ([1, DM], F32),
    "rw": ([DM, 8], F32), "ebias_row": ([1, 8], F32),
    "w1": ([8, DM, DM], F32), "w2": ([8, DM, DM], F32), "w3": ([8, DM, DM], F32),
    "sw1": ([DM, DM], F32), "sw2": ([DM, DM], F32), "sw3": ([DM, DM], F32),
    "id128": ([128, 128], BF16), "id64d": ([128, 64], BF16),
    "ut128": ([128, 128], F32), "ut128b": ([128, 128], BF16),
    "ones_bf": ([128, 1], BF16),
    "onesrow_f": ([1, 128], F32), "onesrow_bf": ([1, 128], BF16),
    "iota1_row": ([1, 128], F32),
    "e127_f": ([128, 1], F32), "e127_bf": ([128, 1], BF16),
    "eps_col": ([128, 1], F32),
    "ecq": ([32, 512], F32), "ecg": ([16, 128], F32),
}


def build_nc(phases=("P0", "AB", "C", "D")):
    nc = bass.Bass("TRN2", target_bir_lowering=False)
    DI = {k: nc.dram_tensor(k, shp, dt, kind="ExternalInput").ap()
          for k, (shp, dt) in _IN_SPECS.items()}
    out_d = nc.dram_tensor("out", [S, DM], F32, kind="ExternalOutput").ap()
    xst_d = nc.dram_tensor("xst", [S, DM], F32, kind="Internal").ap()
    dbg_d = (nc.dram_tensor("dbg", [128, 16384], F32, kind="ExternalOutput").ap()
             if (DBG or DBG2) else None)

    ctx = ExitStack()
    with ctx:
        P = Prog(nc, ctx)
        sb = lambda name, shp, dt: ctx.enter_context(nc.sbuf_tensor(name, shp, dt))
        ps = lambda name, shp, dt: ctx.enter_context(nc.psum_tensor(name, shp, dt))

        pf = [ps(f"pf{i}", [128, 512], F32) for i in range(6)]
        pb = [ps(f"pb{i}", [128, 1024], BF16) for i in range(2)]

        id128 = sb("t_id128", [128, 128], BF16)
        id64d = sb("t_id64d", [128, 64], BF16)
        ut128 = sb("t_ut128", [128, 128], F32)
        ut128b = sb("t_ut128b", [128, 128], BF16)
        ones_bf = sb("t_ones_bf", [128, 1], BF16)
        onesrow_f = sb("t_onesrow_f", [1, 128], F32)
        onesrow_bf = sb("t_onesrow_bf", [1, 128], BF16)
        iota1_row = sb("t_iota1_row", [1, 128], F32)
        e127_f = sb("t_e127_f", [128, 1], F32)
        e127_bf = sb("t_e127_bf", [128, 1], BF16)
        eps_col = sb("t_eps_col", [128, 1], F32)
        lnst = sb("lnst", [128, 12], F32)
        lnmv = sb("lnmv", [128, 2], F32)
        lnlv = sb("lnlv", [128, 1], F32)
        lnrstd = sb("lnrstd", [128, 1], F32)
        lnnmr = sb("lnnmr", [128, 1], F32)
        lnscr2 = sb("lnscr2", [128, 1024], F32)
        dbgs = sb("dbgs", [128, 80], F32) if DBG else None
        mtb = [sb(f"mtb{i}", [128, 128], BF16) for i in range(2)]

        def dbgdump(col, ap, keys, w, always=False):
            if not (DBG or (DBG2 and always)):
                return
            P.dma("pool", lambda e, ap=ap, col=col, w=w: e.dma_start(
                    out=dbg_d[:, col:col + w], in_=ap,
                    allow_slow_non_contiguous=True),
                  r=tuple(keys), w=("dbg_dram",), tag=f"dbg{col}")

        def ldc(name, tile, cast=False):
            q = "pool" if cast else "sp"
            P.dma(q, lambda e, t=tile, d=DI[name]: e.dma_start(out=t[:], in_=d),
                  w=(name,), tag=f"ld_{name}")

        def ln(x_ap, xkeys, outs):
            xkeys = tuple(xkeys)
            P.op("dve", lambda e: e.bn_stats(lnst[:, 0:6], x_ap[:, 0:512]),
                 r=xkeys, w=("lnst",), tag="ln_s0", swb=True)
            P.op("dve", lambda e: e.bn_stats(lnst[:, 6:12], x_ap[:, 512:1024]),
                 r=xkeys, w=("lnst",), tag="ln_s1", swb=True)
            P.op("dve", lambda e: e.bn_aggr(lnmv[:], lnst[:]),
                 r=("lnst",), w=("lnmv",), tag="ln_ag", swb=True)
            P.op("act", lambda e: e.activation(out=lnlv[:], in_=lnmv[:, 1:2],
                                               func=AF.Ln, bias=eps_col[:],
                                               scale=1.0),
                 r=("lnmv", "epsc"), w=("lnlv",), tag="ln_lnv")
            P.op("act", lambda e: e.activation(out=lnrstd[:], in_=lnlv[:],
                                               func=AF.Exp, scale=-0.5),
                 r=("lnlv",), w=("lnrstd",), tag="ln_rstd")
            P.op("dve", lambda e: e.tensor_scalar(
                    out=lnnmr[:], in0=lnmv[:, 0:1], scalar1=lnrstd[:],
                    scalar2=-1.0, op0=ALU.mult, op1=ALU.mult),
                 r=("lnmv", "lnrstd"), w=("lnnmr",), tag="ln_nmr", early=True)
            for (o_ap, o_key) in outs:
                P.op("act", lambda e, o_ap=o_ap: e.activation(
                        out=o_ap, in_=x_ap, func=AF.Identity,
                        scale=lnrstd[:], bias=lnnmr[:]),
                     r=xkeys + ("lnrstd", "lnnmr"), w=(o_key,), tag="ln_apply")

        def ln_stats(x_ap, xkeys, stb, stbk, sm):
            xkeys = tuple(xkeys)
            P.op("dve", lambda e: e.bn_stats(lnst[:, 0:6], x_ap[:, 0:512]),
                 r=xkeys, w=("lnst",), tag="lns_s0")
            P.op("dve", lambda e: e.bn_stats(lnst[:, 6:12], x_ap[:, 512:1024]),
                 r=xkeys, w=("lnst",), tag="lns_s1")
            P.op("dve", lambda e: e.bn_aggr(lnmv[:], lnst[:]),
                 r=("lnst",), w=("lnmv",), tag="lns_ag")
            P.op("act", lambda e: e.activation(out=lnlv[:], in_=lnmv[:, 1:2],
                                               func=AF.Ln, bias=eps_col[:],
                                               scale=1.0),
                 r=("lnmv", "epsc"), w=("lnlv",), tag="lns_lnv")
            P.op("act", lambda e: e.activation(
                    out=stb[:, 2 * sm:2 * sm + 1], in_=lnlv[:],
                    func=AF.Exp, scale=-0.5),
                 r=("lnlv",), w=(stbk,), tag="lns_rstd")
            P.op("dve", lambda e: e.tensor_scalar(
                    out=stb[:, 2 * sm + 1:2 * sm + 2], in0=lnmv[:, 0:1],
                    scalar1=stb[:, 2 * sm:2 * sm + 1],
                    scalar2=-1.0, op0=ALU.mult, op1=ALU.mult),
                 r=("lnmv", stbk), w=(stbk,), tag="lns_nmr")

        def ln_apply(x_ap, xkeys, o_ap, o_key, stb, stbk, sm):
            P.op("act", lambda e: e.activation(
                    out=o_ap, in_=x_ap, func=AF.Identity,
                    scale=stb[:, 2 * sm:2 * sm + 1],
                    bias=stb[:, 2 * sm + 1:2 * sm + 2]),
                 r=tuple(xkeys) + (stbk,), w=(o_key,), tag="lns_apply")

        def transpose_pack(src_fn, srckeys, bank, bankkey, n=8):
            for j in range(n):
                P.op("pe", lambda e, j=j: e.matmul(
                        bank[:, j * 128:(j + 1) * 128], lhsT=src_fn(j),
                        rhs=id128[:], is_transpose=True,
                        start=(j == 0), stop=(j == n - 1)),
                     r=tuple(srckeys) + ("id128",), w=(bankkey,), tag="tpack")

        def repl_row(row_ap, rowkey, dst, dstkey, width, dve=False):
            nh = (width + 511) // 512
            for h in range(nh):
                wgt = min(512, width - h * 512)
                bank = pf[4 + (h % 2)]
                P.op("pe", lambda e, h=h, wgt=wgt, bank=bank: e.matmul(
                        bank[:, 0:wgt], lhsT=onesrow_f[:],
                        rhs=row_ap[0:1, h * 512:h * 512 + wgt],
                        start=True, stop=True),
                     r=(rowkey, "onesrow_f"), w=(f"PF{4 + h % 2}",), tag="repl_mm")
                eng = "dve" if dve else "act"
                if dve:
                    P.op("dve", lambda e, h=h, wgt=wgt, bank=bank: e.tensor_copy(
                            dst[:, h * 512:h * 512 + wgt], bank[:, 0:wgt]),
                         r=(f"PF{4 + h % 2}",), w=(dstkey,), tag="repl_cp")
                else:
                    P.op("act", lambda e, h=h, wgt=wgt, bank=bank: e.copy(
                            out=dst[:, h * 512:h * 512 + wgt], in_=bank[:, 0:wgt]),
                         r=(f"PF{4 + h % 2}",), w=(dstkey,), tag="repl_cp")

        abctx = ExitStack()
        absb = lambda name, shp, dt: abctx.enter_context(
            nc.sbuf_tensor(name, shp, dt))
        wq_sb = absb("wq_sb", [128, 8192], BF16)
        wk_sb = absb("wk_sb", [128, 8192], BF16)
        wv_sb = absb("wv_sb", [128, 8192], BF16)
        wo_sb = absb("wo_sb", [128, 8192], BF16)
        bqc_sb = absb("bqc_sb", [128, 8], F32)
        bkc_sb = absb("bkc_sb", [128, 8], F32)
        bvb = absb("bvb", [1, 1024], BF16)
        bob = absb("bob", [1, 1024], BF16)
        brlb = absb("brlb", [1, 512], BF16)
        bilb = absb("bilb", [1, 512], BF16)
        wrT = absb("wrT", [128, 2048], BF16)
        wiT = absb("wiT", [128, 2048], BF16)
        wiTn = absb("wiTn", [128, 2048], BF16)

        if "P0" in phases:
            p0ctx = ExitStack()
            p0sb = lambda name, shp, dt: p0ctx.enter_context(
                nc.sbuf_tensor(name, shp, dt))
            for nm, t in (("id128", id128), ("id64d", id64d), ("ut128", ut128),
                          ("ut128b", ut128b),
                          ("ones_bf", ones_bf), ("onesrow_f", onesrow_f),
                          ("onesrow_bf", onesrow_bf),
                          ("iota1_row", iota1_row), ("e127_f", e127_f),
                          ("e127_bf", e127_bf)):
                ldc(nm, t)
            P.dma("sp", lambda e: e.dma_start(out=eps_col[:], in_=DI["eps_col"]),
                  w=("epsc",), tag="ld_eps")
            for nm, t in (("wqT", wq_sb), ("wkT", wk_sb), ("wvT", wv_sb),
                          ("woT", wo_sb)):
                P.dma("pool", lambda e, t=t, d=DI[nm]: e.dma_start(
                        out=t[:].rearrange("p (j o) -> p j o", j=8),
                        in_=d.rearrange("(j p) o -> p j o", p=128)),
                      w=(nm,), tag=f"ld_{nm}")
            ldc("bq_c", bqc_sb)
            ldc("bk_c", bkc_sb)
            ldc("bv_row", bvb, cast=True)
            ldc("bo_row", bob, cast=True)
            ldc("brl_row", brlb, cast=True)
            ldc("bil_row", bilb, cast=True)

            art_sb = p0sb("art_sb", [32, 128], F32)
            brt_sb = p0sb("brt_sb", [16, 64], F32)
            ait_sb = p0sb("ait_sb", [32, 128], F32)
            bit_sb = p0sb("bit_sb", [16, 64], F32)
            ecq_sb = p0sb("ecq_sb", [32, 512], F32)
            ecg_sb = p0sb("ecg_sb", [16, 128], F32)
            for nm, t, pat in (("a_rt", art_sb, "c d b -> d c b"),
                               ("a_it", ait_sb, "c d b -> d c b"),
                               ("b_rt", brt_sb, "c g e -> g c e"),
                               ("b_it", bit_sb, "c g e -> g c e")):
                P.dma("sp", lambda e, t=t, d=DI[nm], pat=pat, nm=nm: e.dma_start(
                        out=t[:].rearrange("p (c x) -> p c x", c=4),
                        in_=d.rearrange(pat)),
                      w=(nm,), tag=f"ld_{nm}")
            ldc("ecq", ecq_sb)
            ldc("ecg", ecg_sb)
            wtmpA = p0sb("wtmpA", [128, 32], F32)
            wtmpB = p0sb("wtmpB", [128, 16], F32)
            wtmpM = p0sb("wtmpM", [128, 512], F32)
            wacc = p0sb("wacc", [128, 512], F32)

            def wbuild(a_sb, b_sb, akey, bkey, dstT, dstTn, dkey, dnkey):
                for q in range(4):
                    for cc in range(4):
                        bA = pf[2 * (cc % 2)]
                        bAk = f"PF{2 * (cc % 2)}"
                        bB = pf[1 + 2 * (cc % 2)]
                        bBk = f"PF{1 + 2 * (cc % 2)}"
                        P.op("pe", lambda e, q=q, cc=cc, bA=bA: e.matmul(
                                bA[:, 0:32], lhsT=ecq_sb[:, q * 128:(q + 1) * 128],
                                rhs=a_sb[:, cc * 32:(cc + 1) * 32],
                                start=True, stop=True),
                             r=(akey, "ecq"), w=(bAk), tag="wb_A")
                        P.op("pe", lambda e, cc=cc, bB=bB: e.matmul(
                                bB[:, 0:16], lhsT=ecg_sb[:],
                                rhs=b_sb[:, cc * 16:(cc + 1) * 16],
                                start=True, stop=True),
                             r=(bkey, "ecg"), w=(bBk), tag="wb_B")
                        P.op("act", lambda e, bA=bA: e.copy(
                                out=wtmpA[:], in_=bA[:, 0:32]),
                             r=(bAk), w=("wb_A"), tag="wb_Aev")
                        P.op("dve", lambda e, bB=bB: e.tensor_copy(
                                wtmpB[:], bB[:, 0:16]),
                             r=(bBk), w=("wb_B"), tag="wb_Bev")
                        avb = wtmpA[:].rearrange("p (b one) -> p b one",
                                                 one=1).broadcast_to([128, 32, 16])
                        bvb = wtmpB[:].rearrange("p (one e) -> p one e",
                                                 one=1).broadcast_to([128, 32, 16])
                        if cc == 0:
                            ov = wacc[:].rearrange("p (b e) -> p b e", e=16)
                            P.op("dve", lambda e, avb=avb, bvb=bvb, ov=ov:
                                 e.tensor_tensor(out=ov, in0=avb, in1=bvb,
                                                 op=ALU.mult),
                                 r=("wb_A", "wb_B"), w=("wb_acc"),
                                 tag="wb_mul")
                        else:
                            mv = wtmpM[:].rearrange("p (b e) -> p b e", e=16)
                            P.op("dve", lambda e, avb=avb, bvb=bvb, mv=mv:
                                 e.tensor_tensor(out=mv, in0=avb, in1=bvb,
                                                 op=ALU.mult),
                                 r=("wb_A", "wb_B"), w=("wb_M"),
                                 tag="wb_mul")
                            P.op("pool", lambda e: e.tensor_tensor(
                                    out=wacc[:], in0=wacc[:], in1=wtmpM[:],
                                    op=ALU.add),
                                 r=("wb_M"), w=("wb_acc"), tag="wb_add")
                    P.op("dve", lambda e, q=q: e.tensor_copy(
                            dstT[:, q * 512:(q + 1) * 512], wacc[:]),
                         r=("wb_acc"), w=(dkey), tag="wb_cast")
                    if dstTn is not None:
                        P.op("act", lambda e, q=q: e.mul(
                                out=dstTn[:, q * 512:(q + 1) * 512], in_=wacc[:],
                                mul=-1.0),
                             r=("wb_acc"), w=(dnkey), tag="wb_castn")

            wbuild(art_sb, brt_sb, "a_rt", "b_rt", wrT, None, "wrT", None)
            wbuild(ait_sb, bit_sb, "a_it", "b_it", wiT, wiTn, "wiT", "wiTn")
            dbgdump(0, wrT[:], ("wrT",), 2048, always=True)
            dbgdump(2048, wiTn[:], ("wiTn",), 2048, always=True)
            P.flush("P0")
            p0ctx.close()
            P.barrier()

        if "AB" in phases:
            x_sb = [absb(f"xA{i}", [128, 4096], F32) for i in range(2)]
            nbf2 = [absb(f"nbf{i}", [128, 1024], BF16) for i in range(2)]
            stAB = absb("stAB", [128, 8], F32)
            n1T = absb("n1T", [128, 4096], BF16)
            q_sb = absb("q_sb", [128, 4096], BF16)
            k_sb = absb("k_sb", [128, 4096], BF16)
            v_sb = absb("v_sb", [128, 4096], BF16)
            attn_bf = [absb(f"attnbf{i}", [128, 1024], BF16) for i in range(4)]
            attnT = absb("attnT", [128, 1024], BF16)
            y_sb = absb("yA", [128, 1024], F32)
            n3T = absb("n3T", [128, 1024], BF16)
            f_sb = absb("fA", [128, 1024], F32)
            S_sb = absb("S_sb", [128, 520], F32)
            S_bf = absb("S_bf", [128, 520], BF16)
            rden = absb("rden", [128, 1], F32)
            ktp_sb = absb("ktp_sb", [128, 128], BF16)

            P.op("dve", lambda e: e.memset(S_sb[:], 0.0), w=("S_sb",), tag="S0",
                 swb=True)
            P.op("dve", lambda e: e.memset(S_bf[:], 0.0), w=("S_bf",), tag="Sbf0",
                 swb=True)

            def ab_attn_stage(c, m):
                par = c % 2
                apar = m % 4
                abf = attn_bf[apar]
                for jp in range(8):
                    for pi in range(2):
                        b0 = pi * 64
                        h = 2 * jp + pi
                        qh = q_sb[b0:b0 + 64,
                                  jp * 512 + m * 128:jp * 512 + m * 128 + 128]
                        kh = k_sb[b0:b0 + 64,
                                  jp * 512 + m * 128:jp * 512 + m * 128 + 128]
                        vh = v_sb[:, m * 1024 + h * 64:m * 1024 + h * 64 + 64]
                        sc_bank = pf[pi]
                        sck = f"PF{pi}"
                        nd_bank = pf[2 + pi]
                        ndk = f"PF{2 + pi}"
                        P.op("pe", lambda e, kh=kh, qh=qh, sc_bank=sc_bank:
                             e.matmul(sc_bank[:, 0:128], lhsT=kh, rhs=qh,
                                      start=True, stop=True),
                             r=("q_sb", "k_sb"), w=(sck,), tag="scT")
                        P.op("dve", lambda e, sc_bank=sc_bank, pi=pi:
                             e.tensor_tensor(out=mtb[pi][:],
                                             in0=sc_bank[:, 0:128],
                                             in1=ut128[:], op=ALU.mult),
                             r=(sck, "ut128"), w=(f"mtb{pi}",), tag="mask")
                        P.op("pe", lambda e, pi=pi, vh=vh, nd_bank=nd_bank:
                             e.matmul(nd_bank[:, 0:64], lhsT=mtb[pi][:],
                                      rhs=vh, start=True, stop=False),
                             r=(f"mtb{pi}", "v_sb"), w=(ndk,), tag="num1")
                        P.op("pe", lambda e, qh=qh, jp=jp, b0=b0,
                             nd_bank=nd_bank: e.matmul(
                                nd_bank[:, 0:64], lhsT=qh,
                                rhs=S_bf[b0:b0 + 64, jp * 65:jp * 65 + 64],
                                start=False, stop=False),
                             r=("q_sb", "S_bf"), w=(ndk,), tag="num2")
                        P.op("pe", lambda e, pi=pi, nd_bank=nd_bank:
                             e.matmul(nd_bank[:, 64:65], lhsT=mtb[pi][:],
                                      rhs=ones_bf[:], start=False, stop=False),
                             r=(f"mtb{pi}", "ones_bf"), w=(ndk,), tag="den1")
                        P.op("pe", lambda e, qh=qh, jp=jp, b0=b0,
                             nd_bank=nd_bank: e.matmul(
                                nd_bank[:, 64:65], lhsT=qh,
                                rhs=S_bf[b0:b0 + 64, jp * 65 + 64:jp * 65 + 65],
                                start=False, stop=True),
                             r=("q_sb", "S_bf"), w=(ndk,), tag="den2")
                        P.op("pe", lambda e, kh=kh, b0=b0, pi=pi: e.matmul(
                                pb[1][:, pi * 64:pi * 64 + 64], lhsT=kh,
                                rhs=id64d[b0:b0 + 64, :], is_transpose=True,
                                start=(pi == 0), stop=(pi == 1)),
                             r=("k_sb", "id64d"), w=("PB1",), tag="kT")
                        P.op("dve", lambda e, nd_bank=nd_bank: e.reciprocal(
                                out=rden[:], in_=nd_bank[:, 64:65]),
                             r=(ndk,), w=("rden",), tag="rden", swb=True)
                        P.op("dve", lambda e, abf=abf, h=h, nd_bank=nd_bank,
                             apar=apar: e.tensor_scalar(
                                out=abf[:, h * 64:h * 64 + 64],
                                in0=nd_bank[:, 0:64], scalar1=rden[:],
                                scalar2=None, op0=ALU.mult),
                             r=(ndk, "rden"), w=(f"attnbf{apar}",), tag="attno",
                             early=True)
                    P.op("act", lambda e: e.copy(out=ktp_sb[:],
                                                 in_=pb[1][:, 0:128]),
                         r=("PB1",), w=("ktp",), tag="ktp_ev")
                    for pi in range(2):
                        b0 = pi * 64
                        h = 2 * jp + pi
                        vh = v_sb[:, m * 1024 + h * 64:m * 1024 + h * 64 + 64]
                        P.op("pe", lambda e, pi=pi, b0=b0, vh=vh: e.matmul(
                                pf[4][b0:b0 + 64, 0:64],
                                lhsT=ktp_sb[:, pi * 64:pi * 64 + 64], rhs=vh,
                                start=True, stop=False,
                                tile_position=(0, b0)),
                             r=("ktp", "v_sb"), w=("PF4",), tag="dS")
                        P.op("pe", lambda e, pi=pi, b0=b0: e.matmul(
                                pf[4][b0:b0 + 64, 64:65],
                                lhsT=ktp_sb[:, pi * 64:pi * 64 + 64],
                                rhs=ones_bf[:], start=False, stop=(pi == 1),
                                tile_position=(0, b0)),
                             r=("ktp", "ones_bf"), w=("PF4",), tag="dz")
                    P.op("dve", lambda e, jp=jp: e.tensor_tensor(
                            out=S_bf[:, jp * 65:(jp + 1) * 65],
                            in0=S_sb[:, jp * 65:(jp + 1) * 65],
                            in1=pf[4][:, 0:65], op=ALU.add),
                         r=("S_sb", "PF4"), w=("S_bf",), tag="Snew")
                    P.op("dve", lambda e, jp=jp: e.tensor_tensor(
                            out=S_sb[:, jp * 65:(jp + 1) * 65],
                            in0=S_sb[:, jp * 65:(jp + 1) * 65],
                            in1=pf[4][:, 0:65], op=ALU.add),
                         r=("S_sb", "PF4"), w=("S_sb",), tag="Supd")

            def ab_post_stage(c, m):
                par = c % 2
                xk = f"xA{par}"
                xt = x_sb[par]
                apar = m % 4
                abf = attn_bf[apar]
                xv = xt[:, m * 1024:(m + 1) * 1024]
                transpose_pack(lambda j, abf=abf: abf[:, j * 128:(j + 1) * 128],
                               (f"attnbf{apar}",), pb[0], "PB0")
                P.op("act", lambda e: e.copy(out=attnT[:], in_=pb[0][:]),
                     r=("PB0",), w=("attnT",), tag="attnT_ev")
                for hf in range(2):
                    for j in range(8):
                        P.op("pe", lambda e, j=j, hf=hf:
                             e.matmul(pf[5][:],
                                      lhsT=attnT[:, j * 128:(j + 1) * 128],
                                      rhs=wo_sb[:, j * 1024 + hf * 512:
                                                j * 1024 + hf * 512 + 512],
                                      start=(j == 0), stop=False),
                             r=("attnT", "woT"), w=("PF5",), tag="omm")
                    P.op("pe", lambda e, hf=hf: e.matmul(
                            pf[5][:], lhsT=onesrow_bf[:],
                            rhs=bob[0:1, hf * 512:hf * 512 + 512],
                            start=False, stop=True),
                         r=("bo_row", "onesrow_bf"), w=("PF5",), tag="obmm")
                    P.op("act", lambda e, hf=hf: e.copy(
                            out=y_sb[:, hf * 512:(hf + 1) * 512], in_=pf[5][:]),
                         r=("PF5",), w=("yA",), tag="oev")
                ln(y_sb[:], ("yA",), [(lnscr2[:], "lnscr2")])
                P.op("dve", lambda e, xv=xv: e.tensor_tensor(
                        out=xv, in0=xv, in1=lnscr2[:], op=ALU.add),
                     r=("lnscr2",), w=(xk,), tag="xadd1")
                nb = nbf2[m % 2]
                nbk = f"nbf{m % 2}"
                ln(xv, (xk,), [(nb[:], nbk)])
                transpose_pack(lambda j, nb=nb: nb[:, j * 128:(j + 1) * 128],
                               (nbk,), pb[0], "PB0")
                P.op("act", lambda e: e.copy(out=n3T[:], in_=pb[0][:]),
                     r=("PB0",), w=("n3T",), tag="n3T_ev")
                for half, w_lo, wk_lo, w_hi, wk_hi, bbr, brk in (
                        (0, wrT, "wrT", wiTn, "wiTn", brlb, "brl_row"),
                        (1, wiT, "wiT", wrT, "wrT", bilb, "bil_row")):
                    for j in range(8):
                        wsb = w_lo if j < 4 else w_hi
                        wky = wk_lo if j < 4 else wk_hi
                        qq = j if j < 4 else j - 4
                        P.op("pe", lambda e, j=j, qq=qq, wsb=wsb:
                             e.matmul(pf[5][:],
                                      lhsT=n3T[:, j * 128:(j + 1) * 128],
                                      rhs=wsb[:, qq * 512:(qq + 1) * 512],
                                      start=(j == 0), stop=False),
                             r=("n3T", wky), w=("PF5",), tag="bmm")
                    P.op("pe", lambda e, bbr=bbr: e.matmul(
                            pf[5][:], lhsT=onesrow_bf[:], rhs=bbr[0:1, :],
                            start=False, stop=True),
                         r=(brk, "onesrow_bf"), w=("PF5",), tag="bbmm")
                    P.op("act", lambda e, half=half: e.copy(
                            out=f_sb[:, half * 512:(half + 1) * 512],
                            in_=pf[5][:]),
                         r=("PF5",), w=("fA",), tag="bev")
                ln(f_sb[:], ("fA",), [(lnscr2[:], "lnscr2")])
                P.op("dve", lambda e, xv=xv: e.tensor_tensor(
                        out=xv, in0=xv, in1=lnscr2[:], op=ALU.add),
                     r=("lnscr2",), w=(xk,), tag="xadd2")

            def ab_store(c):
                par = c % 2
                xk = f"xA{par}"
                xt = x_sb[par]
                dst = xst_d[c * CH:(c + 1) * CH, :].rearrange(
                    "(m p) d -> p m d", p=128)
                P.dma("act", lambda e, xt=xt, dst=dst: e.dma_start(
                        out=dst, in_=xt[:].rearrange("p (m d) -> p m d", m=MT)),
                      r=(xk,), w=(f"xst{c}",), tag=f"stxA{c}")

            for c in range(NCH):
                par = c % 2
                xk = f"xA{par}"
                xt = x_sb[par]
                src = DI["x0"][c * CH:(c + 1) * CH, :].rearrange(
                    "(m p) d -> p m d", p=128)
                P.dma("sp", lambda e, xt=xt, src=src: e.dma_start(
                        out=xt[:].rearrange("p (m d) -> p m d", m=MT), in_=src),
                      r=("x0_dram",), w=(xk,), tag=f"ldxA{c}")
                for m in range(MT):
                    xv = xt[:, m * 1024:(m + 1) * 1024]
                    ln_stats(xv, (xk,), stAB, "stAB", m)
                for m in range(MT):
                    xv = xt[:, m * 1024:(m + 1) * 1024]
                    nb = nbf2[m % 2]
                    nbk = f"nbf{m % 2}"
                    ln_apply(xv, (xk,), nb[:], nbk, stAB, "stAB", m)
                    tb = pb[m % 2]
                    transpose_pack(lambda j, nb=nb: nb[:, j * 128:(j + 1) * 128],
                                   (nbk,), tb, f"PB{m % 2}")
                    dv = n1T[:].rearrange("p (j t) -> p j t", j=8)[
                        :, :, m * 128:(m + 1) * 128]
                    sv = tb[:].rearrange("p (j t) -> p j t", j=8)
                    P.op("act", lambda e, dv=dv, sv=sv: e.copy(out=dv, in_=sv),
                         r=(f"PB{m % 2}",), w=("n1T",), tag="n1T_ev")
                for nm, wsb, wkey, bsb, bkey, dst, dkey in (
                        ("q", wq_sb, "wqT", bqc_sb, "bq_c", q_sb, "q_sb"),
                        ("k", wk_sb, "wkT", bkc_sb, "bk_c", k_sb, "k_sb")):
                    for jo in range(8):
                        bank = pf[jo % 2]
                        bk_key = f"PF{jo % 2}"
                        for ji in range(8):
                            P.op("pe", lambda e, jo=jo, ji=ji, wsb=wsb, bank=bank:
                                 e.matmul(bank[:],
                                          lhsT=wsb[:, ji * 1024 + jo * 128:
                                                   ji * 1024 + jo * 128 + 128],
                                          rhs=n1T[:, ji * 512:(ji + 1) * 512],
                                          start=(ji == 0), stop=(ji == 7)),
                                 r=(wkey, "n1T"), w=(bk_key,), tag=f"{nm}mm")
                        P.op("act", lambda e, jo=jo, bsb=bsb, dst=dst, bank=bank:
                             e.activation(out=dst[:, jo * 512:(jo + 1) * 512],
                                          in_=bank[:], func=AF.Exp,
                                          bias=bsb[:, jo:jo + 1], scale=1.0),
                             r=(bk_key, bkey), w=(dkey,), tag=f"{nm}exp")
                for m in range(MT):
                    for hf in range(2):
                        bank = pf[2 + hf]
                        bk_key = f"PF{2 + hf}"
                        for ji in range(8):
                            P.op("pe", lambda e, m=m, hf=hf, ji=ji, bank=bank:
                                 e.matmul(bank[:],
                                          lhsT=n1T[:, ji * 512 + m * 128:
                                                   ji * 512 + m * 128 + 128],
                                          rhs=wv_sb[:, ji * 1024 + hf * 512:
                                                    ji * 1024 + hf * 512 + 512],
                                          start=(ji == 0), stop=False),
                                 r=("wvT", "n1T"), w=(bk_key,), tag="vmm")
                        P.op("pe", lambda e, hf=hf, bank=bank: e.matmul(
                                bank[:], lhsT=onesrow_bf[:],
                                rhs=bvb[0:1, hf * 512:hf * 512 + 512],
                                start=False, stop=True),
                             r=("bv_row", "onesrow_bf"), w=(bk_key,), tag="vbmm")
                        P.op("act", lambda e, m=m, hf=hf, bank=bank: e.copy(
                                out=v_sb[:, m * 1024 + hf * 512:
                                         m * 1024 + hf * 512 + 512],
                                in_=bank[:]),
                             r=(bk_key,), w=("v_sb",), tag="vev")
                ab_attn_stage(c, 0)
                if c > 0:
                    ab_post_stage(c - 1, 2)
                ab_attn_stage(c, 1)
                if c > 0:
                    ab_post_stage(c - 1, 3)
                    ab_store(c - 1)
                ab_attn_stage(c, 2)
                ab_post_stage(c, 0)
                ab_attn_stage(c, 3)
                ab_post_stage(c, 1)
            ab_post_stage(NCH - 1, 2)
            ab_post_stage(NCH - 1, 3)
            ab_store(NCH - 1)
            P.flush("AB")
            abctx.close()
            P.barrier()

        if "C" in phases:
            cctx = ExitStack()
            csb = lambda name, shp, dt: cctx.enter_context(
                nc.sbuf_tensor(name, shp, dt))
            bre_sb = csb("bre_sb", [128, 8192], BF16)
            bim_sb = csb("bim_sb", [128, 8192], BF16)
            cre_sb = csb("cre_sb", [128, 8192], BF16)
            cim_sb = csb("cim_sb", [128, 8192], BF16)
            lp_re = csb("lp_re", [128, 1024], F32)
            lp_im = csb("lp_im", [128, 1024], F32)
            li_re = csb("li_re", [128, 1024], F32)
            li_imN = csb("li_imN", [128, 1024], F32)
            drep = csb("drep", [128, 1024], F32)

            pctx = ExitStack()
            psb = lambda name, shp, dt: pctx.enter_context(
                nc.sbuf_tensor(name, shp, dt))
            nulog_sb = psb("nulog_sb", [1, 1024], F32)
            thlog_sb = psb("thlog_sb", [1, 1024], F32)
            dskip_sb = psb("dskip_sb", [1, 1024], F32)
            nu_row = psb("nu_row", [1, 1024], F32)
            th_row = psb("th_row", [1, 1024], F32)
            g_row = psb("g_row", [1, 1024], F32)
            lpm_sb = psb("lpm_sb", [128, 1024], F32)
            lim_sb = psb("lim_sb", [128, 1024], F32)
            sin_sb = psb("sin_sb", [128, 1024], F32)
            cos_sb = psb("cos_sb", [128, 1024], F32)
            cscrA = psb("cscrA", [128, 512], F32)
            cscrB = psb("cscrB", [128, 512], F32)
            grep = psb("grep", [128, 1024], F32)
            bstage = psb("bstage", [128, 2048], F32)
            uscr = psb("uscr_prep", [128, 1024], F32)

            ldc("nulog_row", nulog_sb)
            ldc("thetalog_row", thlog_sb)
            ldc("dskip_row", dskip_sb)
            P.op("act", lambda e: e.activation(out=nu_row[:], in_=nulog_sb[:],
                                               func=AF.Exp),
                 r=("nulog_row",), w=("nu_row",), tag="nu")
            P.op("act", lambda e: e.activation(out=th_row[:], in_=thlog_sb[:],
                                               func=AF.Exp),
                 r=("thetalog_row",), w=("th_row",), tag="th")
            for h2 in range(2):
                c0, c1 = h2 * 512, (h2 + 1) * 512
                P.op("pe", lambda e, c0=c0, c1=c1: e.matmul(
                        pf[0][:], lhsT=iota1_row[:], rhs=nu_row[0:1, c0:c1],
                        start=True, stop=True),
                     r=("nu_row", "iota1_row"), w=("PF0",), tag="tpn")
                P.op("act", lambda e, c0=c0, c1=c1: e.activation(
                        out=lpm_sb[:, c0:c1], in_=pf[0][:], func=AF.Exp,
                        scale=-1.0),
                     r=("PF0",), w=("lpm",), tag="lpmag")
                P.op("act", lambda e, c0=c0, c1=c1: e.activation(
                        out=lim_sb[:, c0:c1], in_=pf[0][:], func=AF.Exp,
                        scale=1.0),
                     r=("PF0",), w=("lim",), tag="limag")
                P.op("pe", lambda e, c0=c0, c1=c1: e.matmul(
                        pf[1][:], lhsT=iota1_row[:], rhs=th_row[0:1, c0:c1],
                        start=True, stop=True),
                     r=("th_row", "iota1_row"), w=("PF1",), tag="tpt")
                P.op("dve", lambda e: e.tensor_scalar(
                        out=cscrA[:], in0=pf[1][:], scalar1=PIF, scalar2=None,
                        op0=ALU.is_gt),
                     r=("PF1",), w=("cscrA",), tag="phmask")
                P.op("dve", lambda e: e.scalar_tensor_tensor(
                        out=cscrB[:], in0=cscrA[:], scalar=-2.0 * PIF,
                        in1=pf[1][:], op0=ALU.mult, op1=ALU.add),
                     r=("cscrA", "PF1"), w=("cscrB",), tag="phred")
                P.op("act", lambda e, c0=c0, c1=c1: e.activation(
                        out=sin_sb[:, c0:c1], in_=cscrB[:], func=AF.Sin),
                     r=("cscrB",), w=("sin",), tag="sin")
                P.op("act", lambda e: e.activation(
                        out=cscrA[:], in_=cscrB[:], func=AF.Sin, scale=0.5),
                     r=("cscrB",), w=("cscrA",), tag="sinh")
                P.op("act", lambda e: e.activation(
                        out=cscrA[:], in_=cscrA[:], func=AF.Square),
                     r=("cscrA",), w=("cscrA",), tag="sq")
                P.op("dve", lambda e, c0=c0, c1=c1: e.tensor_scalar(
                        out=cos_sb[:, c0:c1], in0=cscrA[:], scalar1=-2.0,
                        scalar2=1.0, op0=ALU.mult, op1=ALU.add),
                     r=("cscrA",), w=("cos",), tag="cos")
            P.op("dve", lambda e: e.tensor_tensor(out=lp_re[:], in0=lpm_sb[:],
                                                  in1=cos_sb[:], op=ALU.mult),
                 r=("lpm", "cos"), w=("lp_re",), tag="lpre")
            P.op("dve", lambda e: e.tensor_tensor(out=lp_im[:], in0=lpm_sb[:],
                                                  in1=sin_sb[:], op=ALU.mult),
                 r=("lpm", "sin"), w=("lp_im",), tag="lpim")
            P.op("dve", lambda e: e.tensor_tensor(out=li_re[:], in0=lim_sb[:],
                                                  in1=cos_sb[:], op=ALU.mult),
                 r=("lim", "cos"), w=("li_re",), tag="lire")
            P.op("dve", lambda e: e.tensor_tensor(out=li_imN[:], in0=lim_sb[:],
                                                  in1=sin_sb[:], op=ALU.mult),
                 r=("lim", "sin"), w=("li_imN",), tag="liimN")
            dbgdump(10240, lp_re[:], ("lp_re",), 1024, always=True)
            dbgdump(11264, li_re[:], ("li_re",), 1024, always=True)
            dbgdump(12288, lp_im[:], ("lp_im",), 1024, always=True)
            dbgdump(13312, li_imN[:], ("li_imN",), 1024, always=True)
            dbgdump(14336, grep[:], ("grep",), 1024, always=True)
            dbgdump(15360, drep[:], ("drep",), 1024, always=True)
            dbgdump(4096, bre_sb[:, 0:2048], ("bre",), 2048, always=True)
            dbgdump(6144, cre_sb[:, 0:2048], ("cre",), 2048, always=True)
            if DBG2:
                P.dma("pool", lambda e: e.dma_start(out=dbg_d[0:1, 9216:10240],
                                                    in_=g_row[:]),
                      r=("g_row",), w=("dbg_dram",), tag="dbg_grow")
                P.dma("pool", lambda e: e.dma_start(out=dbg_d[1:2, 9216:10240],
                                                    in_=dskip_sb[:]),
                      r=("dskip_row",), w=("dbg_dram",), tag="dbg_drow")
            P.op("act", lambda e: e.activation(out=g_row[:], in_=nu_row[:],
                                               func=AF.Exp, scale=-2.0),
                 r=("nu_row",), w=("g_row",), tag="e2")
            P.op("dve", lambda e: e.tensor_scalar(out=g_row[:], in0=g_row[:],
                                                  scalar1=-1.0, scalar2=1.0 + EPS,
                                                  op0=ALU.mult, op1=ALU.add),
                 r=("g_row",), w=("g_row",), tag="g1m")
            P.op("act", lambda e: e.activation(out=g_row[:], in_=g_row[:],
                                               func=AF.Ln),
                 r=("g_row",), w=("g_row",), tag="glng")
            P.op("act", lambda e: e.activation(out=g_row[:], in_=g_row[:],
                                               func=AF.Exp, scale=0.5),
                 r=("g_row",), w=("g_row",), tag="gexp")
            repl_row(g_row[:], "g_row", grep, "grep", 1024)
            repl_row(dskip_sb[:], "dskip_row", drep, "drep", 1024)
            for wnm, dst, dkey in (("breT", bre_sb, "bre"), ("bimT", bim_sb, "bim")):
                for jq in range(4):
                    P.dma("sp", lambda e, d=DI[wnm], jq=jq: e.dma_start(
                            out=bstage[:].rearrange("p (j o) -> p j o", j=2),
                            in_=d[jq * 256:(jq + 1) * 256, :].rearrange(
                                "(j p) o -> p j o", p=128)),
                          w=("bstage",), tag=f"ld_{wnm}{jq}")
                    for j2 in range(2):
                        j = jq * 2 + j2
                        P.op("dve", lambda e, j=j, j2=j2, dst=dst: e.tensor_tensor(
                                out=dst[:, j * 1024:(j + 1) * 1024],
                                in0=bstage[:, j2 * 1024:(j2 + 1) * 1024],
                                in1=grep[:], op=ALU.mult),
                             r=("bstage", "grep"), w=(dkey,), tag=f"scale_{wnm}")
            for wnm, dst, dkey in (("creT", cre_sb, "cre"), ("cimT", cim_sb, "cim")):
                P.dma("pool", lambda e, d=DI[wnm], dst=dst: e.dma_start(
                        out=dst[:].rearrange("p (j o) -> p j o", j=8),
                        in_=d.rearrange("(j p) o -> p j o", p=128)),
                      w=(dkey,), tag=f"ld_{wnm}")
            P.flush("Cprep")
            pctx.close()
            P.barrier()

            xC = [csb(f"xC{i}", [128, 4096], F32) for i in range(2)]
            nbfC2 = [csb(f"nbfC{i}", [128, 1024], BF16) for i in range(2)]
            n1TC = [csb(f"n1TC{i}", [128, 4096], BF16) for i in range(2)]
            lnstC = [csb(f"lnstC{i}", [128, 8], F32) for i in range(2)]
            uA_bf = csb("uA_bf", [128, 2048], BF16)
            uB_bf = csb("uB_bf", [128, 2048], BF16)
            skscr = csb("skscr", [128, 1024], F32)
            sksc2 = csb("sksc2", [128, 1024], F32)
            ct1 = csb("ct1", [128, 512], F32)
            ct2 = csb("ct2", [128, 512], F32)
            hs_bf = [csb(f"hs_bf{i}", [128, 2048], BF16) for i in range(2)]
            hsT = csb("hsT", [128, 2048], BF16)
            carry = csb("carry", [1, 2048], BF16)
            yC = csb("yC", [128, 1024], F32)
            P.op("dve", lambda e: e.memset(carry[:], 0.0), w=("carry",),
                 tag="carry0", swb=True)
            P.op("dve", lambda e: e.memset(hs_bf[0][:], 0.0), w=("hs_bf0",),
                 tag="hs0a", swb=True)
            P.op("dve", lambda e: e.memset(hs_bf[1][:], 0.0), w=("hs_bf1",),
                 tag="hs0b", swb=True)

            def c_scan_stage(c, m):
                par = c % 2
                ntk = f"n1TC{par}"
                mpar = m % 2
                hk = f"hs_bf{mpar}"
                hsb = hs_bf[mpar]
                for ri, wsb, wky in ((0, bre_sb, "bre"), (1, bim_sb, "bim")):
                    for h2 in range(2):
                        bank = pf[2 * ri + h2]
                        bkk = f"PF{2 * ri + h2}"
                        for ji in range(8):
                            P.op("pe", lambda e, ji=ji, h2=h2, m=m, wsb=wsb,
                                 bank=bank, par=par: e.matmul(
                                    bank[:],
                                    lhsT=n1TC[par][:, m * 1024 + ji * 128:
                                                   m * 1024 + ji * 128 + 128],
                                    rhs=wsb[:, ji * 1024 + h2 * 512:
                                            ji * 1024 + h2 * 512 + 512],
                                    start=(ji == 0), stop=(ji == 7)),
                                 r=(ntk, wky), w=(bkk,), tag="bumm")
                for h2 in range(2):
                    c0, c1 = h2 * 512, (h2 + 1) * 512
                    bre_ps, bim_ps = pf[h2], pf[2 + h2]
                    kre, kim = f"PF{h2}", f"PF{2 + h2}"
                    P.op("dve", lambda e, c0=c0, c1=c1, bre_ps=bre_ps:
                         e.tensor_tensor(out=uA_bf[:, c0:c1],
                                         in0=li_re[:, c0:c1],
                                         in1=bre_ps[:], op=ALU.mult),
                         r=("li_re", kre), w=("uA_bf",), tag="u1")
                    P.op("dve", lambda e, c0=c0, c1=c1, bim_ps=bim_ps:
                         e.tensor_tensor(out=uB_bf[:, c0:c1],
                                         in0=li_imN[:, c0:c1],
                                         in1=bim_ps[:], op=ALU.mult),
                         r=("li_imN", kim), w=("uB_bf",), tag="u2")
                    P.op("dve", lambda e, c0=c0, c1=c1, bim_ps=bim_ps:
                         e.scalar_tensor_tensor(
                            out=uA_bf[:, 1024 + c0:1024 + c1], in0=bim_ps[:],
                            scalar=-1.0,
                            in1=li_re[:, c0:c1], op0=ALU.mult, op1=ALU.mult),
                         r=("li_re", kim), w=("uA_bf",), tag="u4")
                    P.op("dve", lambda e, c0=c0, c1=c1, bre_ps=bre_ps:
                         e.tensor_tensor(out=uB_bf[:, 1024 + c0:1024 + c1],
                                         in0=li_imN[:, c0:c1],
                                         in1=bre_ps[:], op=ALU.mult),
                         r=("li_imN", kre), w=("uB_bf",), tag="u5")
                for seg in range(4):
                    bank = pf[seg]
                    bkk = f"PF{seg}"
                    P.op("pe", lambda e, seg=seg, bank=bank: e.matmul(
                            bank[:], lhsT=ut128b[:],
                            rhs=uA_bf[:, seg * 512:(seg + 1) * 512],
                            start=True, stop=False),
                         r=("uA_bf", "ut128b"), w=(bkk,), tag="cumsA")
                    P.op("pe", lambda e, seg=seg, bank=bank: e.matmul(
                            bank[:], lhsT=ut128b[:],
                            rhs=uB_bf[:, seg * 512:(seg + 1) * 512],
                            start=False, stop=False),
                         r=("uB_bf", "ut128b"), w=(bkk,), tag="cumsB")
                    P.op("pe", lambda e, seg=seg, bank=bank: e.matmul(
                            bank[:], lhsT=onesrow_bf[:],
                            rhs=carry[0:1, seg * 512:(seg + 1) * 512],
                            start=False, stop=True),
                         r=("carry", "onesrow_bf"), w=(bkk,), tag="carrymm")
                for h2 in range(2):
                    c0, c1 = h2 * 512, (h2 + 1) * 512
                    sre, sim = pf[h2], pf[2 + h2]
                    kre, kim = f"PF{h2}", f"PF{2 + h2}"
                    P.op("dve", lambda e, c0=c0, c1=c1, sre=sre: e.tensor_tensor(
                            out=ct1[:], in0=lp_re[:, c0:c1], in1=sre[:],
                            op=ALU.mult),
                         r=("lp_re", kre), w=("ct1",), tag="hs1")
                    P.op("dve", lambda e, c0=c0, c1=c1, sim=sim: e.tensor_tensor(
                            out=ct2[:], in0=lp_im[:, c0:c1], in1=sim[:],
                            op=ALU.mult),
                         r=("lp_im", kim), w=("ct2",), tag="hs2")
                    P.op("pool", lambda e, c0=c0, c1=c1, hsb=hsb: e.tensor_tensor(
                            out=hsb[:, c0:c1], in0=ct1[:], in1=ct2[:],
                            op=ALU.add),
                         r=("ct1", "ct2"), w=(hk,), tag="hs3")
                    P.op("dve", lambda e, c0=c0, c1=c1, sim=sim: e.tensor_tensor(
                            out=ct1[:], in0=lp_re[:, c0:c1], in1=sim[:],
                            op=ALU.mult),
                         r=("lp_re", kim), w=("ct1",), tag="hs4")
                    P.op("dve", lambda e, c0=c0, c1=c1, sre=sre: e.scalar_tensor_tensor(
                            out=ct2[:], in0=sre[:], scalar=-1.0,
                            in1=lp_im[:, c0:c1], op0=ALU.mult, op1=ALU.mult),
                         r=("lp_im", kre), w=("ct2",), tag="hs5")
                    P.op("pool", lambda e, c0=c0, c1=c1, hsb=hsb: e.tensor_tensor(
                            out=hsb[:, 1024 + c0:1024 + c1], in0=ct1[:],
                            in1=ct2[:], op=ALU.add),
                         r=("ct1", "ct2"), w=(hk,), tag="hs6")
                for seg in range(4):
                    bank = pf[seg]
                    bkk = f"PF{seg}"
                    P.op("pe", lambda e, seg=seg, bank=bank, hsb=hsb: e.matmul(
                            bank[0:1, :], lhsT=e127_bf[:],
                            rhs=hsb[:, seg * 512:(seg + 1) * 512],
                            start=True, stop=True),
                         r=(hk, "e127_bf"), w=(bkk,), tag="cext")
                    P.op("act", lambda e, seg=seg, bank=bank: e.copy(
                            out=carry[0:1, seg * 512:(seg + 1) * 512],
                            in_=bank[0:1, :]),
                         r=(bkk,), w=("carry",), tag="cev")

            def c_post_stage(c, m):
                par = c % 2
                xk = f"xC{par}"
                xt = xC[par]
                mpar = m % 2
                hk = f"hs_bf{mpar}"
                hsb = hs_bf[mpar]
                xv = xt[:, m * 1024:(m + 1) * 1024]
                transpose_pack(lambda j, hsb=hsb: hsb[:, j * 128:(j + 1) * 128],
                               (hk,), pb[0], "PB0")
                P.op("act", lambda e: e.copy(out=hsT[:, 0:1024], in_=pb[0][:]),
                     r=("PB0",), w=("hsT",), tag="hsTev1")
                transpose_pack(lambda j, hsb=hsb: hsb[:, 1024 + j * 128:
                                                      1024 + (j + 1) * 128],
                               (hk,), pb[1], "PB1")
                P.op("act", lambda e: e.copy(out=hsT[:, 1024:2048],
                                             in_=pb[1][:]),
                     r=("PB1",), w=("hsT",), tag="hsTev2")
                for hf in range(2):
                    bank = pf[4 + hf]
                    bkk = f"PF{4 + hf}"
                    for jh in range(8):
                        P.op("pe", lambda e, jh=jh, hf=hf, bank=bank: e.matmul(
                                bank[:], lhsT=hsT[:, jh * 128:(jh + 1) * 128],
                                rhs=cre_sb[:, jh * 1024 + hf * 512:
                                           jh * 1024 + hf * 512 + 512],
                                start=(jh == 0), stop=False),
                             r=("hsT", "cre"), w=(bkk,), tag="cmmre")
                    for jh in range(8):
                        P.op("pe", lambda e, jh=jh, hf=hf, bank=bank: e.matmul(
                                bank[:],
                                lhsT=hsT[:, 1024 + jh * 128:
                                         1024 + (jh + 1) * 128],
                                rhs=cim_sb[:, jh * 1024 + hf * 512:
                                           jh * 1024 + hf * 512 + 512],
                                start=False, stop=(jh == 7)),
                             r=("hsT", "cim"), w=(bkk,), tag="cmmim")
                    P.op("act", lambda e, hf=hf, m=m, par=par, xv=xv:
                         e.activation(
                            out=skscr[:, hf * 512:(hf + 1) * 512],
                            in_=xv[:, hf * 512:(hf + 1) * 512],
                            func=AF.Identity,
                            scale=lnstC[par][:, 2 * m:2 * m + 1],
                            bias=lnstC[par][:, 2 * m + 1:2 * m + 2]),
                         r=(xk, f"lnstC{par}"), w=("skscr",), tag="skln")
                    P.op("dve", lambda e, hf=hf: e.tensor_tensor(
                            out=sksc2[:, hf * 512:(hf + 1) * 512],
                            in0=drep[:, hf * 512:(hf + 1) * 512],
                            in1=skscr[:, hf * 512:(hf + 1) * 512], op=ALU.mult),
                         r=("drep", "skscr"), w=("sksc2",), tag="skip")
                    P.op("dve", lambda e, hf=hf, bank=bank: e.tensor_tensor(
                            out=yC[:, hf * 512:(hf + 1) * 512], in0=bank[:],
                            in1=sksc2[:, hf * 512:(hf + 1) * 512], op=ALU.add),
                         r=(bkk, "sksc2"), w=("yC",), tag="cout")
                ln(yC[:], ("yC",), [(lnscr2[:], "lnscr2")])
                P.op("dve", lambda e, xv=xv: e.tensor_tensor(
                        out=xv, in0=xv, in1=lnscr2[:], op=ALU.add),
                     r=("lnscr2",), w=(xk,), tag="xaddC")

            def c_leadin(c):
                par = c % 2
                xk = f"xC{par}"
                xt = xC[par]
                ntk = f"n1TC{par}"
                src = xst_d[c * CH:(c + 1) * CH, :].rearrange(
                    "(m p) d -> p m d", p=128)
                P.dma("sp", lambda e, xt=xt, src=src: e.dma_start(
                        out=xt[:].rearrange("p (m d) -> p m d", m=MT), in_=src),
                      r=(f"xst{c}",), w=(xk,), tag=f"ldxC{c}")
                stk = f"lnstC{par}"
                for m in range(MT):
                    xv = xt[:, m * 1024:(m + 1) * 1024]
                    ln_stats(xv, (xk,), lnstC[par], stk, m)
                for m in range(MT):
                    xv = xt[:, m * 1024:(m + 1) * 1024]
                    nb = nbfC2[m % 2]
                    nbk = f"nbfC{m % 2}"
                    ln_apply(xv, (xk,), nb[:], nbk, lnstC[par], stk, m)
                    tb = pb[m % 2]
                    transpose_pack(lambda j, nb=nb: nb[:, j * 128:(j + 1) * 128],
                                   (nbk,), tb, f"PB{m % 2}")
                    P.op("act", lambda e, m=m, tb=tb, par=par: e.copy(
                            out=n1TC[par][:, m * 1024:(m + 1) * 1024],
                            in_=tb[:]),
                         r=(f"PB{m % 2}",), w=(ntk,), tag="n1Tev")

            def c_store(c):
                par = c % 2
                xk = f"xC{par}"
                xt = xC[par]
                dst = xst_d[c * CH:(c + 1) * CH, :].rearrange(
                    "(m p) d -> p m d", p=128)
                P.dma("act", lambda e, xt=xt, dst=dst: e.dma_start(
                        out=dst, in_=xt[:].rearrange("p (m d) -> p m d", m=MT)),
                      r=(xk,), w=(f"xst{c}",), tag=f"stxC{c}")

            c_leadin(0)
            for c in range(NCH):
                c_scan_stage(c, 0)
                if c > 0:
                    c_post_stage(c - 1, 3)
                    c_store(c - 1)
                c_scan_stage(c, 1)
                c_post_stage(c, 0)
                if c + 1 < NCH:
                    c_leadin(c + 1)
                c_scan_stage(c, 2)
                c_post_stage(c, 1)
                c_scan_stage(c, 3)
                c_post_stage(c, 2)
            c_post_stage(NCH - 1, 3)
            c_store(NCH - 1)
            P.flush("C")
            cctx.close()
            P.barrier()

        if "D" in phases:
            dctx = ExitStack()
            dsb = lambda name, shp, dt: dctx.enter_context(
                nc.sbuf_tensor(name, shp, dt))
            n3Tgs = [dsb(f"n3Tg{i}", [128, 8192], BF16) for i in range(2)]
            nbfD2 = [dsb(f"nbfD{i}", [128, 1024], BF16) for i in range(2)]
            stD = dsb("stD", [128, 16], F32)
            f_g = dsb("f_g", [128, 8192], F32)
            gatess = [dsb(f"gates{i}", [128, 64], F32) for i in range(2)]
            w1d = [dsb(f"w1d{i}", [128, 8192], BF16) for i in range(2)]
            w3d = [dsb(f"w3d{i}", [128, 8192], BF16) for i in range(2)]
            w2d = dsb("w2d", [128, 8192], BF16)
            rw_sb = dsb("rw_sb", [128, 64], BF16)
            ebrep = dsb("ebrep", [128, 8], F32)
            ebrow = dsb("ebrow", [1, 8], F32)
            xmt = [dsb(f"xmt{i}", [128, 1024], F32) for i in range(4)]
            xrt = dsb("xrt", [128, 1024], F32)
            y1s = dsb("y1s", [128, 1024], F32)
            mid = dsb("mid", [128, 4096], BF16)
            gz = dsb("gz", [128, 8], F32)
            gp = dsb("gp", [128, 8], F32)
            gt = dsb("gt", [128, 8], F32)
            gt2 = dsb("gt2", [128, 8], F32)
            gtop1 = dsb("gtop1", [128, 8], F32)
            gtop2 = dsb("gtop2", [128, 8], F32)
            gmx = dsb("gmx", [128, 1], F32)
            gnm = dsb("gnm", [128, 1], F32)

            P.dma("pool", lambda e: e.dma_start(
                    out=rw_sb[:].rearrange("p (j x) -> p j x", j=8),
                    in_=DI["rw"].rearrange("(j p) x -> p j x", p=128)),
                  w=("rw",), tag="ld_rw")
            ldc("ebias_row", ebrow)
            P.op("pe", lambda e: e.matmul(pf[5][:, 0:8], lhsT=onesrow_f[:],
                                          rhs=ebrow[:], start=True, stop=True),
                 r=("ebias_row", "onesrow_f"), w=("PF5",), tag="ebmm")
            P.op("act", lambda e: e.copy(out=ebrep[:], in_=pf[5][:, 0:8]),
                 r=("PF5",), w=("ebrep",), tag="ebcp")

            exlist = [("sh", "sw1", "sw2", "sw3", None)] + [
                (f"e{ee}", "w1", "w2", "w3", ee) for ee in range(8)]

            def d_li_stats(g):
                for mt in range(8):
                    xb = xmt[mt % 4]
                    xbk = f"xmt{mt % 4}"
                    src = xst_d[g * GT + mt * 128:g * GT + (mt + 1) * 128, :]
                    P.dma("sp", lambda e, xb=xb, src=src: e.dma_start(
                            out=xb[:], in_=src),
                          r=(f"xst{2 * g + mt // 4}",), w=(xbk,),
                          tag=f"ldxD{g}_{mt}")
                    ln_stats(xb[:], (xbk,), stD, "stD", mt)

            def d_li_applyall(g):
                gpar = g % 2
                n3 = n3Tgs[gpar]
                n3k = f"n3Tg{gpar}"
                gat = gatess[gpar]
                gatk = f"gates{gpar}"
                def d_li_apply(mt):
                    xb = xmt[mt % 4]
                    xbk = f"xmt{mt % 4}"
                    src = xst_d[g * GT + mt * 128:g * GT + (mt + 1) * 128, :]
                    P.dma("sp", lambda e, xb=xb, src=src: e.dma_start(
                            out=xb[:], in_=src),
                          r=(f"xst{2 * g + mt // 4}",), w=(xbk,),
                          tag=f"ldxD2_{g}_{mt}")
                    nb = nbfD2[mt % 2]
                    nbk = f"nbfD{mt % 2}"
                    ln_apply(xb[:], (xbk,), nb[:], nbk, stD, "stD", mt)
                    tb = pb[mt % 2]
                    transpose_pack(lambda j, nb=nb: nb[:, j * 128:(j + 1) * 128],
                                   (nbk,), tb, f"PB{mt % 2}")
                    dv = n3[:].rearrange("p (j t) -> p j t", j=8)[
                        :, :, mt * 128:(mt + 1) * 128]
                    sv = tb[:].rearrange("p (j t) -> p j t", j=8)
                    P.op("dve", lambda e, dv=dv, sv=sv: e.tensor_copy(dv, sv),
                         r=(f"PB{mt % 2}",), w=(n3k,), tag="n3Tgev")
                    for j in range(8):
                        P.op("pe", lambda e, j=j, mt=mt, n3=n3: e.matmul(
                                pf[4][:, 0:8],
                                lhsT=n3[:, j * 1024 + mt * 128:
                                        j * 1024 + mt * 128 + 128],
                                rhs=rw_sb[:, j * 8:(j + 1) * 8],
                                start=(j == 0), stop=(j == 7)),
                             r=(n3k, "rw"), w=("PF4",), tag="rmm")
                    P.op("dve", lambda e: e.tensor_reduce(
                            out=gmx[:], in_=pf[4][:, 0:8], axis=AX, op=ALU.max),
                         r=("PF4",), w=("gmx",), tag="gmx", swb=True)
                    P.op("dve", lambda e: e.tensor_scalar(
                            out=gnm[:], in0=gmx[:], scalar1=-1.0, scalar2=None,
                            op0=ALU.mult),
                         r=("gmx",), w=("gnm",), tag="gnm", early=True)
                    P.op("act", lambda e: e.activation(
                            out=gz[:], in_=pf[4][:, 0:8], func=AF.Exp,
                            bias=gnm[:]),
                         r=("PF4", "gnm"), w=("gz",), tag="gexp")
                    P.op("dve", lambda e: e.tensor_reduce(
                            out=gmx[:], in_=gz[:], axis=AX, op=ALU.add),
                         r=("gz",), w=("gmx",), tag="gsum", swb=True)
                    P.op("dve", lambda e: e.reciprocal(out=gnm[:], in_=gmx[:]),
                         r=("gmx",), w=("gnm",), tag="grs", swb=True)
                    P.op("dve", lambda e: e.tensor_scalar(
                            out=gp[:], in0=gz[:], scalar1=gnm[:], scalar2=None,
                            op0=ALU.mult),
                         r=("gz", "gnm"), w=("gp",), tag="gp", early=True)
                    P.op("dve", lambda e: e.tensor_tensor(
                            out=gt[:], in0=gp[:], in1=ebrep[:], op=ALU.add),
                         r=("gp", "ebrep"), w=("gt",), tag="gt")
                    P.op("dve", lambda e: e.tensor_reduce(
                            out=gmx[:], in_=gt[:], axis=AX, op=ALU.max),
                         r=("gt",), w=("gmx",), tag="gm1", swb=True)
                    P.op("dve", lambda e: e.tensor_scalar(
                            out=gtop1[:], in0=gt[:], scalar1=gmx[:], scalar2=None,
                            op0=ALU.is_ge),
                         r=("gt", "gmx"), w=("gtop1",), tag="gtop1", early=True)
                    P.op("dve", lambda e: e.scalar_tensor_tensor(
                            out=gt2[:], in0=gtop1[:], scalar=-1e9, in1=gt[:],
                            op0=ALU.mult, op1=ALU.add),
                         r=("gtop1", "gt"), w=("gt2",), tag="gt2")
                    P.op("dve", lambda e: e.tensor_reduce(
                            out=gmx[:], in_=gt2[:], axis=AX, op=ALU.max),
                         r=("gt2",), w=("gmx",), tag="gm2", swb=True)
                    P.op("dve", lambda e: e.tensor_scalar(
                            out=gtop2[:], in0=gt2[:], scalar1=gmx[:], scalar2=None,
                            op0=ALU.is_ge),
                         r=("gt2", "gmx"), w=("gtop2",), tag="gtop2", early=True)
                    P.op("dve", lambda e: e.tensor_tensor(
                            out=gtop1[:], in0=gtop1[:], in1=gtop2[:], op=ALU.add),
                         r=("gtop2",), w=("gtop1",), tag="gmask")
                    P.op("dve", lambda e, mt=mt, gat=gat: e.tensor_tensor(
                            out=gat[:, mt * 8:(mt + 1) * 8], in0=gp[:],
                            in1=gtop1[:], op=ALU.mult),
                         r=("gp", "gtop1"), w=(gatk,), tag="gates")

                for mt in range(8):
                    d_li_apply(mt)

            def d_experts(g):
                gpar = g % 2
                n3 = n3Tgs[gpar]
                n3k = f"n3Tg{gpar}"
                gat = gatess[gpar]
                gatk = f"gates{gpar}"
                for xi, (ename, k1, k2, k3, ee) in enumerate(exlist):
                    par = (g * 9 + xi) % 2
                    for knm, wt, wkey in ((k1, w1d[par], f"w1d{par}"),
                                          (k3, w3d[par], f"w3d{par}"),
                                          (k2, w2d, "w2d")):
                        d = DI[knm] if ee is None else DI[knm][ee]
                        P.dma("pool", lambda e, wt=wt, d=d: e.dma_start(
                                out=wt[:].rearrange("p (j o) -> p j o", j=8),
                                in_=d.rearrange("(j p) o -> p j o", p=128)),
                              w=(wkey,), tag=f"ldw_{ename}_{knm}")
                    for u in range(2):
                        for jh in range(8):
                            b1 = pf[jh % 2]
                            b1k = f"PF{jh % 2}"
                            b3 = pf[2 + jh % 2]
                            b3k = f"PF{2 + jh % 2}"
                            for jd in range(8):
                                P.op("pe", lambda e, jd=jd, jh=jh, u=u, par=par,
                                     b1=b1, n3=n3: e.matmul(
                                        b1[:],
                                        lhsT=w1d[par][:, jd * 1024 + jh * 128:
                                                      jd * 1024 + jh * 128 + 128],
                                        rhs=n3[:, jd * 1024 + u * 512:
                                               jd * 1024 + u * 512 + 512],
                                        start=(jd == 0), stop=(jd == 7)),
                                     r=(f"w1d{par}", n3k), w=(b1k,), tag="y1mm")
                            P.op("act", lambda e, jh=jh, b1=b1: e.activation(
                                    out=y1s[:, (jh % 2) * 512:(jh % 2) * 512 + 512],
                                    in_=b1[:], func=AF.Silu),
                                 r=(b1k,), w=("y1s",), tag="silu")
                            for jd in range(8):
                                P.op("pe", lambda e, jd=jd, jh=jh, u=u, par=par,
                                     b3=b3, n3=n3: e.matmul(
                                        b3[:],
                                        lhsT=w3d[par][:, jd * 1024 + jh * 128:
                                                      jd * 1024 + jh * 128 + 128],
                                        rhs=n3[:, jd * 1024 + u * 512:
                                               jd * 1024 + u * 512 + 512],
                                        start=(jd == 0), stop=(jd == 7)),
                                     r=(f"w3d{par}", n3k), w=(b3k,), tag="y3mm")
                            P.op("dve", lambda e, jh=jh, b3=b3: e.tensor_tensor(
                                    out=mid[:, jh * 512:(jh + 1) * 512],
                                    in0=b3[:],
                                    in1=y1s[:, (jh % 2) * 512:(jh % 2) * 512 + 512],
                                    op=ALU.mult),
                                 r=(b3k, "y1s"), w=("mid",), tag="mid")
                        for mt2 in range(4):
                            tile = u * 4 + mt2
                            for dh in range(2):
                                bo = pf[4 + dh]
                                bok = f"PF{4 + dh}"
                                for jh in range(8):
                                    P.op("pe", lambda e, jh=jh, mt2=mt2, dh=dh,
                                         bo=bo: e.matmul(
                                            bo[:],
                                            lhsT=mid[:, jh * 512 + mt2 * 128:
                                                     jh * 512 + mt2 * 128 + 128],
                                            rhs=w2d[:, jh * 1024 + dh * 512:
                                                    jh * 1024 + dh * 512 + 512],
                                            start=(jh == 0), stop=(jh == 7)),
                                         r=("mid", "w2d"), w=(bok,),
                                         tag="y2mm")
                                fsl = f_g[:, tile * 1024 + dh * 512:
                                          tile * 1024 + dh * 512 + 512]
                                if ee is None:
                                    P.op("dve", lambda e, fsl=fsl, bo=bo:
                                         e.tensor_copy(fsl, bo[:]),
                                         r=(bok,), w=("f_g",), tag="fsh")
                                else:
                                    gcol = gat[:, tile * 8 + ee:
                                               tile * 8 + ee + 1]
                                    P.op("dve", lambda e, fsl=fsl, bo=bo,
                                         gcol=gcol: e.scalar_tensor_tensor(
                                            out=fsl, in0=bo[:], scalar=gcol,
                                            in1=fsl, op0=ALU.mult, op1=ALU.add),
                                         r=(bok, gatk, "f_g"), w=("f_g",),
                                         tag="facc", early=True)

            def d_tail(g):
                for mt in range(8):
                    src = xst_d[g * GT + mt * 128:g * GT + (mt + 1) * 128, :]
                    P.dma("sp", lambda e, src=src: e.dma_start(
                            out=xrt[:], in_=src),
                          r=(f"xst{2 * g + mt // 4}",), w=("xrt",),
                          tag=f"ldxT{g}_{mt}")
                    ln(f_g[:, mt * 1024:(mt + 1) * 1024], ("f_g",),
                       [(lnscr2[:], "lnscr2")])
                    P.op("dve", lambda e: e.tensor_tensor(
                            out=xrt[:], in0=xrt[:], in1=lnscr2[:], op=ALU.add),
                         r=("lnscr2",), w=("xrt",), tag="xaddD")
                    dst = out_d[g * GT + mt * 128:g * GT + (mt + 1) * 128, :]
                    P.dma("act", lambda e, dst=dst: e.dma_start(
                            out=dst, in_=xrt[:]),
                          r=("xrt",), w=("out_dram",), tag=f"stD{g}_{mt}")

            d_li_stats(0)
            d_li_applyall(0)
            d_li_stats(1)
            d_li_applyall(1)
            for g in range(NG):
                if g + 2 < NG:
                    d_li_stats(g + 2)
                d_experts(g)
                if g + 2 < NG:
                    d_li_applyall(g + 2)
                d_tail(g)
            P.flush("D")
            dctx.close()

        if "DUMP" in phases:
            for c in range(NCH):
                P.dma("sp", lambda e, c=c: e.dma_start(
                        out=out_d[c * CH:(c + 1) * CH, :],
                        in_=xst_d[c * CH:(c + 1) * CH, :]),
                      r=(f"xst{c}",), w=("out_dram",), tag=f"dump{c}")
            P.flush("DUMP")

        P.op("sp", lambda e: e.nop(),
             r=("out_dram", "dbg_dram") if (DBG or DBG2) else ("out_dram",),
             tag="fence")
        P.flush("FIN")

    return nc


def kernel(**inputs):
    common = _prep_common(inputs)
    x_full = np.ascontiguousarray(np.asarray(inputs["L0_attn_x"], dtype=np.float32))
    nc = build_nc()
    in_maps = []
    for core in range(8):
        m = dict(common)
        m["x0"] = np.ascontiguousarray(x_full[core])
        in_maps.append(m)
    res = run_bass_kernel_spmd(nc, in_maps, core_ids=list(range(8)))
    return np.stack([np.asarray(r["out"], dtype=np.float32)
                     for r in res.results], axis=0)


# version 12
# speedup vs baseline: 1.2137x; 1.2137x over previous
import numpy as np
import ml_dtypes
from contextlib import ExitStack

import concourse.bass as bass
import concourse.mybir as mybir
from concourse.bass_utils import run_bass_kernel_spmd
from concourse.alu_op_type import AluOpType as ALU

F32 = mybir.dt.float32
BF16 = mybir.dt.bfloat16
AF = mybir.ActivationFunctionType
AX = mybir.AxisListType.X
BF16NP = ml_dtypes.bfloat16

S = 4096
DM = 1024
NCH = 8
CH = 512
MT = 4
NG = 4
GT = 1024
PIF = float(np.pi)
EPS = 1e-5
DBG = False
DBG2 = False


class _Op:
    __slots__ = ("i", "eng", "fn", "r", "w", "dma", "dmai", "inc", "incv",
                 "deps", "tag", "flushed", "fend", "waits", "swb", "early")

    def __init__(self, i, eng, fn, r, w, dma, dmai, tag, swb=False, early=False):
        self.i, self.eng, self.fn = i, eng, fn
        self.r, self.w = tuple(r), tuple(w)
        self.dma, self.dmai = dma, dmai
        self.inc, self.incv = False, None
        self.deps, self.tag = [], tag
        self.flushed, self.fend, self.waits = False, None, []
        self.swb, self.early = swb, early


class Prog:
    CENG = ("pe", "act", "dve", "pool", "sp")
    QENG = ("sp", "act", "pool")

    NLANE = 8

    def __init__(self, nc, ctx):
        self.nc = nc
        self.ops = []
        self.pending = []
        self.last_w = {}
        self.readers = {}
        self.inc_cnt = {e: 0 for e in self.CENG}
        self.dma_cnt = {e: 0 for e in self.QENG}
        self.lane_cnt = {e: [0] * self.NLANE for e in self.QENG}
        self.sems = {e: ctx.enter_context(nc.semaphore(f"s_{e}")) for e in self.CENG}
        self.dsems = {e: [ctx.enter_context(nc.semaphore(f"d_{e}{i}"))
                          for i in range(self.NLANE)] for e in self.QENG}
        self.waited = {e: {} for e in self.CENG}
        self.nflush = 0
        self.barrier_req = {}

    def _track(self, o):
        deps = set()
        for k in o.r:
            j = self.last_w.get(k)
            if j is not None:
                deps.add(j)
            self.readers.setdefault(k, []).append(o.i)
        for k in o.w:
            j = self.last_w.get(k)
            if j is not None:
                deps.add(j)
            for rj in self.readers.get(k, ()):
                deps.add(rj)
            self.last_w[k] = o.i
            self.readers[k] = []
        deps.discard(o.i)
        o.deps = sorted(deps)

    def op(self, eng, fn, r=(), w=(), tag="", swb=False, early=False):
        o = _Op(len(self.ops), eng, fn, r, w, False, None, tag, swb, early)
        self._track(o)
        self.ops.append(o)
        self.pending.append(o)
        return o

    def dma(self, eng, fn, r=(), w=(), tag=""):
        assert eng in self.QENG
        lane = self.dma_cnt[eng] % self.NLANE
        self.dma_cnt[eng] += 1
        self.lane_cnt[eng][lane] += 1
        o = _Op(len(self.ops), eng, fn, r, w, True,
                (lane, self.lane_cnt[eng][lane]), tag)
        self._track(o)
        self.ops.append(o)
        self.pending.append(o)
        return o

    def barrier(self):
        assert not self.pending
        snap = {}
        for e2 in self.CENG:
            if self.inc_cnt[e2]:
                snap[("e", e2)] = self.inc_cnt[e2]
        for q in self.QENG:
            for lane in range(self.NLANE):
                if self.lane_cnt[q][lane]:
                    snap[("d", q, lane)] = 16 * self.lane_cnt[q][lane]
        for e in self.CENG:
            self.barrier_req[e] = dict(snap)

    def flush(self, name):
        for o in self.pending:
            for d in o.deps:
                dop = self.ops[d]
                if dop.dma or dop.flushed:
                    continue
                if dop.eng == o.eng and (dop.eng in ("pe", "sp")):
                    continue
                dop.inc = True
        for o in self.pending:
            if o.dma:
                continue
            if o.inc:
                self.inc_cnt[o.eng] += 1
                o.incv = self.inc_cnt[o.eng]
        last = {}
        for o in self.pending:
            if not o.dma:
                last[o.eng] = o
        for o in last.values():
            if o.incv is None:
                o.inc = True
                self.inc_cnt[o.eng] += 1
                o.incv = self.inc_cnt[o.eng]
        first_seen = set()
        for o in self.pending:
            wd = self.waited[o.eng]
            ws = {}
            if o.eng not in first_seen:
                first_seen.add(o.eng)
                breq = self.barrier_req.pop(o.eng, None)
                if breq:
                    for sem, val in breq.items():
                        if val > ws.get(sem, 0):
                            ws[sem] = val
            for d in o.deps:
                dop = self.ops[d]
                if dop.dma:
                    lane, cnt = dop.dmai
                    sem, val = ("d", dop.eng, lane), 16 * cnt
                elif dop.eng == o.eng and (dop.eng in ("pe", "sp")):
                    continue
                elif dop.flushed:
                    sem, val = ("e", dop.eng), dop.fend
                else:
                    assert dop.incv is not None, (dop.tag, o.tag)
                    sem, val = ("e", dop.eng), dop.incv
                if val > ws.get(sem, 0):
                    ws[sem] = val
            o.waits = [(sem, val) for sem, val in sorted(ws.items())
                       if wd.get(sem, 0) < val]
            for sem, val in o.waits:
                wd[sem] = val
        for o in self.pending:
            o.flushed = True
            o.fend = self.inc_cnt[o.eng] if not o.dma else None
        pend = self.pending
        self.pending = []
        self.nflush += 1
        secmap = {"sp": "sync", "pe": "tensor", "act": "scalar",
                  "dve": "vector", "pool": "gpsimd"}
        with self.nc.Block(f"f{self.nflush}_{name}") as blk:
            for engname in self.CENG:
                ops_e = [o for o in pend if o.eng == engname]
                if not ops_e:
                    continue

                def body(eng, ops_e=ops_e, engname=engname):
                    for o in ops_e:
                        for key, val in o.waits:
                            if key[0] == "d":
                                sem = self.dsems[key[1]][key[2]]
                            else:
                                sem = self.sems[key[1]]
                            eng.wait_ge(sem, val)
                        inst = o.fn(eng)
                        if o.dma:
                            inst.then_inc(self.dsems[o.eng][o.dmai[0]], 16)
                        elif o.inc:
                            assert inst is not None, o.tag
                            inst.then_inc(self.sems[o.eng], 1)

                getattr(blk, secmap[engname])(body)


def _prep_common(I):
    a = lambda x: np.ascontiguousarray(np.asarray(x, dtype=np.float32))
    c = {}
    c["wqT"] = a(np.asarray(I["L0_attn_Wq"]).T)
    c["wkT"] = a(np.asarray(I["L0_attn_Wk"]).T)
    c["wvT"] = a(np.asarray(I["L0_attn_Wv"]).T)
    c["woT"] = a(np.asarray(I["L0_attn_Wo"]).T)
    c["bq_c"] = a(np.asarray(I["L0_attn_bq"]).reshape(8, 128).T)
    c["bk_c"] = a(np.asarray(I["L0_attn_bk"]).reshape(8, 128).T)
    c["bv_row"] = a(np.asarray(I["L0_attn_bv"])[None, :])
    c["bo_row"] = a(np.asarray(I["L0_attn_bo"])[None, :])
    c["a_rt"] = a(np.transpose(np.asarray(I["L0_ffn_G1_R"])[0], (1, 2, 0)))
    c["a_it"] = a(np.transpose(np.asarray(I["L0_ffn_G1_I"])[0], (1, 2, 0)))
    c["b_rt"] = a(np.transpose(np.asarray(I["L0_ffn_G2_R"])[:, :, 0, :], (0, 2, 1)))
    c["b_it"] = a(np.transpose(np.asarray(I["L0_ffn_G2_I"])[:, :, 0, :], (0, 2, 1)))
    c["brl_row"] = a(np.asarray(I["L0_ffn_bias_real"])[None, :])
    c["bil_row"] = a(np.asarray(I["L0_ffn_bias_imag"])[None, :])
    c["nulog_row"] = a(np.asarray(I["L1_attn_nu_log"])[None, :])
    c["thetalog_row"] = a(np.asarray(I["L1_attn_theta_log"])[None, :])
    c["breT"] = a(np.asarray(I["L1_attn_B_re"]).T)
    c["bimT"] = a(np.asarray(I["L1_attn_B_im"]).T)
    c["creT"] = a(np.asarray(I["L1_attn_C_re"]).T)
    c["cimT"] = a(np.asarray(I["L1_attn_C_im"]).T)
    c["dskip_row"] = a(np.asarray(I["L1_attn_D_skip"])[None, :])
    c["rw"] = a(I["L1_ffn_router_w"])
    c["ebias_row"] = a(np.asarray(I["L1_ffn_expert_bias"])[None, :])
    c["w1"] = a(I["L1_ffn_w1"])
    c["w2"] = a(I["L1_ffn_w2"])
    c["w3"] = a(I["L1_ffn_w3"])
    c["sw1"] = a(I["L1_ffn_sw1"])
    c["sw2"] = a(I["L1_ffn_sw2"])
    c["sw3"] = a(I["L1_ffn_sw3"])
    c["id128"] = np.eye(128, dtype=BF16NP)
    c["id64d"] = np.vstack([np.eye(64), np.eye(64)]).astype(BF16NP)
    c["ut128"] = np.triu(np.ones((128, 128), dtype=np.float32))
    c["ut128b"] = np.triu(np.ones((128, 128), dtype=np.float32)).astype(BF16NP)
    c["ones_bf"] = np.ones((128, 1), dtype=BF16NP)
    c["onesrow_f"] = np.ones((1, 128), dtype=np.float32)
    c["onesrow_bf"] = np.ones((1, 128), dtype=BF16NP)
    c["iota1_row"] = np.arange(1, 129, dtype=np.float32)[None, :]
    e127 = np.zeros((128, 1), dtype=np.float32)
    e127[127, 0] = 1.0
    c["e127_f"] = e127
    c["e127_bf"] = e127.astype(BF16NP)
    c["eps_col"] = np.full((128, 1), EPS, dtype=np.float32)
    ecq = np.zeros((32, 512), dtype=np.float32)
    for q in range(4):
        for p in range(128):
            ecq[8 * q + p // 16, q * 128 + p] = 1.0
    c["ecq"] = ecq
    ecg = np.zeros((16, 128), dtype=np.float32)
    for p in range(128):
        ecg[p % 16, p] = 1.0
    c["ecg"] = ecg
    return c


_IN_SPECS = {
    "x0": ([S, DM], F32),
    "wqT": ([DM, DM], F32), "wkT": ([DM, DM], F32),
    "wvT": ([DM, DM], F32), "woT": ([DM, DM], F32),
    "bq_c": ([128, 8], F32), "bk_c": ([128, 8], F32),
    "bv_row": ([1, DM], F32), "bo_row": ([1, DM], F32),
    "a_rt": ([4, 32, 32], F32), "a_it": ([4, 32, 32], F32),
    "b_rt": ([4, 16, 16], F32), "b_it": ([4, 16, 16], F32),
    "brl_row": ([1, 512], F32), "bil_row": ([1, 512], F32),
    "nulog_row": ([1, DM], F32), "thetalog_row": ([1, DM], F32),
    "breT": ([DM, DM], F32), "bimT": ([DM, DM], F32),
    "creT": ([DM, DM], F32), "cimT": ([DM, DM], F32),
    "dskip_row": ([1, DM], F32),
    "rw": ([DM, 8], F32), "ebias_row": ([1, 8], F32),
    "w1": ([8, DM, DM], F32), "w2": ([8, DM, DM], F32), "w3": ([8, DM, DM], F32),
    "sw1": ([DM, DM], F32), "sw2": ([DM, DM], F32), "sw3": ([DM, DM], F32),
    "id128": ([128, 128], BF16), "id64d": ([128, 64], BF16),
    "ut128": ([128, 128], F32), "ut128b": ([128, 128], BF16),
    "ones_bf": ([128, 1], BF16),
    "onesrow_f": ([1, 128], F32), "onesrow_bf": ([1, 128], BF16),
    "iota1_row": ([1, 128], F32),
    "e127_f": ([128, 1], F32), "e127_bf": ([128, 1], BF16),
    "eps_col": ([128, 1], F32),
    "ecq": ([32, 512], F32), "ecg": ([16, 128], F32),
}


def build_nc(phases=("P0", "AB", "C", "D")):
    nc = bass.Bass("TRN2", target_bir_lowering=False)
    DI = {k: nc.dram_tensor(k, shp, dt, kind="ExternalInput").ap()
          for k, (shp, dt) in _IN_SPECS.items()}
    out_d = nc.dram_tensor("out", [S, DM], F32, kind="ExternalOutput").ap()
    xst_d = nc.dram_tensor("xst", [S, DM], F32, kind="Internal").ap()
    dbg_d = (nc.dram_tensor("dbg", [128, 16384], F32, kind="ExternalOutput").ap()
             if (DBG or DBG2) else None)

    ctx = ExitStack()
    with ctx:
        P = Prog(nc, ctx)
        sb = lambda name, shp, dt: ctx.enter_context(nc.sbuf_tensor(name, shp, dt))
        ps = lambda name, shp, dt: ctx.enter_context(nc.psum_tensor(name, shp, dt))

        pf = [ps(f"pf{i}", [128, 512], F32) for i in range(6)]
        pb = [ps(f"pb{i}", [128, 1024], BF16) for i in range(2)]

        id128 = sb("t_id128", [128, 128], BF16)
        id64d = sb("t_id64d", [128, 64], BF16)
        ut128 = sb("t_ut128", [128, 128], F32)
        ut128b = sb("t_ut128b", [128, 128], BF16)
        ones_bf = sb("t_ones_bf", [128, 1], BF16)
        onesrow_f = sb("t_onesrow_f", [1, 128], F32)
        onesrow_bf = sb("t_onesrow_bf", [1, 128], BF16)
        iota1_row = sb("t_iota1_row", [1, 128], F32)
        e127_f = sb("t_e127_f", [128, 1], F32)
        e127_bf = sb("t_e127_bf", [128, 1], BF16)
        eps_col = sb("t_eps_col", [128, 1], F32)
        lnst = sb("lnst", [128, 12], F32)
        lnmv = sb("lnmv", [128, 2], F32)
        lnlv = sb("lnlv", [128, 1], F32)
        lnrstd = sb("lnrstd", [128, 1], F32)
        lnnmr = sb("lnnmr", [128, 1], F32)
        lnscr2 = sb("lnscr2", [128, 1024], F32)
        dbgs = sb("dbgs", [128, 80], F32) if DBG else None
        mtb = [sb(f"mtb{i}", [128, 128], BF16) for i in range(2)]

        def dbgdump(col, ap, keys, w, always=False):
            if not (DBG or (DBG2 and always)):
                return
            P.dma("pool", lambda e, ap=ap, col=col, w=w: e.dma_start(
                    out=dbg_d[:, col:col + w], in_=ap,
                    allow_slow_non_contiguous=True),
                  r=tuple(keys), w=("dbg_dram",), tag=f"dbg{col}")

        def ldc(name, tile, cast=False):
            q = "pool" if cast else "sp"
            P.dma(q, lambda e, t=tile, d=DI[name]: e.dma_start(out=t[:], in_=d),
                  w=(name,), tag=f"ld_{name}")

        def ln(x_ap, xkeys, outs):
            xkeys = tuple(xkeys)
            P.op("dve", lambda e: e.bn_stats(lnst[:, 0:6], x_ap[:, 0:512]),
                 r=xkeys, w=("lnst",), tag="ln_s0", swb=True)
            P.op("dve", lambda e: e.bn_stats(lnst[:, 6:12], x_ap[:, 512:1024]),
                 r=xkeys, w=("lnst",), tag="ln_s1", swb=True)
            P.op("dve", lambda e: e.bn_aggr(lnmv[:], lnst[:]),
                 r=("lnst",), w=("lnmv",), tag="ln_ag", swb=True)
            P.op("act", lambda e: e.activation(out=lnlv[:], in_=lnmv[:, 1:2],
                                               func=AF.Ln, bias=eps_col[:],
                                               scale=1.0),
                 r=("lnmv", "epsc"), w=("lnlv",), tag="ln_lnv")
            P.op("act", lambda e: e.activation(out=lnrstd[:], in_=lnlv[:],
                                               func=AF.Exp, scale=-0.5),
                 r=("lnlv",), w=("lnrstd",), tag="ln_rstd")
            P.op("dve", lambda e: e.tensor_scalar(
                    out=lnnmr[:], in0=lnmv[:, 0:1], scalar1=lnrstd[:],
                    scalar2=-1.0, op0=ALU.mult, op1=ALU.mult),
                 r=("lnmv", "lnrstd"), w=("lnnmr",), tag="ln_nmr", early=True)
            for (o_ap, o_key) in outs:
                P.op("act", lambda e, o_ap=o_ap: e.activation(
                        out=o_ap, in_=x_ap, func=AF.Identity,
                        scale=lnrstd[:], bias=lnnmr[:]),
                     r=xkeys + ("lnrstd", "lnnmr"), w=(o_key,), tag="ln_apply")

        def ln_stats(x_ap, xkeys, stb, stbk, sm):
            xkeys = tuple(xkeys)
            P.op("dve", lambda e: e.bn_stats(lnst[:, 0:6], x_ap[:, 0:512]),
                 r=xkeys, w=("lnst",), tag="lns_s0")
            P.op("dve", lambda e: e.bn_stats(lnst[:, 6:12], x_ap[:, 512:1024]),
                 r=xkeys, w=("lnst",), tag="lns_s1")
            P.op("dve", lambda e: e.bn_aggr(lnmv[:], lnst[:]),
                 r=("lnst",), w=("lnmv",), tag="lns_ag")
            P.op("act", lambda e: e.activation(out=lnlv[:], in_=lnmv[:, 1:2],
                                               func=AF.Ln, bias=eps_col[:],
                                               scale=1.0),
                 r=("lnmv", "epsc"), w=("lnlv",), tag="lns_lnv")
            P.op("act", lambda e: e.activation(
                    out=stb[:, 2 * sm:2 * sm + 1], in_=lnlv[:],
                    func=AF.Exp, scale=-0.5),
                 r=("lnlv",), w=(stbk,), tag="lns_rstd")
            P.op("dve", lambda e: e.tensor_scalar(
                    out=stb[:, 2 * sm + 1:2 * sm + 2], in0=lnmv[:, 0:1],
                    scalar1=stb[:, 2 * sm:2 * sm + 1],
                    scalar2=-1.0, op0=ALU.mult, op1=ALU.mult),
                 r=("lnmv", stbk), w=(stbk,), tag="lns_nmr")

        def ln_apply(x_ap, xkeys, o_ap, o_key, stb, stbk, sm):
            P.op("act", lambda e: e.activation(
                    out=o_ap, in_=x_ap, func=AF.Identity,
                    scale=stb[:, 2 * sm:2 * sm + 1],
                    bias=stb[:, 2 * sm + 1:2 * sm + 2]),
                 r=tuple(xkeys) + (stbk,), w=(o_key,), tag="lns_apply")

        def transpose_pack(src_fn, srckeys, bank, bankkey, n=8):
            for j in range(n):
                P.op("pe", lambda e, j=j: e.matmul(
                        bank[:, j * 128:(j + 1) * 128], lhsT=src_fn(j),
                        rhs=id128[:], is_transpose=True,
                        start=(j == 0), stop=(j == n - 1)),
                     r=tuple(srckeys) + ("id128",), w=(bankkey,), tag="tpack")

        def repl_row(row_ap, rowkey, dst, dstkey, width, dve=False):
            nh = (width + 511) // 512
            for h in range(nh):
                wgt = min(512, width - h * 512)
                bank = pf[4 + (h % 2)]
                P.op("pe", lambda e, h=h, wgt=wgt, bank=bank: e.matmul(
                        bank[:, 0:wgt], lhsT=onesrow_f[:],
                        rhs=row_ap[0:1, h * 512:h * 512 + wgt],
                        start=True, stop=True),
                     r=(rowkey, "onesrow_f"), w=(f"PF{4 + h % 2}",), tag="repl_mm")
                eng = "dve" if dve else "act"
                if dve:
                    P.op("dve", lambda e, h=h, wgt=wgt, bank=bank: e.tensor_copy(
                            dst[:, h * 512:h * 512 + wgt], bank[:, 0:wgt]),
                         r=(f"PF{4 + h % 2}",), w=(dstkey,), tag="repl_cp")
                else:
                    P.op("act", lambda e, h=h, wgt=wgt, bank=bank: e.copy(
                            out=dst[:, h * 512:h * 512 + wgt], in_=bank[:, 0:wgt]),
                         r=(f"PF{4 + h % 2}",), w=(dstkey,), tag="repl_cp")

        abctx = ExitStack()
        absb = lambda name, shp, dt: abctx.enter_context(
            nc.sbuf_tensor(name, shp, dt))
        wq_sb = absb("wq_sb", [128, 8192], BF16)
        wk_sb = absb("wk_sb", [128, 8192], BF16)
        wv_sb = absb("wv_sb", [128, 8192], BF16)
        wo_sb = absb("wo_sb", [128, 8192], BF16)
        bqc_sb = absb("bqc_sb", [128, 8], F32)
        bkc_sb = absb("bkc_sb", [128, 8], F32)
        bvb = absb("bvb", [1, 1024], BF16)
        bob = absb("bob", [1, 1024], BF16)
        brlb = absb("brlb", [1, 512], BF16)
        bilb = absb("bilb", [1, 512], BF16)
        wrT = absb("wrT", [128, 2048], BF16)
        wiT = absb("wiT", [128, 2048], BF16)
        wiTn = absb("wiTn", [128, 2048], BF16)

        if "P0" in phases:
            p0ctx = ExitStack()
            p0sb = lambda name, shp, dt: p0ctx.enter_context(
                nc.sbuf_tensor(name, shp, dt))
            for nm, t in (("id128", id128), ("id64d", id64d), ("ut128", ut128),
                          ("ut128b", ut128b),
                          ("ones_bf", ones_bf), ("onesrow_f", onesrow_f),
                          ("onesrow_bf", onesrow_bf),
                          ("iota1_row", iota1_row), ("e127_f", e127_f),
                          ("e127_bf", e127_bf)):
                ldc(nm, t)
            P.dma("sp", lambda e: e.dma_start(out=eps_col[:], in_=DI["eps_col"]),
                  w=("epsc",), tag="ld_eps")
            for nm, t in (("wqT", wq_sb), ("wkT", wk_sb), ("wvT", wv_sb),
                          ("woT", wo_sb)):
                P.dma("pool", lambda e, t=t, d=DI[nm]: e.dma_start(
                        out=t[:].rearrange("p (j o) -> p j o", j=8),
                        in_=d.rearrange("(j p) o -> p j o", p=128)),
                      w=(nm,), tag=f"ld_{nm}")
            ldc("bq_c", bqc_sb)
            ldc("bk_c", bkc_sb)
            ldc("bv_row", bvb, cast=True)
            ldc("bo_row", bob, cast=True)
            ldc("brl_row", brlb, cast=True)
            ldc("bil_row", bilb, cast=True)

            art_sb = p0sb("art_sb", [32, 128], F32)
            brt_sb = p0sb("brt_sb", [16, 64], F32)
            ait_sb = p0sb("ait_sb", [32, 128], F32)
            bit_sb = p0sb("bit_sb", [16, 64], F32)
            ecq_sb = p0sb("ecq_sb", [32, 512], F32)
            ecg_sb = p0sb("ecg_sb", [16, 128], F32)
            for nm, t, pat in (("a_rt", art_sb, "c d b -> d c b"),
                               ("a_it", ait_sb, "c d b -> d c b"),
                               ("b_rt", brt_sb, "c g e -> g c e"),
                               ("b_it", bit_sb, "c g e -> g c e")):
                P.dma("sp", lambda e, t=t, d=DI[nm], pat=pat, nm=nm: e.dma_start(
                        out=t[:].rearrange("p (c x) -> p c x", c=4),
                        in_=d.rearrange(pat)),
                      w=(nm,), tag=f"ld_{nm}")
            ldc("ecq", ecq_sb)
            ldc("ecg", ecg_sb)
            wtmpA = p0sb("wtmpA", [128, 32], F32)
            wtmpB = p0sb("wtmpB", [128, 16], F32)
            wtmpM = p0sb("wtmpM", [128, 512], F32)
            wacc = p0sb("wacc", [128, 512], F32)

            def wbuild(a_sb, b_sb, akey, bkey, dstT, dstTn, dkey, dnkey):
                for q in range(4):
                    for cc in range(4):
                        bA = pf[2 * (cc % 2)]
                        bAk = f"PF{2 * (cc % 2)}"
                        bB = pf[1 + 2 * (cc % 2)]
                        bBk = f"PF{1 + 2 * (cc % 2)}"
                        P.op("pe", lambda e, q=q, cc=cc, bA=bA: e.matmul(
                                bA[:, 0:32], lhsT=ecq_sb[:, q * 128:(q + 1) * 128],
                                rhs=a_sb[:, cc * 32:(cc + 1) * 32],
                                start=True, stop=True),
                             r=(akey, "ecq"), w=(bAk), tag="wb_A")
                        P.op("pe", lambda e, cc=cc, bB=bB: e.matmul(
                                bB[:, 0:16], lhsT=ecg_sb[:],
                                rhs=b_sb[:, cc * 16:(cc + 1) * 16],
                                start=True, stop=True),
                             r=(bkey, "ecg"), w=(bBk), tag="wb_B")
                        P.op("act", lambda e, bA=bA: e.copy(
                                out=wtmpA[:], in_=bA[:, 0:32]),
                             r=(bAk), w=("wb_A"), tag="wb_Aev")
                        P.op("dve", lambda e, bB=bB: e.tensor_copy(
                                wtmpB[:], bB[:, 0:16]),
                             r=(bBk), w=("wb_B"), tag="wb_Bev")
                        avb = wtmpA[:].rearrange("p (b one) -> p b one",
                                                 one=1).broadcast_to([128, 32, 16])
                        bvb = wtmpB[:].rearrange("p (one e) -> p one e",
                                                 one=1).broadcast_to([128, 32, 16])
                        if cc == 0:
                            ov = wacc[:].rearrange("p (b e) -> p b e", e=16)
                            P.op("dve", lambda e, avb=avb, bvb=bvb, ov=ov:
                                 e.tensor_tensor(out=ov, in0=avb, in1=bvb,
                                                 op=ALU.mult),
                                 r=("wb_A", "wb_B"), w=("wb_acc"),
                                 tag="wb_mul")
                        else:
                            mv = wtmpM[:].rearrange("p (b e) -> p b e", e=16)
                            P.op("dve", lambda e, avb=avb, bvb=bvb, mv=mv:
                                 e.tensor_tensor(out=mv, in0=avb, in1=bvb,
                                                 op=ALU.mult),
                                 r=("wb_A", "wb_B"), w=("wb_M"),
                                 tag="wb_mul")
                            P.op("pool", lambda e: e.tensor_tensor(
                                    out=wacc[:], in0=wacc[:], in1=wtmpM[:],
                                    op=ALU.add),
                                 r=("wb_M"), w=("wb_acc"), tag="wb_add")
                    P.op("dve", lambda e, q=q: e.tensor_copy(
                            dstT[:, q * 512:(q + 1) * 512], wacc[:]),
                         r=("wb_acc"), w=(dkey), tag="wb_cast")
                    if dstTn is not None:
                        P.op("act", lambda e, q=q: e.mul(
                                out=dstTn[:, q * 512:(q + 1) * 512], in_=wacc[:],
                                mul=-1.0),
                             r=("wb_acc"), w=(dnkey), tag="wb_castn")

            wbuild(art_sb, brt_sb, "a_rt", "b_rt", wrT, None, "wrT", None)
            wbuild(ait_sb, bit_sb, "a_it", "b_it", wiT, wiTn, "wiT", "wiTn")
            dbgdump(0, wrT[:], ("wrT",), 2048, always=True)
            dbgdump(2048, wiTn[:], ("wiTn",), 2048, always=True)
            P.flush("P0")
            p0ctx.close()
            P.barrier()

        if "AB" in phases:
            x_sb = [absb(f"xA{i}", [128, 4096], F32) for i in range(2)]
            nbf2 = [absb(f"nbf{i}", [128, 1024], BF16) for i in range(2)]
            stAB = absb("stAB", [128, 8], F32)
            n1T = absb("n1T", [128, 4096], BF16)
            q_sb = absb("q_sb", [128, 4096], BF16)
            k_sb = absb("k_sb", [128, 4096], BF16)
            v_sb = absb("v_sb", [128, 4096], BF16)
            attn_bf = [absb(f"attnbf{i}", [128, 1024], BF16) for i in range(4)]
            attnT = absb("attnT", [128, 1024], BF16)
            y_sb = absb("yA", [128, 1024], F32)
            n3T = absb("n3T", [128, 1024], BF16)
            f_sb = absb("fA", [128, 1024], F32)
            S_sb = absb("S_sb", [128, 520], F32)
            S_bf = absb("S_bf", [128, 520], BF16)
            rden = absb("rden", [128, 1], F32)
            ktp_sb = absb("ktp_sb", [128, 128], BF16)

            P.op("dve", lambda e: e.memset(S_sb[:], 0.0), w=("S_sb",), tag="S0",
                 swb=True)
            P.op("dve", lambda e: e.memset(S_bf[:], 0.0), w=("S_bf",), tag="Sbf0",
                 swb=True)

            def ab_attn_stage(c, m):
                par = c % 2
                apar = m % 4
                abf = attn_bf[apar]
                for jp in range(8):
                    for pi in range(2):
                        b0 = pi * 64
                        h = 2 * jp + pi
                        qh = q_sb[b0:b0 + 64,
                                  jp * 512 + m * 128:jp * 512 + m * 128 + 128]
                        kh = k_sb[b0:b0 + 64,
                                  jp * 512 + m * 128:jp * 512 + m * 128 + 128]
                        vh = v_sb[:, m * 1024 + h * 64:m * 1024 + h * 64 + 64]
                        sc_bank = pf[pi]
                        sck = f"PF{pi}"
                        nd_bank = pf[2 + pi]
                        ndk = f"PF{2 + pi}"
                        P.op("pe", lambda e, kh=kh, qh=qh, sc_bank=sc_bank:
                             e.matmul(sc_bank[:, 0:128], lhsT=kh, rhs=qh,
                                      start=True, stop=True),
                             r=("q_sb", "k_sb"), w=(sck,), tag="scT")
                        P.op("dve", lambda e, sc_bank=sc_bank, pi=pi:
                             e.tensor_tensor(out=mtb[pi][:],
                                             in0=sc_bank[:, 0:128],
                                             in1=ut128[:], op=ALU.mult),
                             r=(sck, "ut128"), w=(f"mtb{pi}",), tag="mask")
                        P.op("pe", lambda e, pi=pi, vh=vh, nd_bank=nd_bank:
                             e.matmul(nd_bank[:, 0:64], lhsT=mtb[pi][:],
                                      rhs=vh, start=True, stop=False),
                             r=(f"mtb{pi}", "v_sb"), w=(ndk,), tag="num1")
                        P.op("pe", lambda e, pi=pi, nd_bank=nd_bank:
                             e.matmul(nd_bank[:, 64:65], lhsT=mtb[pi][:],
                                      rhs=ones_bf[:], start=False, stop=False),
                             r=(f"mtb{pi}", "ones_bf"), w=(ndk,), tag="den1")
                        P.op("pe", lambda e, qh=qh, jp=jp, b0=b0,
                             nd_bank=nd_bank: e.matmul(
                                nd_bank[:, 0:64], lhsT=qh,
                                rhs=S_bf[b0:b0 + 64, jp * 65:jp * 65 + 64],
                                start=False, stop=False),
                             r=("q_sb", "S_bf"), w=(ndk,), tag="num2")
                        P.op("pe", lambda e, qh=qh, jp=jp, b0=b0,
                             nd_bank=nd_bank: e.matmul(
                                nd_bank[:, 64:65], lhsT=qh,
                                rhs=S_bf[b0:b0 + 64, jp * 65 + 64:jp * 65 + 65],
                                start=False, stop=True),
                             r=("q_sb", "S_bf"), w=(ndk,), tag="den2")
                        P.op("pe", lambda e, kh=kh, b0=b0, pi=pi: e.matmul(
                                pb[1][:, pi * 64:pi * 64 + 64], lhsT=kh,
                                rhs=id64d[b0:b0 + 64, :], is_transpose=True,
                                start=(pi == 0), stop=(pi == 1)),
                             r=("k_sb", "id64d"), w=("PB1",), tag="kT")
                        P.op("dve", lambda e, nd_bank=nd_bank: e.reciprocal(
                                out=rden[:], in_=nd_bank[:, 64:65]),
                             r=(ndk,), w=("rden",), tag="rden", swb=True)
                        P.op("dve", lambda e, abf=abf, h=h, nd_bank=nd_bank,
                             apar=apar: e.tensor_scalar(
                                out=abf[:, h * 64:h * 64 + 64],
                                in0=nd_bank[:, 0:64], scalar1=rden[:],
                                scalar2=None, op0=ALU.mult),
                             r=(ndk, "rden"), w=(f"attnbf{apar}",), tag="attno",
                             early=True)
                    P.op("act", lambda e: e.copy(out=ktp_sb[:],
                                                 in_=pb[1][:, 0:128]),
                         r=("PB1",), w=("ktp",), tag="ktp_ev")
                    for pi in range(2):
                        b0 = pi * 64
                        h = 2 * jp + pi
                        vh = v_sb[:, m * 1024 + h * 64:m * 1024 + h * 64 + 64]
                        P.op("pe", lambda e, pi=pi, b0=b0, vh=vh: e.matmul(
                                pf[4][b0:b0 + 64, 0:64],
                                lhsT=ktp_sb[:, pi * 64:pi * 64 + 64], rhs=vh,
                                start=True, stop=False,
                                tile_position=(0, b0)),
                             r=("ktp", "v_sb"), w=("PF4",), tag="dS")
                        P.op("pe", lambda e, pi=pi, b0=b0: e.matmul(
                                pf[4][b0:b0 + 64, 64:65],
                                lhsT=ktp_sb[:, pi * 64:pi * 64 + 64],
                                rhs=ones_bf[:], start=False, stop=(pi == 1),
                                tile_position=(0, b0)),
                             r=("ktp", "ones_bf"), w=("PF4",), tag="dz")
                    P.op("dve", lambda e, jp=jp: e.tensor_tensor(
                            out=S_bf[:, jp * 65:(jp + 1) * 65],
                            in0=S_sb[:, jp * 65:(jp + 1) * 65],
                            in1=pf[4][:, 0:65], op=ALU.add),
                         r=("S_sb", "PF4"), w=("S_bf",), tag="Snew")
                    P.op("dve", lambda e, jp=jp: e.tensor_tensor(
                            out=S_sb[:, jp * 65:(jp + 1) * 65],
                            in0=S_sb[:, jp * 65:(jp + 1) * 65],
                            in1=pf[4][:, 0:65], op=ALU.add),
                         r=("S_sb", "PF4"), w=("S_sb",), tag="Supd")

            def ab_post_stage(c, m):
                par = c % 2
                xk = f"xA{par}"
                xt = x_sb[par]
                apar = m % 4
                abf = attn_bf[apar]
                xv = xt[:, m * 1024:(m + 1) * 1024]
                transpose_pack(lambda j, abf=abf: abf[:, j * 128:(j + 1) * 128],
                               (f"attnbf{apar}",), pb[0], "PB0")
                P.op("act", lambda e: e.copy(out=attnT[:], in_=pb[0][:]),
                     r=("PB0",), w=("attnT",), tag="attnT_ev")
                for hf in range(2):
                    for j in range(8):
                        P.op("pe", lambda e, j=j, hf=hf:
                             e.matmul(pf[5][:],
                                      lhsT=attnT[:, j * 128:(j + 1) * 128],
                                      rhs=wo_sb[:, j * 1024 + hf * 512:
                                                j * 1024 + hf * 512 + 512],
                                      start=(j == 0), stop=False),
                             r=("attnT", "woT"), w=("PF5",), tag="omm")
                    P.op("pe", lambda e, hf=hf: e.matmul(
                            pf[5][:], lhsT=onesrow_bf[:],
                            rhs=bob[0:1, hf * 512:hf * 512 + 512],
                            start=False, stop=True),
                         r=("bo_row", "onesrow_bf"), w=("PF5",), tag="obmm")
                    P.op("act", lambda e, hf=hf: e.copy(
                            out=y_sb[:, hf * 512:(hf + 1) * 512], in_=pf[5][:]),
                         r=("PF5",), w=("yA",), tag="oev")
                ln(y_sb[:], ("yA",), [(lnscr2[:], "lnscr2")])
                P.op("dve", lambda e, xv=xv: e.tensor_tensor(
                        out=xv, in0=xv, in1=lnscr2[:], op=ALU.add),
                     r=("lnscr2",), w=(xk,), tag="xadd1")
                nb = nbf2[m % 2]
                nbk = f"nbf{m % 2}"
                ln(xv, (xk,), [(nb[:], nbk)])
                transpose_pack(lambda j, nb=nb: nb[:, j * 128:(j + 1) * 128],
                               (nbk,), pb[0], "PB0")
                P.op("act", lambda e: e.copy(out=n3T[:], in_=pb[0][:]),
                     r=("PB0",), w=("n3T",), tag="n3T_ev")
                for half, w_lo, wk_lo, w_hi, wk_hi, bbr, brk in (
                        (0, wrT, "wrT", wiTn, "wiTn", brlb, "brl_row"),
                        (1, wiT, "wiT", wrT, "wrT", bilb, "bil_row")):
                    for j in range(8):
                        wsb = w_lo if j < 4 else w_hi
                        wky = wk_lo if j < 4 else wk_hi
                        qq = j if j < 4 else j - 4
                        P.op("pe", lambda e, j=j, qq=qq, wsb=wsb:
                             e.matmul(pf[5][:],
                                      lhsT=n3T[:, j * 128:(j + 1) * 128],
                                      rhs=wsb[:, qq * 512:(qq + 1) * 512],
                                      start=(j == 0), stop=False),
                             r=("n3T", wky), w=("PF5",), tag="bmm")
                    P.op("pe", lambda e, bbr=bbr: e.matmul(
                            pf[5][:], lhsT=onesrow_bf[:], rhs=bbr[0:1, :],
                            start=False, stop=True),
                         r=(brk, "onesrow_bf"), w=("PF5",), tag="bbmm")
                    P.op("act", lambda e, half=half: e.copy(
                            out=f_sb[:, half * 512:(half + 1) * 512],
                            in_=pf[5][:]),
                         r=("PF5",), w=("fA",), tag="bev")
                ln(f_sb[:], ("fA",), [(lnscr2[:], "lnscr2")])
                P.op("dve", lambda e, xv=xv: e.tensor_tensor(
                        out=xv, in0=xv, in1=lnscr2[:], op=ALU.add),
                     r=("lnscr2",), w=(xk,), tag="xadd2")

            def ab_store(c):
                par = c % 2
                xk = f"xA{par}"
                xt = x_sb[par]
                dst = xst_d[c * CH:(c + 1) * CH, :].rearrange(
                    "(m p) d -> p m d", p=128)
                P.dma("act", lambda e, xt=xt, dst=dst: e.dma_start(
                        out=dst, in_=xt[:].rearrange("p (m d) -> p m d", m=MT)),
                      r=(xk,), w=(f"xst{c}",), tag=f"stxA{c}")

            for c in range(NCH):
                par = c % 2
                xk = f"xA{par}"
                xt = x_sb[par]
                src = DI["x0"][c * CH:(c + 1) * CH, :].rearrange(
                    "(m p) d -> p m d", p=128)
                P.dma("sp", lambda e, xt=xt, src=src: e.dma_start(
                        out=xt[:].rearrange("p (m d) -> p m d", m=MT), in_=src),
                      r=("x0_dram",), w=(xk,), tag=f"ldxA{c}")
                for m in range(MT):
                    xv = xt[:, m * 1024:(m + 1) * 1024]
                    ln_stats(xv, (xk,), stAB, "stAB", m)
                for m in range(MT):
                    xv = xt[:, m * 1024:(m + 1) * 1024]
                    nb = nbf2[m % 2]
                    nbk = f"nbf{m % 2}"
                    ln_apply(xv, (xk,), nb[:], nbk, stAB, "stAB", m)
                    tb = pb[m % 2]
                    transpose_pack(lambda j, nb=nb: nb[:, j * 128:(j + 1) * 128],
                                   (nbk,), tb, f"PB{m % 2}")
                    dv = n1T[:].rearrange("p (j t) -> p j t", j=8)[
                        :, :, m * 128:(m + 1) * 128]
                    sv = tb[:].rearrange("p (j t) -> p j t", j=8)
                    P.op("act", lambda e, dv=dv, sv=sv: e.copy(out=dv, in_=sv),
                         r=(f"PB{m % 2}",), w=("n1T",), tag="n1T_ev")
                for nm, wsb, wkey, bsb, bkey, dst, dkey in (
                        ("q", wq_sb, "wqT", bqc_sb, "bq_c", q_sb, "q_sb"),
                        ("k", wk_sb, "wkT", bkc_sb, "bk_c", k_sb, "k_sb")):
                    for jo in range(8):
                        bank = pf[jo % 2]
                        bk_key = f"PF{jo % 2}"
                        for ji in range(8):
                            P.op("pe", lambda e, jo=jo, ji=ji, wsb=wsb, bank=bank:
                                 e.matmul(bank[:],
                                          lhsT=wsb[:, ji * 1024 + jo * 128:
                                                   ji * 1024 + jo * 128 + 128],
                                          rhs=n1T[:, ji * 512:(ji + 1) * 512],
                                          start=(ji == 0), stop=(ji == 7)),
                                 r=(wkey, "n1T"), w=(bk_key,), tag=f"{nm}mm")
                        P.op("act", lambda e, jo=jo, bsb=bsb, dst=dst, bank=bank:
                             e.activation(out=dst[:, jo * 512:(jo + 1) * 512],
                                          in_=bank[:], func=AF.Exp,
                                          bias=bsb[:, jo:jo + 1], scale=1.0),
                             r=(bk_key, bkey), w=(dkey,), tag=f"{nm}exp")
                for m in range(MT):
                    for hf in range(2):
                        bank = pf[2 + hf]
                        bk_key = f"PF{2 + hf}"
                        for ji in range(8):
                            P.op("pe", lambda e, m=m, hf=hf, ji=ji, bank=bank:
                                 e.matmul(bank[:],
                                          lhsT=n1T[:, ji * 512 + m * 128:
                                                   ji * 512 + m * 128 + 128],
                                          rhs=wv_sb[:, ji * 1024 + hf * 512:
                                                    ji * 1024 + hf * 512 + 512],
                                          start=(ji == 0), stop=False),
                                 r=("wvT", "n1T"), w=(bk_key,), tag="vmm")
                        P.op("pe", lambda e, hf=hf, bank=bank: e.matmul(
                                bank[:], lhsT=onesrow_bf[:],
                                rhs=bvb[0:1, hf * 512:hf * 512 + 512],
                                start=False, stop=True),
                             r=("bv_row", "onesrow_bf"), w=(bk_key,), tag="vbmm")
                        P.op("act", lambda e, m=m, hf=hf, bank=bank: e.copy(
                                out=v_sb[:, m * 1024 + hf * 512:
                                         m * 1024 + hf * 512 + 512],
                                in_=bank[:]),
                             r=(bk_key,), w=("v_sb",), tag="vev")
                ab_attn_stage(c, 0)
                if c > 0:
                    ab_post_stage(c - 1, 2)
                ab_attn_stage(c, 1)
                if c > 0:
                    ab_post_stage(c - 1, 3)
                    ab_store(c - 1)
                ab_attn_stage(c, 2)
                ab_post_stage(c, 0)
                ab_attn_stage(c, 3)
                ab_post_stage(c, 1)
            ab_post_stage(NCH - 1, 2)
            ab_post_stage(NCH - 1, 3)
            ab_store(NCH - 1)
            P.flush("AB")
            abctx.close()
            P.barrier()

        if "C" in phases:
            cctx = ExitStack()
            csb = lambda name, shp, dt: cctx.enter_context(
                nc.sbuf_tensor(name, shp, dt))
            bre_sb = csb("bre_sb", [128, 8192], BF16)
            bim_sb = csb("bim_sb", [128, 8192], BF16)
            cre_sb = csb("cre_sb", [128, 8192], BF16)
            cim_sb = csb("cim_sb", [128, 8192], BF16)
            lp_re = csb("lp_re", [128, 1024], F32)
            lp_im = csb("lp_im", [128, 1024], F32)
            li_re = csb("li_re", [128, 1024], F32)
            li_imN = csb("li_imN", [128, 1024], F32)
            drep = csb("drep", [128, 1024], F32)

            pctx = ExitStack()
            psb = lambda name, shp, dt: pctx.enter_context(
                nc.sbuf_tensor(name, shp, dt))
            nulog_sb = psb("nulog_sb", [1, 1024], F32)
            thlog_sb = psb("thlog_sb", [1, 1024], F32)
            dskip_sb = psb("dskip_sb", [1, 1024], F32)
            nu_row = psb("nu_row", [1, 1024], F32)
            th_row = psb("th_row", [1, 1024], F32)
            g_row = psb("g_row", [1, 1024], F32)
            lpm_sb = psb("lpm_sb", [128, 1024], F32)
            lim_sb = psb("lim_sb", [128, 1024], F32)
            sin_sb = psb("sin_sb", [128, 1024], F32)
            cos_sb = psb("cos_sb", [128, 1024], F32)
            cscrA = psb("cscrA", [128, 512], F32)
            cscrB = psb("cscrB", [128, 512], F32)
            grep = psb("grep", [128, 1024], F32)
            bstage = psb("bstage", [128, 2048], F32)
            uscr = psb("uscr_prep", [128, 1024], F32)

            ldc("nulog_row", nulog_sb)
            ldc("thetalog_row", thlog_sb)
            ldc("dskip_row", dskip_sb)
            P.op("act", lambda e: e.activation(out=nu_row[:], in_=nulog_sb[:],
                                               func=AF.Exp),
                 r=("nulog_row",), w=("nu_row",), tag="nu")
            P.op("act", lambda e: e.activation(out=th_row[:], in_=thlog_sb[:],
                                               func=AF.Exp),
                 r=("thetalog_row",), w=("th_row",), tag="th")
            for h2 in range(2):
                c0, c1 = h2 * 512, (h2 + 1) * 512
                P.op("pe", lambda e, c0=c0, c1=c1: e.matmul(
                        pf[0][:], lhsT=iota1_row[:], rhs=nu_row[0:1, c0:c1],
                        start=True, stop=True),
                     r=("nu_row", "iota1_row"), w=("PF0",), tag="tpn")
                P.op("act", lambda e, c0=c0, c1=c1: e.activation(
                        out=lpm_sb[:, c0:c1], in_=pf[0][:], func=AF.Exp,
                        scale=-1.0),
                     r=("PF0",), w=("lpm",), tag="lpmag")
                P.op("act", lambda e, c0=c0, c1=c1: e.activation(
                        out=lim_sb[:, c0:c1], in_=pf[0][:], func=AF.Exp,
                        scale=1.0),
                     r=("PF0",), w=("lim",), tag="limag")
                P.op("pe", lambda e, c0=c0, c1=c1: e.matmul(
                        pf[1][:], lhsT=iota1_row[:], rhs=th_row[0:1, c0:c1],
                        start=True, stop=True),
                     r=("th_row", "iota1_row"), w=("PF1",), tag="tpt")
                P.op("dve", lambda e: e.tensor_scalar(
                        out=cscrA[:], in0=pf[1][:], scalar1=PIF, scalar2=None,
                        op0=ALU.is_gt),
                     r=("PF1",), w=("cscrA",), tag="phmask")
                P.op("dve", lambda e: e.scalar_tensor_tensor(
                        out=cscrB[:], in0=cscrA[:], scalar=-2.0 * PIF,
                        in1=pf[1][:], op0=ALU.mult, op1=ALU.add),
                     r=("cscrA", "PF1"), w=("cscrB",), tag="phred")
                P.op("act", lambda e, c0=c0, c1=c1: e.activation(
                        out=sin_sb[:, c0:c1], in_=cscrB[:], func=AF.Sin),
                     r=("cscrB",), w=("sin",), tag="sin")
                P.op("act", lambda e: e.activation(
                        out=cscrA[:], in_=cscrB[:], func=AF.Sin, scale=0.5),
                     r=("cscrB",), w=("cscrA",), tag="sinh")
                P.op("act", lambda e: e.activation(
                        out=cscrA[:], in_=cscrA[:], func=AF.Square),
                     r=("cscrA",), w=("cscrA",), tag="sq")
                P.op("dve", lambda e, c0=c0, c1=c1: e.tensor_scalar(
                        out=cos_sb[:, c0:c1], in0=cscrA[:], scalar1=-2.0,
                        scalar2=1.0, op0=ALU.mult, op1=ALU.add),
                     r=("cscrA",), w=("cos",), tag="cos")
            P.op("dve", lambda e: e.tensor_tensor(out=lp_re[:], in0=lpm_sb[:],
                                                  in1=cos_sb[:], op=ALU.mult),
                 r=("lpm", "cos"), w=("lp_re",), tag="lpre")
            P.op("dve", lambda e: e.tensor_tensor(out=lp_im[:], in0=lpm_sb[:],
                                                  in1=sin_sb[:], op=ALU.mult),
                 r=("lpm", "sin"), w=("lp_im",), tag="lpim")
            P.op("dve", lambda e: e.tensor_tensor(out=li_re[:], in0=lim_sb[:],
                                                  in1=cos_sb[:], op=ALU.mult),
                 r=("lim", "cos"), w=("li_re",), tag="lire")
            P.op("dve", lambda e: e.tensor_tensor(out=li_imN[:], in0=lim_sb[:],
                                                  in1=sin_sb[:], op=ALU.mult),
                 r=("lim", "sin"), w=("li_imN",), tag="liimN")
            dbgdump(10240, lp_re[:], ("lp_re",), 1024, always=True)
            dbgdump(11264, li_re[:], ("li_re",), 1024, always=True)
            dbgdump(12288, lp_im[:], ("lp_im",), 1024, always=True)
            dbgdump(13312, li_imN[:], ("li_imN",), 1024, always=True)
            dbgdump(14336, grep[:], ("grep",), 1024, always=True)
            dbgdump(15360, drep[:], ("drep",), 1024, always=True)
            dbgdump(4096, bre_sb[:, 0:2048], ("bre",), 2048, always=True)
            dbgdump(6144, cre_sb[:, 0:2048], ("cre",), 2048, always=True)
            if DBG2:
                P.dma("pool", lambda e: e.dma_start(out=dbg_d[0:1, 9216:10240],
                                                    in_=g_row[:]),
                      r=("g_row",), w=("dbg_dram",), tag="dbg_grow")
                P.dma("pool", lambda e: e.dma_start(out=dbg_d[1:2, 9216:10240],
                                                    in_=dskip_sb[:]),
                      r=("dskip_row",), w=("dbg_dram",), tag="dbg_drow")
            P.op("act", lambda e: e.activation(out=g_row[:], in_=nu_row[:],
                                               func=AF.Exp, scale=-2.0),
                 r=("nu_row",), w=("g_row",), tag="e2")
            P.op("dve", lambda e: e.tensor_scalar(out=g_row[:], in0=g_row[:],
                                                  scalar1=-1.0, scalar2=1.0 + EPS,
                                                  op0=ALU.mult, op1=ALU.add),
                 r=("g_row",), w=("g_row",), tag="g1m")
            P.op("act", lambda e: e.activation(out=g_row[:], in_=g_row[:],
                                               func=AF.Ln),
                 r=("g_row",), w=("g_row",), tag="glng")
            P.op("act", lambda e: e.activation(out=g_row[:], in_=g_row[:],
                                               func=AF.Exp, scale=0.5),
                 r=("g_row",), w=("g_row",), tag="gexp")
            repl_row(g_row[:], "g_row", grep, "grep", 1024)
            repl_row(dskip_sb[:], "dskip_row", drep, "drep", 1024)
            for wnm, dst, dkey in (("breT", bre_sb, "bre"), ("bimT", bim_sb, "bim")):
                for jq in range(4):
                    P.dma("sp", lambda e, d=DI[wnm], jq=jq: e.dma_start(
                            out=bstage[:].rearrange("p (j o) -> p j o", j=2),
                            in_=d[jq * 256:(jq + 1) * 256, :].rearrange(
                                "(j p) o -> p j o", p=128)),
                          w=("bstage",), tag=f"ld_{wnm}{jq}")
                    for j2 in range(2):
                        j = jq * 2 + j2
                        P.op("dve", lambda e, j=j, j2=j2, dst=dst: e.tensor_tensor(
                                out=dst[:, j * 1024:(j + 1) * 1024],
                                in0=bstage[:, j2 * 1024:(j2 + 1) * 1024],
                                in1=grep[:], op=ALU.mult),
                             r=("bstage", "grep"), w=(dkey,), tag=f"scale_{wnm}")
            for wnm, dst, dkey in (("creT", cre_sb, "cre"), ("cimT", cim_sb, "cim")):
                P.dma("pool", lambda e, d=DI[wnm], dst=dst: e.dma_start(
                        out=dst[:].rearrange("p (j o) -> p j o", j=8),
                        in_=d.rearrange("(j p) o -> p j o", p=128)),
                      w=(dkey,), tag=f"ld_{wnm}")
            P.flush("Cprep")
            pctx.close()
            P.barrier()

            xC = [csb(f"xC{i}", [128, 4096], F32) for i in range(2)]
            nbfC2 = [csb(f"nbfC{i}", [128, 1024], BF16) for i in range(2)]
            n1TC = [csb(f"n1TC{i}", [128, 4096], BF16) for i in range(2)]
            lnstC = [csb(f"lnstC{i}", [128, 8], F32) for i in range(2)]
            uA_bf = csb("uA_bf", [128, 2048], BF16)
            uB_bf = csb("uB_bf", [128, 2048], BF16)
            skscr = csb("skscr", [128, 1024], F32)
            sksc2 = csb("sksc2", [128, 1024], F32)
            ct1 = csb("ct1", [128, 512], F32)
            ct2 = csb("ct2", [128, 512], F32)
            hs_bf = [csb(f"hs_bf{i}", [128, 2048], BF16) for i in range(2)]
            hsT = csb("hsT", [128, 2048], BF16)
            carry = csb("carry", [1, 2048], BF16)
            yC = csb("yC", [128, 1024], F32)
            P.op("dve", lambda e: e.memset(carry[:], 0.0), w=("carry",),
                 tag="carry0", swb=True)
            P.op("dve", lambda e: e.memset(hs_bf[0][:], 0.0), w=("hs_bf0",),
                 tag="hs0a", swb=True)
            P.op("dve", lambda e: e.memset(hs_bf[1][:], 0.0), w=("hs_bf1",),
                 tag="hs0b", swb=True)

            def c_scan_stage(c, m):
                par = c % 2
                ntk = f"n1TC{par}"
                mpar = m % 2
                hk = f"hs_bf{mpar}"
                hsb = hs_bf[mpar]
                for ri, wsb, wky in ((0, bre_sb, "bre"), (1, bim_sb, "bim")):
                    for h2 in range(2):
                        bank = pf[2 * ri + h2]
                        bkk = f"PF{2 * ri + h2}"
                        for ji in range(8):
                            P.op("pe", lambda e, ji=ji, h2=h2, m=m, wsb=wsb,
                                 bank=bank, par=par: e.matmul(
                                    bank[:],
                                    lhsT=n1TC[par][:, m * 1024 + ji * 128:
                                                   m * 1024 + ji * 128 + 128],
                                    rhs=wsb[:, ji * 1024 + h2 * 512:
                                            ji * 1024 + h2 * 512 + 512],
                                    start=(ji == 0), stop=(ji == 7)),
                                 r=(ntk, wky), w=(bkk,), tag="bumm")
                for h2 in range(2):
                    c0, c1 = h2 * 512, (h2 + 1) * 512
                    bre_ps, bim_ps = pf[h2], pf[2 + h2]
                    kre, kim = f"PF{h2}", f"PF{2 + h2}"
                    P.op("dve", lambda e, c0=c0, c1=c1, bre_ps=bre_ps:
                         e.tensor_tensor(out=uA_bf[:, c0:c1],
                                         in0=li_re[:, c0:c1],
                                         in1=bre_ps[:], op=ALU.mult),
                         r=("li_re", kre), w=("uA_bf",), tag="u1")
                    P.op("dve", lambda e, c0=c0, c1=c1, bim_ps=bim_ps:
                         e.tensor_tensor(out=uB_bf[:, c0:c1],
                                         in0=li_imN[:, c0:c1],
                                         in1=bim_ps[:], op=ALU.mult),
                         r=("li_imN", kim), w=("uB_bf",), tag="u2")
                    P.op("dve", lambda e, c0=c0, c1=c1, bim_ps=bim_ps:
                         e.scalar_tensor_tensor(
                            out=uA_bf[:, 1024 + c0:1024 + c1], in0=bim_ps[:],
                            scalar=-1.0,
                            in1=li_re[:, c0:c1], op0=ALU.mult, op1=ALU.mult),
                         r=("li_re", kim), w=("uA_bf",), tag="u4")
                    P.op("dve", lambda e, c0=c0, c1=c1, bre_ps=bre_ps:
                         e.tensor_tensor(out=uB_bf[:, 1024 + c0:1024 + c1],
                                         in0=li_imN[:, c0:c1],
                                         in1=bre_ps[:], op=ALU.mult),
                         r=("li_imN", kre), w=("uB_bf",), tag="u5")
                for seg in range(4):
                    bank = pf[seg]
                    bkk = f"PF{seg}"
                    P.op("pe", lambda e, seg=seg, bank=bank: e.matmul(
                            bank[:], lhsT=ut128b[:],
                            rhs=uA_bf[:, seg * 512:(seg + 1) * 512],
                            start=True, stop=False),
                         r=("uA_bf", "ut128b"), w=(bkk,), tag="cumsA")
                    P.op("pe", lambda e, seg=seg, bank=bank: e.matmul(
                            bank[:], lhsT=ut128b[:],
                            rhs=uB_bf[:, seg * 512:(seg + 1) * 512],
                            start=False, stop=False),
                         r=("uB_bf", "ut128b"), w=(bkk,), tag="cumsB")
                    P.op("pe", lambda e, seg=seg, bank=bank: e.matmul(
                            bank[:], lhsT=onesrow_bf[:],
                            rhs=carry[0:1, seg * 512:(seg + 1) * 512],
                            start=False, stop=True),
                         r=("carry", "onesrow_bf"), w=(bkk,), tag="carrymm")
                for h2 in range(2):
                    c0, c1 = h2 * 512, (h2 + 1) * 512
                    sre, sim = pf[h2], pf[2 + h2]
                    kre, kim = f"PF{h2}", f"PF{2 + h2}"
                    P.op("dve", lambda e, c0=c0, c1=c1, sre=sre: e.tensor_tensor(
                            out=ct1[:], in0=lp_re[:, c0:c1], in1=sre[:],
                            op=ALU.mult),
                         r=("lp_re", kre), w=("ct1",), tag="hs1")
                    P.op("dve", lambda e, c0=c0, c1=c1, sim=sim: e.tensor_tensor(
                            out=ct2[:], in0=lp_im[:, c0:c1], in1=sim[:],
                            op=ALU.mult),
                         r=("lp_im", kim), w=("ct2",), tag="hs2")
                    P.op("pool", lambda e, c0=c0, c1=c1, hsb=hsb: e.tensor_tensor(
                            out=hsb[:, c0:c1], in0=ct1[:], in1=ct2[:],
                            op=ALU.add),
                         r=("ct1", "ct2"), w=(hk,), tag="hs3")
                    P.op("dve", lambda e, c0=c0, c1=c1, sim=sim: e.tensor_tensor(
                            out=ct1[:], in0=lp_re[:, c0:c1], in1=sim[:],
                            op=ALU.mult),
                         r=("lp_re", kim), w=("ct1",), tag="hs4")
                    P.op("dve", lambda e, c0=c0, c1=c1, sre=sre: e.scalar_tensor_tensor(
                            out=ct2[:], in0=sre[:], scalar=-1.0,
                            in1=lp_im[:, c0:c1], op0=ALU.mult, op1=ALU.mult),
                         r=("lp_im", kre), w=("ct2",), tag="hs5")
                    P.op("pool", lambda e, c0=c0, c1=c1, hsb=hsb: e.tensor_tensor(
                            out=hsb[:, 1024 + c0:1024 + c1], in0=ct1[:],
                            in1=ct2[:], op=ALU.add),
                         r=("ct1", "ct2"), w=(hk,), tag="hs6")
                    for seg in (h2, 2 + h2):
                        bank = pf[seg]
                        bkk = f"PF{seg}"
                        P.op("pe", lambda e, seg=seg, bank=bank, hsb=hsb: e.matmul(
                                bank[0:1, :], lhsT=e127_bf[:],
                                rhs=hsb[:, seg * 512:(seg + 1) * 512],
                                start=True, stop=True),
                             r=(hk, "e127_bf"), w=(bkk,), tag="cext")
                        P.op("act", lambda e, seg=seg, bank=bank: e.copy(
                                out=carry[0:1, seg * 512:(seg + 1) * 512],
                                in_=bank[0:1, :]),
                             r=(bkk,), w=("carry",), tag="cev")


            def c_post_stage(c, m):
                par = c % 2
                xk = f"xC{par}"
                xt = xC[par]
                mpar = m % 2
                hk = f"hs_bf{mpar}"
                hsb = hs_bf[mpar]
                xv = xt[:, m * 1024:(m + 1) * 1024]
                transpose_pack(lambda j, hsb=hsb: hsb[:, j * 128:(j + 1) * 128],
                               (hk,), pb[0], "PB0")
                P.op("act", lambda e: e.copy(out=hsT[:, 0:1024], in_=pb[0][:]),
                     r=("PB0",), w=("hsT",), tag="hsTev1")
                transpose_pack(lambda j, hsb=hsb: hsb[:, 1024 + j * 128:
                                                      1024 + (j + 1) * 128],
                               (hk,), pb[1], "PB1")
                P.op("act", lambda e: e.copy(out=hsT[:, 1024:2048],
                                             in_=pb[1][:]),
                     r=("PB1",), w=("hsT",), tag="hsTev2")
                for hf in range(2):
                    bank = pf[4 + hf]
                    bkk = f"PF{4 + hf}"
                    for jh in range(8):
                        P.op("pe", lambda e, jh=jh, hf=hf, bank=bank: e.matmul(
                                bank[:], lhsT=hsT[:, jh * 128:(jh + 1) * 128],
                                rhs=cre_sb[:, jh * 1024 + hf * 512:
                                           jh * 1024 + hf * 512 + 512],
                                start=(jh == 0), stop=False),
                             r=("hsT", "cre"), w=(bkk,), tag="cmmre")
                    for jh in range(8):
                        P.op("pe", lambda e, jh=jh, hf=hf, bank=bank: e.matmul(
                                bank[:],
                                lhsT=hsT[:, 1024 + jh * 128:
                                         1024 + (jh + 1) * 128],
                                rhs=cim_sb[:, jh * 1024 + hf * 512:
                                           jh * 1024 + hf * 512 + 512],
                                start=False, stop=(jh == 7)),
                             r=("hsT", "cim"), w=(bkk,), tag="cmmim")
                    P.op("act", lambda e, hf=hf, m=m, par=par, xv=xv:
                         e.activation(
                            out=skscr[:, hf * 512:(hf + 1) * 512],
                            in_=xv[:, hf * 512:(hf + 1) * 512],
                            func=AF.Identity,
                            scale=lnstC[par][:, 2 * m:2 * m + 1],
                            bias=lnstC[par][:, 2 * m + 1:2 * m + 2]),
                         r=(xk, f"lnstC{par}"), w=("skscr",), tag="skln")
                    P.op("dve", lambda e, hf=hf: e.tensor_tensor(
                            out=sksc2[:, hf * 512:(hf + 1) * 512],
                            in0=drep[:, hf * 512:(hf + 1) * 512],
                            in1=skscr[:, hf * 512:(hf + 1) * 512], op=ALU.mult),
                         r=("drep", "skscr"), w=("sksc2",), tag="skip")
                    P.op("dve", lambda e, hf=hf, bank=bank: e.tensor_tensor(
                            out=yC[:, hf * 512:(hf + 1) * 512], in0=bank[:],
                            in1=sksc2[:, hf * 512:(hf + 1) * 512], op=ALU.add),
                         r=(bkk, "sksc2"), w=("yC",), tag="cout")
                ln(yC[:], ("yC",), [(lnscr2[:], "lnscr2")])
                P.op("dve", lambda e, xv=xv: e.tensor_tensor(
                        out=xv, in0=xv, in1=lnscr2[:], op=ALU.add),
                     r=("lnscr2",), w=(xk,), tag="xaddC")

            def c_leadin(c):
                par = c % 2
                xk = f"xC{par}"
                xt = xC[par]
                ntk = f"n1TC{par}"
                src = xst_d[c * CH:(c + 1) * CH, :].rearrange(
                    "(m p) d -> p m d", p=128)
                P.dma("sp", lambda e, xt=xt, src=src: e.dma_start(
                        out=xt[:].rearrange("p (m d) -> p m d", m=MT), in_=src),
                      r=(f"xst{c}",), w=(xk,), tag=f"ldxC{c}")
                stk = f"lnstC{par}"
                for m in range(MT):
                    xv = xt[:, m * 1024:(m + 1) * 1024]
                    ln_stats(xv, (xk,), lnstC[par], stk, m)
                for m in range(MT):
                    xv = xt[:, m * 1024:(m + 1) * 1024]
                    nb = nbfC2[m % 2]
                    nbk = f"nbfC{m % 2}"
                    ln_apply(xv, (xk,), nb[:], nbk, lnstC[par], stk, m)
                    tb = pb[m % 2]
                    transpose_pack(lambda j, nb=nb: nb[:, j * 128:(j + 1) * 128],
                                   (nbk,), tb, f"PB{m % 2}")
                    P.op("act", lambda e, m=m, tb=tb, par=par: e.copy(
                            out=n1TC[par][:, m * 1024:(m + 1) * 1024],
                            in_=tb[:]),
                         r=(f"PB{m % 2}",), w=(ntk,), tag="n1Tev")

            def c_store(c):
                par = c % 2
                xk = f"xC{par}"
                xt = xC[par]
                dst = xst_d[c * CH:(c + 1) * CH, :].rearrange(
                    "(m p) d -> p m d", p=128)
                P.dma("act", lambda e, xt=xt, dst=dst: e.dma_start(
                        out=dst, in_=xt[:].rearrange("p (m d) -> p m d", m=MT)),
                      r=(xk,), w=(f"xst{c}",), tag=f"stxC{c}")

            c_leadin(0)
            for c in range(NCH):
                c_scan_stage(c, 0)
                if c > 0:
                    c_post_stage(c - 1, 3)
                    c_store(c - 1)
                c_scan_stage(c, 1)
                c_post_stage(c, 0)
                if c + 1 < NCH:
                    c_leadin(c + 1)
                c_scan_stage(c, 2)
                c_post_stage(c, 1)
                c_scan_stage(c, 3)
                c_post_stage(c, 2)
            c_post_stage(NCH - 1, 3)
            c_store(NCH - 1)
            P.flush("C")
            cctx.close()
            P.barrier()

        if "D" in phases:
            dctx = ExitStack()
            dsb = lambda name, shp, dt: dctx.enter_context(
                nc.sbuf_tensor(name, shp, dt))
            n3Tgs = [dsb(f"n3Tg{i}", [128, 8192], BF16) for i in range(2)]
            nbfD2 = [dsb(f"nbfD{i}", [128, 1024], BF16) for i in range(2)]
            stD = dsb("stD", [128, 16], F32)
            f_g = dsb("f_g", [128, 8192], F32)
            gatess = [dsb(f"gates{i}", [128, 64], F32) for i in range(2)]
            w1d = [dsb(f"w1d{i}", [128, 8192], BF16) for i in range(2)]
            w3d = [dsb(f"w3d{i}", [128, 8192], BF16) for i in range(2)]
            w2d = dsb("w2d", [128, 8192], BF16)
            rw_sb = dsb("rw_sb", [128, 64], BF16)
            ebrep = dsb("ebrep", [128, 8], F32)
            ebrow = dsb("ebrow", [1, 8], F32)
            xmt = [dsb(f"xmt{i}", [128, 1024], F32) for i in range(4)]
            xrt = dsb("xrt", [128, 1024], F32)
            y1s = dsb("y1s", [128, 1024], F32)
            mid = dsb("mid", [128, 4096], BF16)
            gz = dsb("gz", [128, 8], F32)
            gp = dsb("gp", [128, 8], F32)
            gt = dsb("gt", [128, 8], F32)
            gt2 = dsb("gt2", [128, 8], F32)
            gtop1 = dsb("gtop1", [128, 8], F32)
            gtop2 = dsb("gtop2", [128, 8], F32)
            gmx = dsb("gmx", [128, 1], F32)
            gnm = dsb("gnm", [128, 1], F32)

            P.dma("pool", lambda e: e.dma_start(
                    out=rw_sb[:].rearrange("p (j x) -> p j x", j=8),
                    in_=DI["rw"].rearrange("(j p) x -> p j x", p=128)),
                  w=("rw",), tag="ld_rw")
            ldc("ebias_row", ebrow)
            P.op("pe", lambda e: e.matmul(pf[5][:, 0:8], lhsT=onesrow_f[:],
                                          rhs=ebrow[:], start=True, stop=True),
                 r=("ebias_row", "onesrow_f"), w=("PF5",), tag="ebmm")
            P.op("act", lambda e: e.copy(out=ebrep[:], in_=pf[5][:, 0:8]),
                 r=("PF5",), w=("ebrep",), tag="ebcp")

            exlist = [("sh", "sw1", "sw2", "sw3", None)] + [
                (f"e{ee}", "w1", "w2", "w3", ee) for ee in range(8)]

            def d_li_stats(g):
                for mt in range(8):
                    xb = xmt[mt % 4]
                    xbk = f"xmt{mt % 4}"
                    src = xst_d[g * GT + mt * 128:g * GT + (mt + 1) * 128, :]
                    P.dma("sp", lambda e, xb=xb, src=src: e.dma_start(
                            out=xb[:], in_=src),
                          r=(f"xst{2 * g + mt // 4}",), w=(xbk,),
                          tag=f"ldxD{g}_{mt}")
                    ln_stats(xb[:], (xbk,), stD, "stD", mt)

            def d_li_applyall(g):
                gpar = g % 2
                n3 = n3Tgs[gpar]
                n3k = f"n3Tg{gpar}"
                gat = gatess[gpar]
                gatk = f"gates{gpar}"
                def d_li_apply(mt):
                    xb = xmt[mt % 4]
                    xbk = f"xmt{mt % 4}"
                    src = xst_d[g * GT + mt * 128:g * GT + (mt + 1) * 128, :]
                    P.dma("sp", lambda e, xb=xb, src=src: e.dma_start(
                            out=xb[:], in_=src),
                          r=(f"xst{2 * g + mt // 4}",), w=(xbk,),
                          tag=f"ldxD2_{g}_{mt}")
                    nb = nbfD2[mt % 2]
                    nbk = f"nbfD{mt % 2}"
                    ln_apply(xb[:], (xbk,), nb[:], nbk, stD, "stD", mt)
                    tb = pb[mt % 2]
                    transpose_pack(lambda j, nb=nb: nb[:, j * 128:(j + 1) * 128],
                                   (nbk,), tb, f"PB{mt % 2}")
                    dv = n3[:].rearrange("p (j t) -> p j t", j=8)[
                        :, :, mt * 128:(mt + 1) * 128]
                    sv = tb[:].rearrange("p (j t) -> p j t", j=8)
                    P.op("dve", lambda e, dv=dv, sv=sv: e.tensor_copy(dv, sv),
                         r=(f"PB{mt % 2}",), w=(n3k,), tag="n3Tgev")
                    for j in range(8):
                        P.op("pe", lambda e, j=j, mt=mt, n3=n3: e.matmul(
                                pf[4][:, 0:8],
                                lhsT=n3[:, j * 1024 + mt * 128:
                                        j * 1024 + mt * 128 + 128],
                                rhs=rw_sb[:, j * 8:(j + 1) * 8],
                                start=(j == 0), stop=(j == 7)),
                             r=(n3k, "rw"), w=("PF4",), tag="rmm")
                    P.op("dve", lambda e: e.tensor_reduce(
                            out=gmx[:], in_=pf[4][:, 0:8], axis=AX, op=ALU.max),
                         r=("PF4",), w=("gmx",), tag="gmx", swb=True)
                    P.op("dve", lambda e: e.tensor_scalar(
                            out=gnm[:], in0=gmx[:], scalar1=-1.0, scalar2=None,
                            op0=ALU.mult),
                         r=("gmx",), w=("gnm",), tag="gnm", early=True)
                    P.op("act", lambda e: e.activation(
                            out=gz[:], in_=pf[4][:, 0:8], func=AF.Exp,
                            bias=gnm[:]),
                         r=("PF4", "gnm"), w=("gz",), tag="gexp")
                    P.op("dve", lambda e: e.tensor_reduce(
                            out=gmx[:], in_=gz[:], axis=AX, op=ALU.add),
                         r=("gz",), w=("gmx",), tag="gsum", swb=True)
                    P.op("dve", lambda e: e.reciprocal(out=gnm[:], in_=gmx[:]),
                         r=("gmx",), w=("gnm",), tag="grs", swb=True)
                    P.op("dve", lambda e: e.tensor_scalar(
                            out=gp[:], in0=gz[:], scalar1=gnm[:], scalar2=None,
                            op0=ALU.mult),
                         r=("gz", "gnm"), w=("gp",), tag="gp", early=True)
                    P.op("dve", lambda e: e.tensor_tensor(
                            out=gt[:], in0=gp[:], in1=ebrep[:], op=ALU.add),
                         r=("gp", "ebrep"), w=("gt",), tag="gt")
                    P.op("dve", lambda e: e.tensor_reduce(
                            out=gmx[:], in_=gt[:], axis=AX, op=ALU.max),
                         r=("gt",), w=("gmx",), tag="gm1", swb=True)
                    P.op("dve", lambda e: e.tensor_scalar(
                            out=gtop1[:], in0=gt[:], scalar1=gmx[:], scalar2=None,
                            op0=ALU.is_ge),
                         r=("gt", "gmx"), w=("gtop1",), tag="gtop1", early=True)
                    P.op("dve", lambda e: e.scalar_tensor_tensor(
                            out=gt2[:], in0=gtop1[:], scalar=-1e9, in1=gt[:],
                            op0=ALU.mult, op1=ALU.add),
                         r=("gtop1", "gt"), w=("gt2",), tag="gt2")
                    P.op("dve", lambda e: e.tensor_reduce(
                            out=gmx[:], in_=gt2[:], axis=AX, op=ALU.max),
                         r=("gt2",), w=("gmx",), tag="gm2", swb=True)
                    P.op("dve", lambda e: e.tensor_scalar(
                            out=gtop2[:], in0=gt2[:], scalar1=gmx[:], scalar2=None,
                            op0=ALU.is_ge),
                         r=("gt2", "gmx"), w=("gtop2",), tag="gtop2", early=True)
                    P.op("dve", lambda e: e.tensor_tensor(
                            out=gtop1[:], in0=gtop1[:], in1=gtop2[:], op=ALU.add),
                         r=("gtop2",), w=("gtop1",), tag="gmask")
                    P.op("dve", lambda e, mt=mt, gat=gat: e.tensor_tensor(
                            out=gat[:, mt * 8:(mt + 1) * 8], in0=gp[:],
                            in1=gtop1[:], op=ALU.mult),
                         r=("gp", "gtop1"), w=(gatk,), tag="gates")

                for mt in range(8):
                    d_li_apply(mt)

            def d_experts(g):
                gpar = g % 2
                n3 = n3Tgs[gpar]
                n3k = f"n3Tg{gpar}"
                gat = gatess[gpar]
                gatk = f"gates{gpar}"
                for xi, (ename, k1, k2, k3, ee) in enumerate(exlist):
                    par = (g * 9 + xi) % 2
                    for knm, wt, wkey in ((k1, w1d[par], f"w1d{par}"),
                                          (k3, w3d[par], f"w3d{par}"),
                                          (k2, w2d, "w2d")):
                        d = DI[knm] if ee is None else DI[knm][ee]
                        P.dma("pool", lambda e, wt=wt, d=d: e.dma_start(
                                out=wt[:].rearrange("p (j o) -> p j o", j=8),
                                in_=d.rearrange("(j p) o -> p j o", p=128)),
                              w=(wkey,), tag=f"ldw_{ename}_{knm}")
                    for u in range(2):
                        for jh in range(8):
                            b1 = pf[jh % 2]
                            b1k = f"PF{jh % 2}"
                            b3 = pf[2 + jh % 2]
                            b3k = f"PF{2 + jh % 2}"
                            for jd in range(8):
                                P.op("pe", lambda e, jd=jd, jh=jh, u=u, par=par,
                                     b1=b1, n3=n3: e.matmul(
                                        b1[:],
                                        lhsT=w1d[par][:, jd * 1024 + jh * 128:
                                                      jd * 1024 + jh * 128 + 128],
                                        rhs=n3[:, jd * 1024 + u * 512:
                                               jd * 1024 + u * 512 + 512],
                                        start=(jd == 0), stop=(jd == 7)),
                                     r=(f"w1d{par}", n3k), w=(b1k,), tag="y1mm")
                            P.op("act", lambda e, jh=jh, b1=b1: e.activation(
                                    out=y1s[:, (jh % 2) * 512:(jh % 2) * 512 + 512],
                                    in_=b1[:], func=AF.Silu),
                                 r=(b1k,), w=("y1s",), tag="silu")
                            for jd in range(8):
                                P.op("pe", lambda e, jd=jd, jh=jh, u=u, par=par,
                                     b3=b3, n3=n3: e.matmul(
                                        b3[:],
                                        lhsT=w3d[par][:, jd * 1024 + jh * 128:
                                                      jd * 1024 + jh * 128 + 128],
                                        rhs=n3[:, jd * 1024 + u * 512:
                                               jd * 1024 + u * 512 + 512],
                                        start=(jd == 0), stop=(jd == 7)),
                                     r=(f"w3d{par}", n3k), w=(b3k,), tag="y3mm")
                            P.op("dve", lambda e, jh=jh, b3=b3: e.tensor_tensor(
                                    out=mid[:, jh * 512:(jh + 1) * 512],
                                    in0=b3[:],
                                    in1=y1s[:, (jh % 2) * 512:(jh % 2) * 512 + 512],
                                    op=ALU.mult),
                                 r=(b3k, "y1s"), w=("mid",), tag="mid")
                        for mt2 in range(4):
                            tile = u * 4 + mt2
                            for dh in range(2):
                                bo = pf[4 + dh]
                                bok = f"PF{4 + dh}"
                                for jh in range(8):
                                    P.op("pe", lambda e, jh=jh, mt2=mt2, dh=dh,
                                         bo=bo: e.matmul(
                                            bo[:],
                                            lhsT=mid[:, jh * 512 + mt2 * 128:
                                                     jh * 512 + mt2 * 128 + 128],
                                            rhs=w2d[:, jh * 1024 + dh * 512:
                                                    jh * 1024 + dh * 512 + 512],
                                            start=(jh == 0), stop=(jh == 7)),
                                         r=("mid", "w2d"), w=(bok,),
                                         tag="y2mm")
                                fsl = f_g[:, tile * 1024 + dh * 512:
                                          tile * 1024 + dh * 512 + 512]
                                if ee is None:
                                    P.op("dve", lambda e, fsl=fsl, bo=bo:
                                         e.tensor_copy(fsl, bo[:]),
                                         r=(bok,), w=("f_g",), tag="fsh")
                                else:
                                    gcol = gat[:, tile * 8 + ee:
                                               tile * 8 + ee + 1]
                                    P.op("dve", lambda e, fsl=fsl, bo=bo,
                                         gcol=gcol: e.scalar_tensor_tensor(
                                            out=fsl, in0=bo[:], scalar=gcol,
                                            in1=fsl, op0=ALU.mult, op1=ALU.add),
                                         r=(bok, gatk, "f_g"), w=("f_g",),
                                         tag="facc", early=True)

            def d_tail(g):
                for mt in range(8):
                    src = xst_d[g * GT + mt * 128:g * GT + (mt + 1) * 128, :]
                    P.dma("sp", lambda e, src=src: e.dma_start(
                            out=xrt[:], in_=src),
                          r=(f"xst{2 * g + mt // 4}",), w=("xrt",),
                          tag=f"ldxT{g}_{mt}")
                    ln(f_g[:, mt * 1024:(mt + 1) * 1024], ("f_g",),
                       [(lnscr2[:], "lnscr2")])
                    P.op("dve", lambda e: e.tensor_tensor(
                            out=xrt[:], in0=xrt[:], in1=lnscr2[:], op=ALU.add),
                         r=("lnscr2",), w=("xrt",), tag="xaddD")
                    dst = out_d[g * GT + mt * 128:g * GT + (mt + 1) * 128, :]
                    P.dma("act", lambda e, dst=dst: e.dma_start(
                            out=dst, in_=xrt[:]),
                          r=("xrt",), w=("out_dram",), tag=f"stD{g}_{mt}")

            d_li_stats(0)
            d_li_applyall(0)
            d_li_stats(1)
            d_li_applyall(1)
            for g in range(NG):
                if g + 2 < NG:
                    d_li_stats(g + 2)
                d_experts(g)
                if g + 2 < NG:
                    d_li_applyall(g + 2)
                d_tail(g)
            P.flush("D")
            dctx.close()

        if "DUMP" in phases:
            for c in range(NCH):
                P.dma("sp", lambda e, c=c: e.dma_start(
                        out=out_d[c * CH:(c + 1) * CH, :],
                        in_=xst_d[c * CH:(c + 1) * CH, :]),
                      r=(f"xst{c}",), w=("out_dram",), tag=f"dump{c}")
            P.flush("DUMP")

        P.op("sp", lambda e: e.nop(),
             r=("out_dram", "dbg_dram") if (DBG or DBG2) else ("out_dram",),
             tag="fence")
        P.flush("FIN")

    return nc


def kernel(**inputs):
    common = _prep_common(inputs)
    x_full = np.ascontiguousarray(np.asarray(inputs["L0_attn_x"], dtype=np.float32))
    nc = build_nc()
    in_maps = []
    for core in range(8):
        m = dict(common)
        m["x0"] = np.ascontiguousarray(x_full[core])
        in_maps.append(m)
    res = run_bass_kernel_spmd(nc, in_maps, core_ids=list(range(8)))
    return np.stack([np.asarray(r["out"], dtype=np.float32)
                     for r in res.results], axis=0)


# version 13
# speedup vs baseline: 1.2167x; 1.0025x over previous
import numpy as np
import ml_dtypes
from contextlib import ExitStack

import concourse.bass as bass
import concourse.mybir as mybir
from concourse.bass_utils import run_bass_kernel_spmd
from concourse.alu_op_type import AluOpType as ALU

F32 = mybir.dt.float32
BF16 = mybir.dt.bfloat16
AF = mybir.ActivationFunctionType
AX = mybir.AxisListType.X
BF16NP = ml_dtypes.bfloat16

S = 4096
DM = 1024
NCH = 8
CH = 512
MT = 4
NG = 4
GT = 1024
PIF = float(np.pi)
EPS = 1e-5
DBG = False
DBG2 = False


class _Op:
    __slots__ = ("i", "eng", "fn", "r", "w", "dma", "dmai", "inc", "incv",
                 "deps", "tag", "flushed", "fend", "waits", "swb", "early")

    def __init__(self, i, eng, fn, r, w, dma, dmai, tag, swb=False, early=False):
        self.i, self.eng, self.fn = i, eng, fn
        self.r, self.w = tuple(r), tuple(w)
        self.dma, self.dmai = dma, dmai
        self.inc, self.incv = False, None
        self.deps, self.tag = [], tag
        self.flushed, self.fend, self.waits = False, None, []
        self.swb, self.early = swb, early


class Prog:
    CENG = ("pe", "act", "dve", "pool", "sp")
    QENG = ("sp", "act", "pool")

    NLANE = 8

    def __init__(self, nc, ctx):
        self.nc = nc
        self.ops = []
        self.pending = []
        self.last_w = {}
        self.readers = {}
        self.inc_cnt = {e: 0 for e in self.CENG}
        self.dma_cnt = {e: 0 for e in self.QENG}
        self.lane_cnt = {e: [0] * self.NLANE for e in self.QENG}
        self.sems = {e: ctx.enter_context(nc.semaphore(f"s_{e}")) for e in self.CENG}
        self.dsems = {e: [ctx.enter_context(nc.semaphore(f"d_{e}{i}"))
                          for i in range(self.NLANE)] for e in self.QENG}
        self.waited = {e: {} for e in self.CENG}
        self.nflush = 0
        self.barrier_req = {}

    def _track(self, o):
        deps = set()
        for k in o.r:
            j = self.last_w.get(k)
            if j is not None:
                deps.add(j)
            self.readers.setdefault(k, []).append(o.i)
        for k in o.w:
            j = self.last_w.get(k)
            if j is not None:
                deps.add(j)
            for rj in self.readers.get(k, ()):
                deps.add(rj)
            self.last_w[k] = o.i
            self.readers[k] = []
        deps.discard(o.i)
        o.deps = sorted(deps)

    def op(self, eng, fn, r=(), w=(), tag="", swb=False, early=False):
        o = _Op(len(self.ops), eng, fn, r, w, False, None, tag, swb, early)
        self._track(o)
        self.ops.append(o)
        self.pending.append(o)
        return o

    def dma(self, eng, fn, r=(), w=(), tag=""):
        assert eng in self.QENG
        lane = self.dma_cnt[eng] % self.NLANE
        self.dma_cnt[eng] += 1
        self.lane_cnt[eng][lane] += 1
        o = _Op(len(self.ops), eng, fn, r, w, True,
                (lane, self.lane_cnt[eng][lane]), tag)
        self._track(o)
        self.ops.append(o)
        self.pending.append(o)
        return o

    def barrier(self):
        assert not self.pending
        snap = {}
        for e2 in self.CENG:
            if self.inc_cnt[e2]:
                snap[("e", e2)] = self.inc_cnt[e2]
        for q in self.QENG:
            for lane in range(self.NLANE):
                if self.lane_cnt[q][lane]:
                    snap[("d", q, lane)] = 16 * self.lane_cnt[q][lane]
        for e in self.CENG:
            self.barrier_req[e] = dict(snap)

    def flush(self, name):
        for o in self.pending:
            for d in o.deps:
                dop = self.ops[d]
                if dop.dma or dop.flushed:
                    continue
                if dop.eng == o.eng and (dop.eng in ("pe", "sp")):
                    continue
                dop.inc = True
        for o in self.pending:
            if o.dma:
                continue
            if o.inc:
                self.inc_cnt[o.eng] += 1
                o.incv = self.inc_cnt[o.eng]
        last = {}
        for o in self.pending:
            if not o.dma:
                last[o.eng] = o
        for o in last.values():
            if o.incv is None:
                o.inc = True
                self.inc_cnt[o.eng] += 1
                o.incv = self.inc_cnt[o.eng]
        first_seen = set()
        for o in self.pending:
            wd = self.waited[o.eng]
            ws = {}
            if o.eng not in first_seen:
                first_seen.add(o.eng)
                breq = self.barrier_req.pop(o.eng, None)
                if breq:
                    for sem, val in breq.items():
                        if val > ws.get(sem, 0):
                            ws[sem] = val
            for d in o.deps:
                dop = self.ops[d]
                if dop.dma:
                    lane, cnt = dop.dmai
                    sem, val = ("d", dop.eng, lane), 16 * cnt
                elif dop.eng == o.eng and (dop.eng in ("pe", "sp")):
                    continue
                elif dop.flushed:
                    sem, val = ("e", dop.eng), dop.fend
                else:
                    assert dop.incv is not None, (dop.tag, o.tag)
                    sem, val = ("e", dop.eng), dop.incv
                if val > ws.get(sem, 0):
                    ws[sem] = val
            o.waits = [(sem, val) for sem, val in sorted(ws.items())
                       if wd.get(sem, 0) < val]
            for sem, val in o.waits:
                wd[sem] = val
        for o in self.pending:
            o.flushed = True
            o.fend = self.inc_cnt[o.eng] if not o.dma else None
        pend = self.pending
        self.pending = []
        self.nflush += 1
        secmap = {"sp": "sync", "pe": "tensor", "act": "scalar",
                  "dve": "vector", "pool": "gpsimd"}
        with self.nc.Block(f"f{self.nflush}_{name}") as blk:
            for engname in self.CENG:
                ops_e = [o for o in pend if o.eng == engname]
                if not ops_e:
                    continue

                def body(eng, ops_e=ops_e, engname=engname):
                    for o in ops_e:
                        for key, val in o.waits:
                            if key[0] == "d":
                                sem = self.dsems[key[1]][key[2]]
                            else:
                                sem = self.sems[key[1]]
                            eng.wait_ge(sem, val)
                        inst = o.fn(eng)
                        if o.dma:
                            inst.then_inc(self.dsems[o.eng][o.dmai[0]], 16)
                        elif o.inc:
                            assert inst is not None, o.tag
                            inst.then_inc(self.sems[o.eng], 1)

                getattr(blk, secmap[engname])(body)


def _prep_common(I):
    a = lambda x: np.ascontiguousarray(np.asarray(x, dtype=np.float32))
    c = {}
    c["wqT"] = a(np.asarray(I["L0_attn_Wq"]).T)
    c["wkT"] = a(np.asarray(I["L0_attn_Wk"]).T)
    c["wvT"] = a(np.asarray(I["L0_attn_Wv"]).T)
    c["woT"] = a(np.asarray(I["L0_attn_Wo"]).T)
    c["bq_c"] = a(np.asarray(I["L0_attn_bq"]).reshape(8, 128).T)
    c["bk_c"] = a(np.asarray(I["L0_attn_bk"]).reshape(8, 128).T)
    c["bv_row"] = a(np.asarray(I["L0_attn_bv"])[None, :])
    c["bo_row"] = a(np.asarray(I["L0_attn_bo"])[None, :])
    c["a_rt"] = a(np.transpose(np.asarray(I["L0_ffn_G1_R"])[0], (1, 2, 0)))
    c["a_it"] = a(np.transpose(np.asarray(I["L0_ffn_G1_I"])[0], (1, 2, 0)))
    c["b_rt"] = a(np.transpose(np.asarray(I["L0_ffn_G2_R"])[:, :, 0, :], (0, 2, 1)))
    c["b_it"] = a(np.transpose(np.asarray(I["L0_ffn_G2_I"])[:, :, 0, :], (0, 2, 1)))
    c["brl_row"] = a(np.asarray(I["L0_ffn_bias_real"])[None, :])
    c["bil_row"] = a(np.asarray(I["L0_ffn_bias_imag"])[None, :])
    c["nulog_row"] = a(np.asarray(I["L1_attn_nu_log"])[None, :])
    c["thetalog_row"] = a(np.asarray(I["L1_attn_theta_log"])[None, :])
    c["breT"] = a(np.asarray(I["L1_attn_B_re"]).T)
    c["bimT"] = a(np.asarray(I["L1_attn_B_im"]).T)
    c["creT"] = a(np.asarray(I["L1_attn_C_re"]).T)
    c["cimT"] = a(np.asarray(I["L1_attn_C_im"]).T)
    c["dskip_row"] = a(np.asarray(I["L1_attn_D_skip"])[None, :])
    c["rw"] = a(I["L1_ffn_router_w"])
    c["ebias_row"] = a(np.asarray(I["L1_ffn_expert_bias"])[None, :])
    c["w1"] = a(I["L1_ffn_w1"])
    c["w2"] = a(I["L1_ffn_w2"])
    c["w3"] = a(I["L1_ffn_w3"])
    c["sw1"] = a(I["L1_ffn_sw1"])
    c["sw2"] = a(I["L1_ffn_sw2"])
    c["sw3"] = a(I["L1_ffn_sw3"])
    c["id128"] = np.eye(128, dtype=BF16NP)
    c["id64d"] = np.vstack([np.eye(64), np.eye(64)]).astype(BF16NP)
    c["ut128"] = np.triu(np.ones((128, 128), dtype=np.float32))
    c["ut128b"] = np.triu(np.ones((128, 128), dtype=np.float32)).astype(BF16NP)
    c["ones_bf"] = np.ones((128, 1), dtype=BF16NP)
    c["onesrow_f"] = np.ones((1, 128), dtype=np.float32)
    c["onesrow_bf"] = np.ones((1, 128), dtype=BF16NP)
    c["iota1_row"] = np.arange(1, 129, dtype=np.float32)[None, :]
    e127 = np.zeros((128, 1), dtype=np.float32)
    e127[127, 0] = 1.0
    c["e127_f"] = e127
    c["e127_bf"] = e127.astype(BF16NP)
    c["eps_col"] = np.full((128, 1), EPS, dtype=np.float32)
    ecq = np.zeros((32, 512), dtype=np.float32)
    for q in range(4):
        for p in range(128):
            ecq[8 * q + p // 16, q * 128 + p] = 1.0
    c["ecq"] = ecq
    ecg = np.zeros((16, 128), dtype=np.float32)
    for p in range(128):
        ecg[p % 16, p] = 1.0
    c["ecg"] = ecg
    return c


_IN_SPECS = {
    "x0": ([S, DM], F32),
    "wqT": ([DM, DM], F32), "wkT": ([DM, DM], F32),
    "wvT": ([DM, DM], F32), "woT": ([DM, DM], F32),
    "bq_c": ([128, 8], F32), "bk_c": ([128, 8], F32),
    "bv_row": ([1, DM], F32), "bo_row": ([1, DM], F32),
    "a_rt": ([4, 32, 32], F32), "a_it": ([4, 32, 32], F32),
    "b_rt": ([4, 16, 16], F32), "b_it": ([4, 16, 16], F32),
    "brl_row": ([1, 512], F32), "bil_row": ([1, 512], F32),
    "nulog_row": ([1, DM], F32), "thetalog_row": ([1, DM], F32),
    "breT": ([DM, DM], F32), "bimT": ([DM, DM], F32),
    "creT": ([DM, DM], F32), "cimT": ([DM, DM], F32),
    "dskip_row": ([1, DM], F32),
    "rw": ([DM, 8], F32), "ebias_row": ([1, 8], F32),
    "w1": ([8, DM, DM], F32), "w2": ([8, DM, DM], F32), "w3": ([8, DM, DM], F32),
    "sw1": ([DM, DM], F32), "sw2": ([DM, DM], F32), "sw3": ([DM, DM], F32),
    "id128": ([128, 128], BF16), "id64d": ([128, 64], BF16),
    "ut128": ([128, 128], F32), "ut128b": ([128, 128], BF16),
    "ones_bf": ([128, 1], BF16),
    "onesrow_f": ([1, 128], F32), "onesrow_bf": ([1, 128], BF16),
    "iota1_row": ([1, 128], F32),
    "e127_f": ([128, 1], F32), "e127_bf": ([128, 1], BF16),
    "eps_col": ([128, 1], F32),
    "ecq": ([32, 512], F32), "ecg": ([16, 128], F32),
}


def build_nc(phases=("P0", "AB", "C", "D")):
    nc = bass.Bass("TRN2", target_bir_lowering=False)
    DI = {k: nc.dram_tensor(k, shp, dt, kind="ExternalInput").ap()
          for k, (shp, dt) in _IN_SPECS.items()}
    out_d = nc.dram_tensor("out", [S, DM], F32, kind="ExternalOutput").ap()
    xst_d = nc.dram_tensor("xst", [S, DM], F32, kind="Internal").ap()
    dbg_d = (nc.dram_tensor("dbg", [128, 16384], F32, kind="ExternalOutput").ap()
             if (DBG or DBG2) else None)

    ctx = ExitStack()
    with ctx:
        P = Prog(nc, ctx)
        sb = lambda name, shp, dt: ctx.enter_context(nc.sbuf_tensor(name, shp, dt))
        ps = lambda name, shp, dt: ctx.enter_context(nc.psum_tensor(name, shp, dt))

        pf = [ps(f"pf{i}", [128, 512], F32) for i in range(6)]
        pb = [ps(f"pb{i}", [128, 1024], BF16) for i in range(2)]

        id128 = sb("t_id128", [128, 128], BF16)
        id64d = sb("t_id64d", [128, 64], BF16)
        ut128 = sb("t_ut128", [128, 128], F32)
        ut128b = sb("t_ut128b", [128, 128], BF16)
        ones_bf = sb("t_ones_bf", [128, 1], BF16)
        onesrow_f = sb("t_onesrow_f", [1, 128], F32)
        onesrow_bf = sb("t_onesrow_bf", [1, 128], BF16)
        iota1_row = sb("t_iota1_row", [1, 128], F32)
        e127_f = sb("t_e127_f", [128, 1], F32)
        e127_bf = sb("t_e127_bf", [128, 1], BF16)
        eps_col = sb("t_eps_col", [128, 1], F32)
        lnst = sb("lnst", [128, 12], F32)
        lnmv = sb("lnmv", [128, 2], F32)
        lnlv = sb("lnlv", [128, 1], F32)
        lnrstd = sb("lnrstd", [128, 1], F32)
        lnnmr = sb("lnnmr", [128, 1], F32)
        lnscr2 = sb("lnscr2", [128, 1024], F32)
        dbgs = sb("dbgs", [128, 80], F32) if DBG else None
        mtb = [sb(f"mtb{i}", [128, 128], BF16) for i in range(2)]

        def dbgdump(col, ap, keys, w, always=False):
            if not (DBG or (DBG2 and always)):
                return
            P.dma("pool", lambda e, ap=ap, col=col, w=w: e.dma_start(
                    out=dbg_d[:, col:col + w], in_=ap,
                    allow_slow_non_contiguous=True),
                  r=tuple(keys), w=("dbg_dram",), tag=f"dbg{col}")

        def ldc(name, tile, cast=False):
            q = "pool" if cast else "sp"
            P.dma(q, lambda e, t=tile, d=DI[name]: e.dma_start(out=t[:], in_=d),
                  w=(name,), tag=f"ld_{name}")

        def ln(x_ap, xkeys, outs):
            xkeys = tuple(xkeys)
            P.op("dve", lambda e: e.bn_stats(lnst[:, 0:6], x_ap[:, 0:512]),
                 r=xkeys, w=("lnst",), tag="ln_s0", swb=True)
            P.op("dve", lambda e: e.bn_stats(lnst[:, 6:12], x_ap[:, 512:1024]),
                 r=xkeys, w=("lnst",), tag="ln_s1", swb=True)
            P.op("dve", lambda e: e.bn_aggr(lnmv[:], lnst[:]),
                 r=("lnst",), w=("lnmv",), tag="ln_ag", swb=True)
            P.op("act", lambda e: e.activation(out=lnlv[:], in_=lnmv[:, 1:2],
                                               func=AF.Ln, bias=eps_col[:],
                                               scale=1.0),
                 r=("lnmv", "epsc"), w=("lnlv",), tag="ln_lnv")
            P.op("act", lambda e: e.activation(out=lnrstd[:], in_=lnlv[:],
                                               func=AF.Exp, scale=-0.5),
                 r=("lnlv",), w=("lnrstd",), tag="ln_rstd")
            P.op("dve", lambda e: e.tensor_scalar(
                    out=lnnmr[:], in0=lnmv[:, 0:1], scalar1=lnrstd[:],
                    scalar2=-1.0, op0=ALU.mult, op1=ALU.mult),
                 r=("lnmv", "lnrstd"), w=("lnnmr",), tag="ln_nmr", early=True)
            for (o_ap, o_key) in outs:
                P.op("act", lambda e, o_ap=o_ap: e.activation(
                        out=o_ap, in_=x_ap, func=AF.Identity,
                        scale=lnrstd[:], bias=lnnmr[:]),
                     r=xkeys + ("lnrstd", "lnnmr"), w=(o_key,), tag="ln_apply")

        def ln_stats(x_ap, xkeys, stb, stbk, sm):
            xkeys = tuple(xkeys)
            P.op("dve", lambda e: e.bn_stats(lnst[:, 0:6], x_ap[:, 0:512]),
                 r=xkeys, w=("lnst",), tag="lns_s0")
            P.op("dve", lambda e: e.bn_stats(lnst[:, 6:12], x_ap[:, 512:1024]),
                 r=xkeys, w=("lnst",), tag="lns_s1")
            P.op("dve", lambda e: e.bn_aggr(lnmv[:], lnst[:]),
                 r=("lnst",), w=("lnmv",), tag="lns_ag")
            P.op("act", lambda e: e.activation(out=lnlv[:], in_=lnmv[:, 1:2],
                                               func=AF.Ln, bias=eps_col[:],
                                               scale=1.0),
                 r=("lnmv", "epsc"), w=("lnlv",), tag="lns_lnv")
            P.op("act", lambda e: e.activation(
                    out=stb[:, 2 * sm:2 * sm + 1], in_=lnlv[:],
                    func=AF.Exp, scale=-0.5),
                 r=("lnlv",), w=(stbk,), tag="lns_rstd")
            P.op("dve", lambda e: e.tensor_scalar(
                    out=stb[:, 2 * sm + 1:2 * sm + 2], in0=lnmv[:, 0:1],
                    scalar1=stb[:, 2 * sm:2 * sm + 1],
                    scalar2=-1.0, op0=ALU.mult, op1=ALU.mult),
                 r=("lnmv", stbk), w=(stbk,), tag="lns_nmr")

        def ln_apply(x_ap, xkeys, o_ap, o_key, stb, stbk, sm):
            P.op("act", lambda e: e.activation(
                    out=o_ap, in_=x_ap, func=AF.Identity,
                    scale=stb[:, 2 * sm:2 * sm + 1],
                    bias=stb[:, 2 * sm + 1:2 * sm + 2]),
                 r=tuple(xkeys) + (stbk,), w=(o_key,), tag="lns_apply")

        def transpose_pack(src_fn, srckeys, bank, bankkey, n=8):
            for j in range(n):
                P.op("pe", lambda e, j=j: e.matmul(
                        bank[:, j * 128:(j + 1) * 128], lhsT=src_fn(j),
                        rhs=id128[:], is_transpose=True,
                        start=(j == 0), stop=(j == n - 1)),
                     r=tuple(srckeys) + ("id128",), w=(bankkey,), tag="tpack")

        def repl_row(row_ap, rowkey, dst, dstkey, width, dve=False):
            nh = (width + 511) // 512
            for h in range(nh):
                wgt = min(512, width - h * 512)
                bank = pf[4 + (h % 2)]
                P.op("pe", lambda e, h=h, wgt=wgt, bank=bank: e.matmul(
                        bank[:, 0:wgt], lhsT=onesrow_f[:],
                        rhs=row_ap[0:1, h * 512:h * 512 + wgt],
                        start=True, stop=True),
                     r=(rowkey, "onesrow_f"), w=(f"PF{4 + h % 2}",), tag="repl_mm")
                eng = "dve" if dve else "act"
                if dve:
                    P.op("dve", lambda e, h=h, wgt=wgt, bank=bank: e.tensor_copy(
                            dst[:, h * 512:h * 512 + wgt], bank[:, 0:wgt]),
                         r=(f"PF{4 + h % 2}",), w=(dstkey,), tag="repl_cp")
                else:
                    P.op("act", lambda e, h=h, wgt=wgt, bank=bank: e.copy(
                            out=dst[:, h * 512:h * 512 + wgt], in_=bank[:, 0:wgt]),
                         r=(f"PF{4 + h % 2}",), w=(dstkey,), tag="repl_cp")

        abctx = ExitStack()
        absb = lambda name, shp, dt: abctx.enter_context(
            nc.sbuf_tensor(name, shp, dt))
        wq_sb = absb("wq_sb", [128, 8192], BF16)
        wk_sb = absb("wk_sb", [128, 8192], BF16)
        wv_sb = absb("wv_sb", [128, 8192], BF16)
        wo_sb = absb("wo_sb", [128, 8192], BF16)
        bqc_sb = absb("bqc_sb", [128, 8], F32)
        bkc_sb = absb("bkc_sb", [128, 8], F32)
        bvb = absb("bvb", [1, 1024], BF16)
        bob = absb("bob", [1, 1024], BF16)
        brlb = absb("brlb", [1, 512], BF16)
        bilb = absb("bilb", [1, 512], BF16)
        wrT = absb("wrT", [128, 2048], BF16)
        wiT = absb("wiT", [128, 2048], BF16)
        wiTn = absb("wiTn", [128, 2048], BF16)

        if "P0" in phases:
            p0ctx = ExitStack()
            p0sb = lambda name, shp, dt: p0ctx.enter_context(
                nc.sbuf_tensor(name, shp, dt))
            for nm, t in (("id128", id128), ("id64d", id64d), ("ut128", ut128),
                          ("ut128b", ut128b),
                          ("ones_bf", ones_bf), ("onesrow_f", onesrow_f),
                          ("onesrow_bf", onesrow_bf),
                          ("iota1_row", iota1_row), ("e127_f", e127_f),
                          ("e127_bf", e127_bf)):
                ldc(nm, t)
            P.dma("sp", lambda e: e.dma_start(out=eps_col[:], in_=DI["eps_col"]),
                  w=("epsc",), tag="ld_eps")
            for nm, t in (("wqT", wq_sb), ("wkT", wk_sb), ("wvT", wv_sb),
                          ("woT", wo_sb)):
                P.dma("pool", lambda e, t=t, d=DI[nm]: e.dma_start(
                        out=t[:].rearrange("p (j o) -> p j o", j=8),
                        in_=d.rearrange("(j p) o -> p j o", p=128)),
                      w=(nm,), tag=f"ld_{nm}")
            ldc("bq_c", bqc_sb)
            ldc("bk_c", bkc_sb)
            ldc("bv_row", bvb, cast=True)
            ldc("bo_row", bob, cast=True)
            ldc("brl_row", brlb, cast=True)
            ldc("bil_row", bilb, cast=True)

            art_sb = p0sb("art_sb", [32, 128], F32)
            brt_sb = p0sb("brt_sb", [16, 64], F32)
            ait_sb = p0sb("ait_sb", [32, 128], F32)
            bit_sb = p0sb("bit_sb", [16, 64], F32)
            ecq_sb = p0sb("ecq_sb", [32, 512], F32)
            ecg_sb = p0sb("ecg_sb", [16, 128], F32)
            for nm, t, pat in (("a_rt", art_sb, "c d b -> d c b"),
                               ("a_it", ait_sb, "c d b -> d c b"),
                               ("b_rt", brt_sb, "c g e -> g c e"),
                               ("b_it", bit_sb, "c g e -> g c e")):
                P.dma("sp", lambda e, t=t, d=DI[nm], pat=pat, nm=nm: e.dma_start(
                        out=t[:].rearrange("p (c x) -> p c x", c=4),
                        in_=d.rearrange(pat)),
                      w=(nm,), tag=f"ld_{nm}")
            ldc("ecq", ecq_sb)
            ldc("ecg", ecg_sb)
            wtmpA = p0sb("wtmpA", [128, 32], F32)
            wtmpB = p0sb("wtmpB", [128, 16], F32)
            wtmpM = p0sb("wtmpM", [128, 512], F32)
            wacc = p0sb("wacc", [128, 512], F32)

            def wbuild(a_sb, b_sb, akey, bkey, dstT, dstTn, dkey, dnkey):
                for q in range(4):
                    for cc in range(4):
                        bA = pf[2 * (cc % 2)]
                        bAk = f"PF{2 * (cc % 2)}"
                        bB = pf[1 + 2 * (cc % 2)]
                        bBk = f"PF{1 + 2 * (cc % 2)}"
                        P.op("pe", lambda e, q=q, cc=cc, bA=bA: e.matmul(
                                bA[:, 0:32], lhsT=ecq_sb[:, q * 128:(q + 1) * 128],
                                rhs=a_sb[:, cc * 32:(cc + 1) * 32],
                                start=True, stop=True),
                             r=(akey, "ecq"), w=(bAk), tag="wb_A")
                        P.op("pe", lambda e, cc=cc, bB=bB: e.matmul(
                                bB[:, 0:16], lhsT=ecg_sb[:],
                                rhs=b_sb[:, cc * 16:(cc + 1) * 16],
                                start=True, stop=True),
                             r=(bkey, "ecg"), w=(bBk), tag="wb_B")
                        P.op("act", lambda e, bA=bA: e.copy(
                                out=wtmpA[:], in_=bA[:, 0:32]),
                             r=(bAk), w=("wb_A"), tag="wb_Aev")
                        P.op("dve", lambda e, bB=bB: e.tensor_copy(
                                wtmpB[:], bB[:, 0:16]),
                             r=(bBk), w=("wb_B"), tag="wb_Bev")
                        avb = wtmpA[:].rearrange("p (b one) -> p b one",
                                                 one=1).broadcast_to([128, 32, 16])
                        bvb = wtmpB[:].rearrange("p (one e) -> p one e",
                                                 one=1).broadcast_to([128, 32, 16])
                        if cc == 0:
                            ov = wacc[:].rearrange("p (b e) -> p b e", e=16)
                            P.op("dve", lambda e, avb=avb, bvb=bvb, ov=ov:
                                 e.tensor_tensor(out=ov, in0=avb, in1=bvb,
                                                 op=ALU.mult),
                                 r=("wb_A", "wb_B"), w=("wb_acc"),
                                 tag="wb_mul")
                        else:
                            mv = wtmpM[:].rearrange("p (b e) -> p b e", e=16)
                            P.op("dve", lambda e, avb=avb, bvb=bvb, mv=mv:
                                 e.tensor_tensor(out=mv, in0=avb, in1=bvb,
                                                 op=ALU.mult),
                                 r=("wb_A", "wb_B"), w=("wb_M"),
                                 tag="wb_mul")
                            P.op("pool", lambda e: e.tensor_tensor(
                                    out=wacc[:], in0=wacc[:], in1=wtmpM[:],
                                    op=ALU.add),
                                 r=("wb_M"), w=("wb_acc"), tag="wb_add")
                    P.op("dve", lambda e, q=q: e.tensor_copy(
                            dstT[:, q * 512:(q + 1) * 512], wacc[:]),
                         r=("wb_acc"), w=(dkey), tag="wb_cast")
                    if dstTn is not None:
                        P.op("act", lambda e, q=q: e.mul(
                                out=dstTn[:, q * 512:(q + 1) * 512], in_=wacc[:],
                                mul=-1.0),
                             r=("wb_acc"), w=(dnkey), tag="wb_castn")

            wbuild(art_sb, brt_sb, "a_rt", "b_rt", wrT, None, "wrT", None)
            wbuild(ait_sb, bit_sb, "a_it", "b_it", wiT, wiTn, "wiT", "wiTn")
            dbgdump(0, wrT[:], ("wrT",), 2048, always=True)
            dbgdump(2048, wiTn[:], ("wiTn",), 2048, always=True)
            P.flush("P0")
            p0ctx.close()
            P.barrier()

        if "AB" in phases:
            x_sb = [absb(f"xA{i}", [128, 4096], F32) for i in range(2)]
            nbf2 = [absb(f"nbf{i}", [128, 1024], BF16) for i in range(2)]
            stAB = absb("stAB", [128, 8], F32)
            n1T = absb("n1T", [128, 4096], BF16)
            q_sb = absb("q_sb", [128, 4096], BF16)
            k_sb = absb("k_sb", [128, 4096], BF16)
            v_sb = absb("v_sb", [128, 4096], BF16)
            attn_bf = [absb(f"attnbf{i}", [128, 1024], BF16) for i in range(4)]
            attnT = absb("attnT", [128, 1024], BF16)
            y_sb = absb("yA", [128, 1024], F32)
            n3T = absb("n3T", [128, 1024], BF16)
            f_sb = absb("fA", [128, 1024], F32)
            S_sb = absb("S_sb", [128, 520], F32)
            S_bf = absb("S_bf", [128, 520], BF16)
            rden = absb("rden", [128, 1], F32)
            ktp_sb = absb("ktp_sb", [128, 128], BF16)

            P.op("dve", lambda e: e.memset(S_sb[:], 0.0), w=("S_sb",), tag="S0",
                 swb=True)
            P.op("dve", lambda e: e.memset(S_bf[:], 0.0), w=("S_bf",), tag="Sbf0",
                 swb=True)

            def ab_attn_stage(c, m):
                par = c % 2
                apar = m % 4
                abf = attn_bf[apar]
                for jp in range(8):
                    for pi in range(2):
                        b0 = pi * 64
                        h = 2 * jp + pi
                        qh = q_sb[b0:b0 + 64,
                                  jp * 512 + m * 128:jp * 512 + m * 128 + 128]
                        kh = k_sb[b0:b0 + 64,
                                  jp * 512 + m * 128:jp * 512 + m * 128 + 128]
                        vh = v_sb[:, m * 1024 + h * 64:m * 1024 + h * 64 + 64]
                        sc_bank = pf[pi]
                        sck = f"PF{pi}"
                        nd_bank = pf[2 + pi]
                        ndk = f"PF{2 + pi}"
                        P.op("pe", lambda e, kh=kh, qh=qh, sc_bank=sc_bank:
                             e.matmul(sc_bank[:, 0:128], lhsT=kh, rhs=qh,
                                      start=True, stop=True),
                             r=("q_sb", "k_sb"), w=(sck,), tag="scT")
                        P.op("dve", lambda e, sc_bank=sc_bank, pi=pi:
                             e.tensor_tensor(out=mtb[pi][:],
                                             in0=sc_bank[:, 0:128],
                                             in1=ut128[:], op=ALU.mult),
                             r=(sck, "ut128"), w=(f"mtb{pi}",), tag="mask")
                        P.op("pe", lambda e, pi=pi, vh=vh, nd_bank=nd_bank:
                             e.matmul(nd_bank[:, 0:64], lhsT=mtb[pi][:],
                                      rhs=vh, start=True, stop=False),
                             r=(f"mtb{pi}", "v_sb"), w=(ndk,), tag="num1")
                        P.op("pe", lambda e, pi=pi, nd_bank=nd_bank:
                             e.matmul(nd_bank[:, 64:65], lhsT=mtb[pi][:],
                                      rhs=ones_bf[:], start=False, stop=False),
                             r=(f"mtb{pi}", "ones_bf"), w=(ndk,), tag="den1")
                        P.op("pe", lambda e, qh=qh, jp=jp, b0=b0,
                             nd_bank=nd_bank: e.matmul(
                                nd_bank[:, 0:64], lhsT=qh,
                                rhs=S_bf[b0:b0 + 64, jp * 65:jp * 65 + 64],
                                start=False, stop=False),
                             r=("q_sb", "S_bf"), w=(ndk,), tag="num2")
                        P.op("pe", lambda e, qh=qh, jp=jp, b0=b0,
                             nd_bank=nd_bank: e.matmul(
                                nd_bank[:, 64:65], lhsT=qh,
                                rhs=S_bf[b0:b0 + 64, jp * 65 + 64:jp * 65 + 65],
                                start=False, stop=True),
                             r=("q_sb", "S_bf"), w=(ndk,), tag="den2")
                        P.op("pe", lambda e, kh=kh, b0=b0, pi=pi: e.matmul(
                                pb[1][:, pi * 64:pi * 64 + 64], lhsT=kh,
                                rhs=id64d[b0:b0 + 64, :], is_transpose=True,
                                start=(pi == 0), stop=(pi == 1)),
                             r=("k_sb", "id64d"), w=("PB1",), tag="kT")
                        P.op("dve", lambda e, nd_bank=nd_bank: e.reciprocal(
                                out=rden[:], in_=nd_bank[:, 64:65]),
                             r=(ndk,), w=("rden",), tag="rden", swb=True)
                        P.op("dve", lambda e, abf=abf, h=h, nd_bank=nd_bank,
                             apar=apar: e.tensor_scalar(
                                out=abf[:, h * 64:h * 64 + 64],
                                in0=nd_bank[:, 0:64], scalar1=rden[:],
                                scalar2=None, op0=ALU.mult),
                             r=(ndk, "rden"), w=(f"attnbf{apar}",), tag="attno",
                             early=True)
                    P.op("act", lambda e: e.copy(out=ktp_sb[:],
                                                 in_=pb[1][:, 0:128]),
                         r=("PB1",), w=("ktp",), tag="ktp_ev")
                    for pi in range(2):
                        b0 = pi * 64
                        h = 2 * jp + pi
                        vh = v_sb[:, m * 1024 + h * 64:m * 1024 + h * 64 + 64]
                        P.op("pe", lambda e, pi=pi, b0=b0, vh=vh: e.matmul(
                                pf[4][b0:b0 + 64, 0:64],
                                lhsT=ktp_sb[:, pi * 64:pi * 64 + 64], rhs=vh,
                                start=True, stop=False,
                                tile_position=(0, b0)),
                             r=("ktp", "v_sb"), w=("PF4",), tag="dS")
                        P.op("pe", lambda e, pi=pi, b0=b0: e.matmul(
                                pf[4][b0:b0 + 64, 64:65],
                                lhsT=ktp_sb[:, pi * 64:pi * 64 + 64],
                                rhs=ones_bf[:], start=False, stop=(pi == 1),
                                tile_position=(0, b0)),
                             r=("ktp", "ones_bf"), w=("PF4",), tag="dz")
                    P.op("dve", lambda e, jp=jp: e.tensor_tensor(
                            out=S_bf[:, jp * 65:(jp + 1) * 65],
                            in0=S_sb[:, jp * 65:(jp + 1) * 65],
                            in1=pf[4][:, 0:65], op=ALU.add),
                         r=("S_sb", "PF4"), w=("S_bf",), tag="Snew")
                    P.op("dve", lambda e, jp=jp: e.tensor_tensor(
                            out=S_sb[:, jp * 65:(jp + 1) * 65],
                            in0=S_sb[:, jp * 65:(jp + 1) * 65],
                            in1=pf[4][:, 0:65], op=ALU.add),
                         r=("S_sb", "PF4"), w=("S_sb",), tag="Supd")

            def ab_post_stage(c, m):
                par = c % 2
                xk = f"xA{par}"
                xt = x_sb[par]
                apar = m % 4
                abf = attn_bf[apar]
                xv = xt[:, m * 1024:(m + 1) * 1024]
                transpose_pack(lambda j, abf=abf: abf[:, j * 128:(j + 1) * 128],
                               (f"attnbf{apar}",), pb[0], "PB0")
                P.op("act", lambda e: e.copy(out=attnT[:], in_=pb[0][:]),
                     r=("PB0",), w=("attnT",), tag="attnT_ev")
                for hf in range(2):
                    for j in range(8):
                        P.op("pe", lambda e, j=j, hf=hf:
                             e.matmul(pf[5][:],
                                      lhsT=attnT[:, j * 128:(j + 1) * 128],
                                      rhs=wo_sb[:, j * 1024 + hf * 512:
                                                j * 1024 + hf * 512 + 512],
                                      start=(j == 0), stop=False),
                             r=("attnT", "woT"), w=("PF5",), tag="omm")
                    P.op("pe", lambda e, hf=hf: e.matmul(
                            pf[5][:], lhsT=onesrow_bf[:],
                            rhs=bob[0:1, hf * 512:hf * 512 + 512],
                            start=False, stop=True),
                         r=("bo_row", "onesrow_bf"), w=("PF5",), tag="obmm")
                    P.op("act", lambda e, hf=hf: e.copy(
                            out=y_sb[:, hf * 512:(hf + 1) * 512], in_=pf[5][:]),
                         r=("PF5",), w=("yA",), tag="oev")
                ln(y_sb[:], ("yA",), [(lnscr2[:], "lnscr2")])
                P.op("dve", lambda e, xv=xv: e.tensor_tensor(
                        out=xv, in0=xv, in1=lnscr2[:], op=ALU.add),
                     r=("lnscr2",), w=(xk,), tag="xadd1")
                nb = nbf2[m % 2]
                nbk = f"nbf{m % 2}"
                ln(xv, (xk,), [(nb[:], nbk)])
                transpose_pack(lambda j, nb=nb: nb[:, j * 128:(j + 1) * 128],
                               (nbk,), pb[0], "PB0")
                P.op("act", lambda e: e.copy(out=n3T[:], in_=pb[0][:]),
                     r=("PB0",), w=("n3T",), tag="n3T_ev")
                for half, w_lo, wk_lo, w_hi, wk_hi, bbr, brk in (
                        (0, wrT, "wrT", wiTn, "wiTn", brlb, "brl_row"),
                        (1, wiT, "wiT", wrT, "wrT", bilb, "bil_row")):
                    for j in range(8):
                        wsb = w_lo if j < 4 else w_hi
                        wky = wk_lo if j < 4 else wk_hi
                        qq = j if j < 4 else j - 4
                        P.op("pe", lambda e, j=j, qq=qq, wsb=wsb:
                             e.matmul(pf[5][:],
                                      lhsT=n3T[:, j * 128:(j + 1) * 128],
                                      rhs=wsb[:, qq * 512:(qq + 1) * 512],
                                      start=(j == 0), stop=False),
                             r=("n3T", wky), w=("PF5",), tag="bmm")
                    P.op("pe", lambda e, bbr=bbr: e.matmul(
                            pf[5][:], lhsT=onesrow_bf[:], rhs=bbr[0:1, :],
                            start=False, stop=True),
                         r=(brk, "onesrow_bf"), w=("PF5",), tag="bbmm")
                    P.op("act", lambda e, half=half: e.copy(
                            out=f_sb[:, half * 512:(half + 1) * 512],
                            in_=pf[5][:]),
                         r=("PF5",), w=("fA",), tag="bev")
                ln(f_sb[:], ("fA",), [(lnscr2[:], "lnscr2")])
                P.op("dve", lambda e, xv=xv: e.tensor_tensor(
                        out=xv, in0=xv, in1=lnscr2[:], op=ALU.add),
                     r=("lnscr2",), w=(xk,), tag="xadd2")

            def ab_store(c):
                par = c % 2
                xk = f"xA{par}"
                xt = x_sb[par]
                dst = xst_d[c * CH:(c + 1) * CH, :].rearrange(
                    "(m p) d -> p m d", p=128)
                P.dma("act", lambda e, xt=xt, dst=dst: e.dma_start(
                        out=dst, in_=xt[:].rearrange("p (m d) -> p m d", m=MT)),
                      r=(xk,), w=(f"xst{c}",), tag=f"stxA{c}")

            for c in range(NCH):
                par = c % 2
                xk = f"xA{par}"
                xt = x_sb[par]
                src = DI["x0"][c * CH:(c + 1) * CH, :].rearrange(
                    "(m p) d -> p m d", p=128)
                P.dma("sp", lambda e, xt=xt, src=src: e.dma_start(
                        out=xt[:].rearrange("p (m d) -> p m d", m=MT), in_=src),
                      r=("x0_dram",), w=(xk,), tag=f"ldxA{c}")
                for m in range(MT):
                    xv = xt[:, m * 1024:(m + 1) * 1024]
                    ln_stats(xv, (xk,), stAB, "stAB", m)
                for m in range(MT):
                    xv = xt[:, m * 1024:(m + 1) * 1024]
                    nb = nbf2[m % 2]
                    nbk = f"nbf{m % 2}"
                    ln_apply(xv, (xk,), nb[:], nbk, stAB, "stAB", m)
                    tb = pb[m % 2]
                    transpose_pack(lambda j, nb=nb: nb[:, j * 128:(j + 1) * 128],
                                   (nbk,), tb, f"PB{m % 2}")
                    dv = n1T[:].rearrange("p (j t) -> p j t", j=8)[
                        :, :, m * 128:(m + 1) * 128]
                    sv = tb[:].rearrange("p (j t) -> p j t", j=8)
                    P.op("act", lambda e, dv=dv, sv=sv: e.copy(out=dv, in_=sv),
                         r=(f"PB{m % 2}",), w=("n1T",), tag="n1T_ev")
                for nm, wsb, wkey, bsb, bkey, dst, dkey in (
                        ("q", wq_sb, "wqT", bqc_sb, "bq_c", q_sb, "q_sb"),
                        ("k", wk_sb, "wkT", bkc_sb, "bk_c", k_sb, "k_sb")):
                    for jo in range(8):
                        bank = pf[jo % 2]
                        bk_key = f"PF{jo % 2}"
                        for ji in range(8):
                            P.op("pe", lambda e, jo=jo, ji=ji, wsb=wsb, bank=bank:
                                 e.matmul(bank[:],
                                          lhsT=wsb[:, ji * 1024 + jo * 128:
                                                   ji * 1024 + jo * 128 + 128],
                                          rhs=n1T[:, ji * 512:(ji + 1) * 512],
                                          start=(ji == 0), stop=(ji == 7)),
                                 r=(wkey, "n1T"), w=(bk_key,), tag=f"{nm}mm")
                        P.op("act", lambda e, jo=jo, bsb=bsb, dst=dst, bank=bank:
                             e.activation(out=dst[:, jo * 512:(jo + 1) * 512],
                                          in_=bank[:], func=AF.Exp,
                                          bias=bsb[:, jo:jo + 1], scale=1.0),
                             r=(bk_key, bkey), w=(dkey,), tag=f"{nm}exp")
                for m in range(MT):
                    for hf in range(2):
                        bank = pf[2 + hf]
                        bk_key = f"PF{2 + hf}"
                        for ji in range(8):
                            P.op("pe", lambda e, m=m, hf=hf, ji=ji, bank=bank:
                                 e.matmul(bank[:],
                                          lhsT=n1T[:, ji * 512 + m * 128:
                                                   ji * 512 + m * 128 + 128],
                                          rhs=wv_sb[:, ji * 1024 + hf * 512:
                                                    ji * 1024 + hf * 512 + 512],
                                          start=(ji == 0), stop=False),
                                 r=("wvT", "n1T"), w=(bk_key,), tag="vmm")
                        P.op("pe", lambda e, hf=hf, bank=bank: e.matmul(
                                bank[:], lhsT=onesrow_bf[:],
                                rhs=bvb[0:1, hf * 512:hf * 512 + 512],
                                start=False, stop=True),
                             r=("bv_row", "onesrow_bf"), w=(bk_key,), tag="vbmm")
                        P.op("act", lambda e, m=m, hf=hf, bank=bank: e.copy(
                                out=v_sb[:, m * 1024 + hf * 512:
                                         m * 1024 + hf * 512 + 512],
                                in_=bank[:]),
                             r=(bk_key,), w=("v_sb",), tag="vev")
                ab_attn_stage(c, 0)
                if c > 0:
                    ab_post_stage(c - 1, 2)
                ab_attn_stage(c, 1)
                if c > 0:
                    ab_post_stage(c - 1, 3)
                    ab_store(c - 1)
                ab_attn_stage(c, 2)
                ab_post_stage(c, 0)
                ab_attn_stage(c, 3)
                ab_post_stage(c, 1)
            ab_post_stage(NCH - 1, 2)
            ab_post_stage(NCH - 1, 3)
            ab_store(NCH - 1)
            P.flush("AB")
            abctx.close()
            P.barrier()

        if "C" in phases:
            cctx = ExitStack()
            csb = lambda name, shp, dt: cctx.enter_context(
                nc.sbuf_tensor(name, shp, dt))
            bre_sb = csb("bre_sb", [128, 8192], BF16)
            bim_sb = csb("bim_sb", [128, 8192], BF16)
            cre_sb = csb("cre_sb", [128, 8192], BF16)
            cim_sb = csb("cim_sb", [128, 8192], BF16)
            lp_re = csb("lp_re", [128, 1024], F32)
            lp_im = csb("lp_im", [128, 1024], F32)
            li_re = csb("li_re", [128, 1024], F32)
            li_imN = csb("li_imN", [128, 1024], F32)
            drep = csb("drep", [128, 1024], F32)

            pctx = ExitStack()
            psb = lambda name, shp, dt: pctx.enter_context(
                nc.sbuf_tensor(name, shp, dt))
            nulog_sb = psb("nulog_sb", [1, 1024], F32)
            thlog_sb = psb("thlog_sb", [1, 1024], F32)
            dskip_sb = psb("dskip_sb", [1, 1024], F32)
            nu_row = psb("nu_row", [1, 1024], F32)
            th_row = psb("th_row", [1, 1024], F32)
            g_row = psb("g_row", [1, 1024], F32)
            lpm_sb = psb("lpm_sb", [128, 1024], F32)
            lim_sb = psb("lim_sb", [128, 1024], F32)
            sin_sb = psb("sin_sb", [128, 1024], F32)
            cos_sb = psb("cos_sb", [128, 1024], F32)
            cscrA = psb("cscrA", [128, 512], F32)
            cscrB = psb("cscrB", [128, 512], F32)
            grep = psb("grep", [128, 1024], F32)
            bstage = psb("bstage", [128, 2048], F32)
            uscr = psb("uscr_prep", [128, 1024], F32)

            ldc("nulog_row", nulog_sb)
            ldc("thetalog_row", thlog_sb)
            ldc("dskip_row", dskip_sb)
            P.op("act", lambda e: e.activation(out=nu_row[:], in_=nulog_sb[:],
                                               func=AF.Exp),
                 r=("nulog_row",), w=("nu_row",), tag="nu")
            P.op("act", lambda e: e.activation(out=th_row[:], in_=thlog_sb[:],
                                               func=AF.Exp),
                 r=("thetalog_row",), w=("th_row",), tag="th")
            for h2 in range(2):
                c0, c1 = h2 * 512, (h2 + 1) * 512
                P.op("pe", lambda e, c0=c0, c1=c1: e.matmul(
                        pf[0][:], lhsT=iota1_row[:], rhs=nu_row[0:1, c0:c1],
                        start=True, stop=True),
                     r=("nu_row", "iota1_row"), w=("PF0",), tag="tpn")
                P.op("act", lambda e, c0=c0, c1=c1: e.activation(
                        out=lpm_sb[:, c0:c1], in_=pf[0][:], func=AF.Exp,
                        scale=-1.0),
                     r=("PF0",), w=("lpm",), tag="lpmag")
                P.op("act", lambda e, c0=c0, c1=c1: e.activation(
                        out=lim_sb[:, c0:c1], in_=pf[0][:], func=AF.Exp,
                        scale=1.0),
                     r=("PF0",), w=("lim",), tag="limag")
                P.op("pe", lambda e, c0=c0, c1=c1: e.matmul(
                        pf[1][:], lhsT=iota1_row[:], rhs=th_row[0:1, c0:c1],
                        start=True, stop=True),
                     r=("th_row", "iota1_row"), w=("PF1",), tag="tpt")
                P.op("dve", lambda e: e.tensor_scalar(
                        out=cscrA[:], in0=pf[1][:], scalar1=PIF, scalar2=None,
                        op0=ALU.is_gt),
                     r=("PF1",), w=("cscrA",), tag="phmask")
                P.op("dve", lambda e: e.scalar_tensor_tensor(
                        out=cscrB[:], in0=cscrA[:], scalar=-2.0 * PIF,
                        in1=pf[1][:], op0=ALU.mult, op1=ALU.add),
                     r=("cscrA", "PF1"), w=("cscrB",), tag="phred")
                P.op("act", lambda e, c0=c0, c1=c1: e.activation(
                        out=sin_sb[:, c0:c1], in_=cscrB[:], func=AF.Sin),
                     r=("cscrB",), w=("sin",), tag="sin")
                P.op("act", lambda e: e.activation(
                        out=cscrA[:], in_=cscrB[:], func=AF.Sin, scale=0.5),
                     r=("cscrB",), w=("cscrA",), tag="sinh")
                P.op("act", lambda e: e.activation(
                        out=cscrA[:], in_=cscrA[:], func=AF.Square),
                     r=("cscrA",), w=("cscrA",), tag="sq")
                P.op("dve", lambda e, c0=c0, c1=c1: e.tensor_scalar(
                        out=cos_sb[:, c0:c1], in0=cscrA[:], scalar1=-2.0,
                        scalar2=1.0, op0=ALU.mult, op1=ALU.add),
                     r=("cscrA",), w=("cos",), tag="cos")
            P.op("dve", lambda e: e.tensor_tensor(out=lp_re[:], in0=lpm_sb[:],
                                                  in1=cos_sb[:], op=ALU.mult),
                 r=("lpm", "cos"), w=("lp_re",), tag="lpre")
            P.op("dve", lambda e: e.tensor_tensor(out=lp_im[:], in0=lpm_sb[:],
                                                  in1=sin_sb[:], op=ALU.mult),
                 r=("lpm", "sin"), w=("lp_im",), tag="lpim")
            P.op("dve", lambda e: e.tensor_tensor(out=li_re[:], in0=lim_sb[:],
                                                  in1=cos_sb[:], op=ALU.mult),
                 r=("lim", "cos"), w=("li_re",), tag="lire")
            P.op("dve", lambda e: e.tensor_tensor(out=li_imN[:], in0=lim_sb[:],
                                                  in1=sin_sb[:], op=ALU.mult),
                 r=("lim", "sin"), w=("li_imN",), tag="liimN")
            dbgdump(10240, lp_re[:], ("lp_re",), 1024, always=True)
            dbgdump(11264, li_re[:], ("li_re",), 1024, always=True)
            dbgdump(12288, lp_im[:], ("lp_im",), 1024, always=True)
            dbgdump(13312, li_imN[:], ("li_imN",), 1024, always=True)
            dbgdump(14336, grep[:], ("grep",), 1024, always=True)
            dbgdump(15360, drep[:], ("drep",), 1024, always=True)
            dbgdump(4096, bre_sb[:, 0:2048], ("bre",), 2048, always=True)
            dbgdump(6144, cre_sb[:, 0:2048], ("cre",), 2048, always=True)
            if DBG2:
                P.dma("pool", lambda e: e.dma_start(out=dbg_d[0:1, 9216:10240],
                                                    in_=g_row[:]),
                      r=("g_row",), w=("dbg_dram",), tag="dbg_grow")
                P.dma("pool", lambda e: e.dma_start(out=dbg_d[1:2, 9216:10240],
                                                    in_=dskip_sb[:]),
                      r=("dskip_row",), w=("dbg_dram",), tag="dbg_drow")
            P.op("act", lambda e: e.activation(out=g_row[:], in_=nu_row[:],
                                               func=AF.Exp, scale=-2.0),
                 r=("nu_row",), w=("g_row",), tag="e2")
            P.op("dve", lambda e: e.tensor_scalar(out=g_row[:], in0=g_row[:],
                                                  scalar1=-1.0, scalar2=1.0 + EPS,
                                                  op0=ALU.mult, op1=ALU.add),
                 r=("g_row",), w=("g_row",), tag="g1m")
            P.op("act", lambda e: e.activation(out=g_row[:], in_=g_row[:],
                                               func=AF.Ln),
                 r=("g_row",), w=("g_row",), tag="glng")
            P.op("act", lambda e: e.activation(out=g_row[:], in_=g_row[:],
                                               func=AF.Exp, scale=0.5),
                 r=("g_row",), w=("g_row",), tag="gexp")
            repl_row(g_row[:], "g_row", grep, "grep", 1024)
            repl_row(dskip_sb[:], "dskip_row", drep, "drep", 1024)
            for wnm, dst, dkey in (("breT", bre_sb, "bre"), ("bimT", bim_sb, "bim")):
                for jq in range(4):
                    P.dma("sp", lambda e, d=DI[wnm], jq=jq: e.dma_start(
                            out=bstage[:].rearrange("p (j o) -> p j o", j=2),
                            in_=d[jq * 256:(jq + 1) * 256, :].rearrange(
                                "(j p) o -> p j o", p=128)),
                          w=("bstage",), tag=f"ld_{wnm}{jq}")
                    for j2 in range(2):
                        j = jq * 2 + j2
                        P.op("dve", lambda e, j=j, j2=j2, dst=dst: e.tensor_tensor(
                                out=dst[:, j * 1024:(j + 1) * 1024],
                                in0=bstage[:, j2 * 1024:(j2 + 1) * 1024],
                                in1=grep[:], op=ALU.mult),
                             r=("bstage", "grep"), w=(dkey,), tag=f"scale_{wnm}")
            for wnm, dst, dkey in (("creT", cre_sb, "cre"), ("cimT", cim_sb, "cim")):
                P.dma("pool", lambda e, d=DI[wnm], dst=dst: e.dma_start(
                        out=dst[:].rearrange("p (j o) -> p j o", j=8),
                        in_=d.rearrange("(j p) o -> p j o", p=128)),
                      w=(dkey,), tag=f"ld_{wnm}")
            P.flush("Cprep")
            pctx.close()
            P.barrier()

            xC = [csb(f"xC{i}", [128, 4096], F32) for i in range(2)]
            nbfC2 = [csb(f"nbfC{i}", [128, 1024], BF16) for i in range(2)]
            n1TC = [csb(f"n1TC{i}", [128, 4096], BF16) for i in range(2)]
            lnstC = [csb(f"lnstC{i}", [128, 8], F32) for i in range(2)]
            uA_bf = csb("uA_bf", [128, 2048], BF16)
            uB_bf = csb("uB_bf", [128, 2048], BF16)
            skscr = csb("skscr", [128, 1024], F32)
            sksc2 = csb("sksc2", [128, 1024], F32)
            ct1 = csb("ct1", [128, 512], F32)
            ct2 = csb("ct2", [128, 512], F32)
            hs_bf = [csb(f"hs_bf{i}", [128, 2048], BF16) for i in range(2)]
            hsT = csb("hsT", [128, 2048], BF16)
            carry = csb("carry", [1, 2048], BF16)
            yC = csb("yC", [128, 1024], F32)
            P.op("dve", lambda e: e.memset(carry[:], 0.0), w=("carry",),
                 tag="carry0", swb=True)
            P.op("dve", lambda e: e.memset(hs_bf[0][:], 0.0), w=("hs_bf0",),
                 tag="hs0a", swb=True)
            P.op("dve", lambda e: e.memset(hs_bf[1][:], 0.0), w=("hs_bf1",),
                 tag="hs0b", swb=True)

            def c_scan_stage(c, m):
                par = c % 2
                ntk = f"n1TC{par}"
                mpar = m % 2
                hk = f"hs_bf{mpar}"
                hsb = hs_bf[mpar]
                for ri, wsb, wky in ((0, bre_sb, "bre"), (1, bim_sb, "bim")):
                    for h2 in range(2):
                        bank = pf[2 * ri + h2]
                        bkk = f"PF{2 * ri + h2}"
                        for ji in range(8):
                            P.op("pe", lambda e, ji=ji, h2=h2, m=m, wsb=wsb,
                                 bank=bank, par=par: e.matmul(
                                    bank[:],
                                    lhsT=n1TC[par][:, m * 1024 + ji * 128:
                                                   m * 1024 + ji * 128 + 128],
                                    rhs=wsb[:, ji * 1024 + h2 * 512:
                                            ji * 1024 + h2 * 512 + 512],
                                    start=(ji == 0), stop=(ji == 7)),
                                 r=(ntk, wky), w=(bkk,), tag="bumm")
                for h2 in range(2):
                    c0, c1 = h2 * 512, (h2 + 1) * 512
                    bre_ps, bim_ps = pf[h2], pf[2 + h2]
                    kre, kim = f"PF{h2}", f"PF{2 + h2}"
                    P.op("dve", lambda e, c0=c0, c1=c1, bre_ps=bre_ps:
                         e.tensor_tensor(out=uA_bf[:, c0:c1],
                                         in0=li_re[:, c0:c1],
                                         in1=bre_ps[:], op=ALU.mult),
                         r=("li_re", kre), w=("uA_bf",), tag="u1")
                    P.op("dve", lambda e, c0=c0, c1=c1, bim_ps=bim_ps:
                         e.tensor_tensor(out=uB_bf[:, c0:c1],
                                         in0=li_imN[:, c0:c1],
                                         in1=bim_ps[:], op=ALU.mult),
                         r=("li_imN", kim), w=("uB_bf",), tag="u2")
                    P.op("dve", lambda e, c0=c0, c1=c1, bim_ps=bim_ps:
                         e.scalar_tensor_tensor(
                            out=uA_bf[:, 1024 + c0:1024 + c1], in0=bim_ps[:],
                            scalar=-1.0,
                            in1=li_re[:, c0:c1], op0=ALU.mult, op1=ALU.mult),
                         r=("li_re", kim), w=("uA_bf",), tag="u4")
                    P.op("dve", lambda e, c0=c0, c1=c1, bre_ps=bre_ps:
                         e.tensor_tensor(out=uB_bf[:, 1024 + c0:1024 + c1],
                                         in0=li_imN[:, c0:c1],
                                         in1=bre_ps[:], op=ALU.mult),
                         r=("li_imN", kre), w=("uB_bf",), tag="u5")
                for seg in range(4):
                    bank = pf[seg]
                    bkk = f"PF{seg}"
                    P.op("pe", lambda e, seg=seg, bank=bank: e.matmul(
                            bank[:], lhsT=ut128b[:],
                            rhs=uA_bf[:, seg * 512:(seg + 1) * 512],
                            start=True, stop=False),
                         r=("uA_bf", "ut128b"), w=(bkk,), tag="cumsA")
                    P.op("pe", lambda e, seg=seg, bank=bank: e.matmul(
                            bank[:], lhsT=ut128b[:],
                            rhs=uB_bf[:, seg * 512:(seg + 1) * 512],
                            start=False, stop=False),
                         r=("uB_bf", "ut128b"), w=(bkk,), tag="cumsB")
                    P.op("pe", lambda e, seg=seg, bank=bank: e.matmul(
                            bank[:], lhsT=onesrow_bf[:],
                            rhs=carry[0:1, seg * 512:(seg + 1) * 512],
                            start=False, stop=True),
                         r=("carry", "onesrow_bf"), w=(bkk,), tag="carrymm")
                for h2 in range(2):
                    c0, c1 = h2 * 512, (h2 + 1) * 512
                    sre, sim = pf[h2], pf[2 + h2]
                    kre, kim = f"PF{h2}", f"PF{2 + h2}"
                    P.op("dve", lambda e, c0=c0, c1=c1, sre=sre: e.tensor_tensor(
                            out=ct1[:], in0=lp_re[:, c0:c1], in1=sre[:],
                            op=ALU.mult),
                         r=("lp_re", kre), w=("ct1",), tag="hs1")
                    P.op("dve", lambda e, c0=c0, c1=c1, sim=sim: e.tensor_tensor(
                            out=ct2[:], in0=lp_im[:, c0:c1], in1=sim[:],
                            op=ALU.mult),
                         r=("lp_im", kim), w=("ct2",), tag="hs2")
                    P.op("pool", lambda e, c0=c0, c1=c1, hsb=hsb: e.tensor_tensor(
                            out=hsb[:, c0:c1], in0=ct1[:], in1=ct2[:],
                            op=ALU.add),
                         r=("ct1", "ct2"), w=(hk,), tag="hs3")
                    P.op("dve", lambda e, c0=c0, c1=c1, sim=sim: e.tensor_tensor(
                            out=ct1[:], in0=lp_re[:, c0:c1], in1=sim[:],
                            op=ALU.mult),
                         r=("lp_re", kim), w=("ct1",), tag="hs4")
                    P.op("dve", lambda e, c0=c0, c1=c1, sre=sre: e.scalar_tensor_tensor(
                            out=ct2[:], in0=sre[:], scalar=-1.0,
                            in1=lp_im[:, c0:c1], op0=ALU.mult, op1=ALU.mult),
                         r=("lp_im", kre), w=("ct2",), tag="hs5")
                    P.op("pool", lambda e, c0=c0, c1=c1, hsb=hsb: e.tensor_tensor(
                            out=hsb[:, 1024 + c0:1024 + c1], in0=ct1[:],
                            in1=ct2[:], op=ALU.add),
                         r=("ct1", "ct2"), w=(hk,), tag="hs6")
                    for seg in (h2, 2 + h2):
                        bank = pf[seg]
                        bkk = f"PF{seg}"
                        P.op("pe", lambda e, seg=seg, bank=bank, hsb=hsb: e.matmul(
                                bank[0:1, :], lhsT=e127_bf[:],
                                rhs=hsb[:, seg * 512:(seg + 1) * 512],
                                start=True, stop=True),
                             r=(hk, "e127_bf"), w=(bkk,), tag="cext")
                        P.op("act", lambda e, seg=seg, bank=bank: e.copy(
                                out=carry[0:1, seg * 512:(seg + 1) * 512],
                                in_=bank[0:1, :]),
                             r=(bkk,), w=("carry",), tag="cev")


            def c_post_stage(c, m):
                par = c % 2
                xk = f"xC{par}"
                xt = xC[par]
                mpar = m % 2
                hk = f"hs_bf{mpar}"
                hsb = hs_bf[mpar]
                xv = xt[:, m * 1024:(m + 1) * 1024]
                transpose_pack(lambda j, hsb=hsb: hsb[:, j * 128:(j + 1) * 128],
                               (hk,), pb[0], "PB0")
                P.op("act", lambda e: e.copy(out=hsT[:, 0:1024], in_=pb[0][:]),
                     r=("PB0",), w=("hsT",), tag="hsTev1")
                transpose_pack(lambda j, hsb=hsb: hsb[:, 1024 + j * 128:
                                                      1024 + (j + 1) * 128],
                               (hk,), pb[1], "PB1")
                P.op("act", lambda e: e.copy(out=hsT[:, 1024:2048],
                                             in_=pb[1][:]),
                     r=("PB1",), w=("hsT",), tag="hsTev2")
                for hf in range(2):
                    bank = pf[4 + hf]
                    bkk = f"PF{4 + hf}"
                    for jh in range(8):
                        P.op("pe", lambda e, jh=jh, hf=hf, bank=bank: e.matmul(
                                bank[:], lhsT=hsT[:, jh * 128:(jh + 1) * 128],
                                rhs=cre_sb[:, jh * 1024 + hf * 512:
                                           jh * 1024 + hf * 512 + 512],
                                start=(jh == 0), stop=False),
                             r=("hsT", "cre"), w=(bkk,), tag="cmmre")
                    for jh in range(8):
                        P.op("pe", lambda e, jh=jh, hf=hf, bank=bank: e.matmul(
                                bank[:],
                                lhsT=hsT[:, 1024 + jh * 128:
                                         1024 + (jh + 1) * 128],
                                rhs=cim_sb[:, jh * 1024 + hf * 512:
                                           jh * 1024 + hf * 512 + 512],
                                start=False, stop=(jh == 7)),
                             r=("hsT", "cim"), w=(bkk,), tag="cmmim")
                    P.op("act", lambda e, hf=hf, m=m, par=par, xv=xv:
                         e.activation(
                            out=skscr[:, hf * 512:(hf + 1) * 512],
                            in_=xv[:, hf * 512:(hf + 1) * 512],
                            func=AF.Identity,
                            scale=lnstC[par][:, 2 * m:2 * m + 1],
                            bias=lnstC[par][:, 2 * m + 1:2 * m + 2]),
                         r=(xk, f"lnstC{par}"), w=("skscr",), tag="skln")
                    P.op("dve", lambda e, hf=hf: e.tensor_tensor(
                            out=sksc2[:, hf * 512:(hf + 1) * 512],
                            in0=drep[:, hf * 512:(hf + 1) * 512],
                            in1=skscr[:, hf * 512:(hf + 1) * 512], op=ALU.mult),
                         r=("drep", "skscr"), w=("sksc2",), tag="skip")
                    P.op("dve", lambda e, hf=hf, bank=bank: e.tensor_tensor(
                            out=yC[:, hf * 512:(hf + 1) * 512], in0=bank[:],
                            in1=sksc2[:, hf * 512:(hf + 1) * 512], op=ALU.add),
                         r=(bkk, "sksc2"), w=("yC",), tag="cout")
                ln(yC[:], ("yC",), [(lnscr2[:], "lnscr2")])
                P.op("dve", lambda e, xv=xv: e.tensor_tensor(
                        out=xv, in0=xv, in1=lnscr2[:], op=ALU.add),
                     r=("lnscr2",), w=(xk,), tag="xaddC")

            def c_leadin(c):
                par = c % 2
                xk = f"xC{par}"
                xt = xC[par]
                ntk = f"n1TC{par}"
                src = xst_d[c * CH:(c + 1) * CH, :].rearrange(
                    "(m p) d -> p m d", p=128)
                P.dma("sp", lambda e, xt=xt, src=src: e.dma_start(
                        out=xt[:].rearrange("p (m d) -> p m d", m=MT), in_=src),
                      r=(f"xst{c}",), w=(xk,), tag=f"ldxC{c}")
                stk = f"lnstC{par}"
                for m in range(MT):
                    xv = xt[:, m * 1024:(m + 1) * 1024]
                    ln_stats(xv, (xk,), lnstC[par], stk, m)
                for m in range(MT):
                    xv = xt[:, m * 1024:(m + 1) * 1024]
                    nb = nbfC2[m % 2]
                    nbk = f"nbfC{m % 2}"
                    ln_apply(xv, (xk,), nb[:], nbk, lnstC[par], stk, m)
                    tb = pb[m % 2]
                    transpose_pack(lambda j, nb=nb: nb[:, j * 128:(j + 1) * 128],
                                   (nbk,), tb, f"PB{m % 2}")
                    P.op("act", lambda e, m=m, tb=tb, par=par: e.copy(
                            out=n1TC[par][:, m * 1024:(m + 1) * 1024],
                            in_=tb[:]),
                         r=(f"PB{m % 2}",), w=(ntk,), tag="n1Tev")

            def c_store(c):
                par = c % 2
                xk = f"xC{par}"
                xt = xC[par]
                dst = xst_d[c * CH:(c + 1) * CH, :].rearrange(
                    "(m p) d -> p m d", p=128)
                P.dma("act", lambda e, xt=xt, dst=dst: e.dma_start(
                        out=dst, in_=xt[:].rearrange("p (m d) -> p m d", m=MT)),
                      r=(xk,), w=(f"xst{c}",), tag=f"stxC{c}")

            c_leadin(0)
            for c in range(NCH):
                c_scan_stage(c, 0)
                if c > 0:
                    c_post_stage(c - 1, 3)
                    c_store(c - 1)
                c_scan_stage(c, 1)
                c_post_stage(c, 0)
                if c + 1 < NCH:
                    c_leadin(c + 1)
                c_scan_stage(c, 2)
                c_post_stage(c, 1)
                c_scan_stage(c, 3)
                c_post_stage(c, 2)
            c_post_stage(NCH - 1, 3)
            c_store(NCH - 1)
            P.flush("C")
            cctx.close()
            P.barrier()

        if "D" in phases:
            dctx = ExitStack()
            dsb = lambda name, shp, dt: dctx.enter_context(
                nc.sbuf_tensor(name, shp, dt))
            n3Tgs = [dsb(f"n3Tg{i}", [128, 8192], BF16) for i in range(2)]
            nbfD2 = [dsb(f"nbfD{i}", [128, 1024], BF16) for i in range(2)]
            stD = dsb("stD", [128, 16], F32)
            f_g = dsb("f_g", [128, 8192], F32)
            gatess = [dsb(f"gates{i}", [128, 64], F32) for i in range(2)]
            w1d = [dsb(f"w1d{i}", [128, 8192], BF16) for i in range(2)]
            w3d = [dsb(f"w3d{i}", [128, 8192], BF16) for i in range(2)]
            w2d = dsb("w2d", [128, 8192], BF16)
            rw_sb = dsb("rw_sb", [128, 64], BF16)
            ebrep = dsb("ebrep", [128, 8], F32)
            ebrow = dsb("ebrow", [1, 8], F32)
            xmt = [dsb(f"xmt{i}", [128, 1024], F32) for i in range(4)]
            xrt = dsb("xrt", [128, 1024], F32)
            y1s = dsb("y1s", [128, 1024], F32)
            mid = dsb("mid", [128, 4096], BF16)
            gz = dsb("gz", [128, 8], F32)
            gp = dsb("gp", [128, 8], F32)
            gt = dsb("gt", [128, 8], F32)
            gt2 = dsb("gt2", [128, 8], F32)
            gtop1 = dsb("gtop1", [128, 8], F32)
            gtop2 = dsb("gtop2", [128, 8], F32)
            gmx = dsb("gmx", [128, 1], F32)
            gnm = dsb("gnm", [128, 1], F32)

            P.dma("pool", lambda e: e.dma_start(
                    out=rw_sb[:].rearrange("p (j x) -> p j x", j=8),
                    in_=DI["rw"].rearrange("(j p) x -> p j x", p=128)),
                  w=("rw",), tag="ld_rw")
            ldc("ebias_row", ebrow)
            P.op("pe", lambda e: e.matmul(pf[5][:, 0:8], lhsT=onesrow_f[:],
                                          rhs=ebrow[:], start=True, stop=True),
                 r=("ebias_row", "onesrow_f"), w=("PF5",), tag="ebmm")
            P.op("act", lambda e: e.copy(out=ebrep[:], in_=pf[5][:, 0:8]),
                 r=("PF5",), w=("ebrep",), tag="ebcp")

            exlist = [("sh", "sw1", "sw2", "sw3", None)] + [
                (f"e{ee}", "w1", "w2", "w3", ee) for ee in range(8)]

            def d_li_stats(g):
                for mt in range(8):
                    xb = xmt[mt % 4]
                    xbk = f"xmt{mt % 4}"
                    src = xst_d[g * GT + mt * 128:g * GT + (mt + 1) * 128, :]
                    P.dma("sp", lambda e, xb=xb, src=src: e.dma_start(
                            out=xb[:], in_=src),
                          r=(f"xst{2 * g + mt // 4}",), w=(xbk,),
                          tag=f"ldxD{g}_{mt}")
                    ln_stats(xb[:], (xbk,), stD, "stD", mt)

            def d_li_applyall(g):
                gpar = g % 2
                n3 = n3Tgs[gpar]
                n3k = f"n3Tg{gpar}"
                gat = gatess[gpar]
                gatk = f"gates{gpar}"
                def d_li_apply(mt):
                    xb = xmt[mt % 4]
                    xbk = f"xmt{mt % 4}"
                    src = xst_d[g * GT + mt * 128:g * GT + (mt + 1) * 128, :]
                    P.dma("sp", lambda e, xb=xb, src=src: e.dma_start(
                            out=xb[:], in_=src),
                          r=(f"xst{2 * g + mt // 4}",), w=(xbk,),
                          tag=f"ldxD2_{g}_{mt}")
                    nb = nbfD2[mt % 2]
                    nbk = f"nbfD{mt % 2}"
                    ln_apply(xb[:], (xbk,), nb[:], nbk, stD, "stD", mt)
                    tb = pb[mt % 2]
                    transpose_pack(lambda j, nb=nb: nb[:, j * 128:(j + 1) * 128],
                                   (nbk,), tb, f"PB{mt % 2}")
                    dv = n3[:].rearrange("p (j t) -> p j t", j=8)[
                        :, :, mt * 128:(mt + 1) * 128]
                    sv = tb[:].rearrange("p (j t) -> p j t", j=8)
                    P.op("dve", lambda e, dv=dv, sv=sv: e.tensor_copy(dv, sv),
                         r=(f"PB{mt % 2}",), w=(n3k,), tag="n3Tgev")
                    for j in range(8):
                        P.op("pe", lambda e, j=j, mt=mt, n3=n3: e.matmul(
                                pf[4][:, 0:8],
                                lhsT=n3[:, j * 1024 + mt * 128:
                                        j * 1024 + mt * 128 + 128],
                                rhs=rw_sb[:, j * 8:(j + 1) * 8],
                                start=(j == 0), stop=(j == 7)),
                             r=(n3k, "rw"), w=("PF4",), tag="rmm")
                    P.op("dve", lambda e: e.tensor_reduce(
                            out=gmx[:], in_=pf[4][:, 0:8], axis=AX, op=ALU.max),
                         r=("PF4",), w=("gmx",), tag="gmx", swb=True)
                    P.op("dve", lambda e: e.tensor_scalar(
                            out=gnm[:], in0=gmx[:], scalar1=-1.0, scalar2=None,
                            op0=ALU.mult),
                         r=("gmx",), w=("gnm",), tag="gnm", early=True)
                    P.op("act", lambda e: e.activation(
                            out=gz[:], in_=pf[4][:, 0:8], func=AF.Exp,
                            bias=gnm[:]),
                         r=("PF4", "gnm"), w=("gz",), tag="gexp")
                    P.op("dve", lambda e: e.tensor_reduce(
                            out=gmx[:], in_=gz[:], axis=AX, op=ALU.add),
                         r=("gz",), w=("gmx",), tag="gsum", swb=True)
                    P.op("dve", lambda e: e.reciprocal(out=gnm[:], in_=gmx[:]),
                         r=("gmx",), w=("gnm",), tag="grs", swb=True)
                    P.op("dve", lambda e: e.tensor_scalar(
                            out=gp[:], in0=gz[:], scalar1=gnm[:], scalar2=None,
                            op0=ALU.mult),
                         r=("gz", "gnm"), w=("gp",), tag="gp", early=True)
                    P.op("dve", lambda e: e.tensor_tensor(
                            out=gt[:], in0=gp[:], in1=ebrep[:], op=ALU.add),
                         r=("gp", "ebrep"), w=("gt",), tag="gt")
                    P.op("dve", lambda e: e.tensor_reduce(
                            out=gmx[:], in_=gt[:], axis=AX, op=ALU.max),
                         r=("gt",), w=("gmx",), tag="gm1", swb=True)
                    P.op("dve", lambda e: e.tensor_scalar(
                            out=gtop1[:], in0=gt[:], scalar1=gmx[:], scalar2=None,
                            op0=ALU.is_ge),
                         r=("gt", "gmx"), w=("gtop1",), tag="gtop1", early=True)
                    P.op("dve", lambda e: e.scalar_tensor_tensor(
                            out=gt2[:], in0=gtop1[:], scalar=-1e9, in1=gt[:],
                            op0=ALU.mult, op1=ALU.add),
                         r=("gtop1", "gt"), w=("gt2",), tag="gt2")
                    P.op("dve", lambda e: e.tensor_reduce(
                            out=gmx[:], in_=gt2[:], axis=AX, op=ALU.max),
                         r=("gt2",), w=("gmx",), tag="gm2", swb=True)
                    P.op("dve", lambda e: e.tensor_scalar(
                            out=gtop2[:], in0=gt2[:], scalar1=gmx[:], scalar2=None,
                            op0=ALU.is_ge),
                         r=("gt2", "gmx"), w=("gtop2",), tag="gtop2", early=True)
                    P.op("dve", lambda e: e.tensor_tensor(
                            out=gtop1[:], in0=gtop1[:], in1=gtop2[:], op=ALU.add),
                         r=("gtop2",), w=("gtop1",), tag="gmask")
                    P.op("dve", lambda e, mt=mt, gat=gat: e.tensor_tensor(
                            out=gat[:, mt * 8:(mt + 1) * 8], in0=gp[:],
                            in1=gtop1[:], op=ALU.mult),
                         r=("gp", "gtop1"), w=(gatk,), tag="gates")

                for mt in range(8):
                    d_li_apply(mt)

            def d_experts(g):
                gpar = g % 2
                n3 = n3Tgs[gpar]
                n3k = f"n3Tg{gpar}"
                gat = gatess[gpar]
                gatk = f"gates{gpar}"
                for xi, (ename, k1, k2, k3, ee) in enumerate(exlist):
                    par = (g * 9 + xi) % 2
                    for knm, wt, wkey in ((k1, w1d[par], f"w1d{par}"),
                                          (k3, w3d[par], f"w3d{par}"),
                                          (k2, w2d, "w2d")):
                        d = DI[knm] if ee is None else DI[knm][ee]
                        P.dma("pool", lambda e, wt=wt, d=d: e.dma_start(
                                out=wt[:].rearrange("p (j o) -> p j o", j=8),
                                in_=d.rearrange("(j p) o -> p j o", p=128)),
                              w=(wkey,), tag=f"ldw_{ename}_{knm}")
                    for u in range(2):
                        for jh in range(8):
                            b1 = pf[jh % 2]
                            b1k = f"PF{jh % 2}"
                            b3 = pf[2 + jh % 2]
                            b3k = f"PF{2 + jh % 2}"
                            for jd in range(8):
                                P.op("pe", lambda e, jd=jd, jh=jh, u=u, par=par,
                                     b1=b1, n3=n3: e.matmul(
                                        b1[:],
                                        lhsT=w1d[par][:, jd * 1024 + jh * 128:
                                                      jd * 1024 + jh * 128 + 128],
                                        rhs=n3[:, jd * 1024 + u * 512:
                                               jd * 1024 + u * 512 + 512],
                                        start=(jd == 0), stop=(jd == 7)),
                                     r=(f"w1d{par}", n3k), w=(b1k,), tag="y1mm")
                            P.op("act", lambda e, jh=jh, b1=b1: e.activation(
                                    out=y1s[:, (jh % 2) * 512:(jh % 2) * 512 + 512],
                                    in_=b1[:], func=AF.Silu),
                                 r=(b1k,), w=("y1s",), tag="silu")
                            for jd in range(8):
                                P.op("pe", lambda e, jd=jd, jh=jh, u=u, par=par,
                                     b3=b3, n3=n3: e.matmul(
                                        b3[:],
                                        lhsT=w3d[par][:, jd * 1024 + jh * 128:
                                                      jd * 1024 + jh * 128 + 128],
                                        rhs=n3[:, jd * 1024 + u * 512:
                                               jd * 1024 + u * 512 + 512],
                                        start=(jd == 0), stop=(jd == 7)),
                                     r=(f"w3d{par}", n3k), w=(b3k,), tag="y3mm")
                            P.op("dve", lambda e, jh=jh, b3=b3: e.tensor_tensor(
                                    out=mid[:, jh * 512:(jh + 1) * 512],
                                    in0=b3[:],
                                    in1=y1s[:, (jh % 2) * 512:(jh % 2) * 512 + 512],
                                    op=ALU.mult),
                                 r=(b3k, "y1s"), w=("mid",), tag="mid")
                        for mt2 in range(4):
                            tile = u * 4 + mt2
                            for dh in range(2):
                                bo = pf[4 + dh]
                                bok = f"PF{4 + dh}"
                                for jh in range(8):
                                    P.op("pe", lambda e, jh=jh, mt2=mt2, dh=dh,
                                         bo=bo: e.matmul(
                                            bo[:],
                                            lhsT=mid[:, jh * 512 + mt2 * 128:
                                                     jh * 512 + mt2 * 128 + 128],
                                            rhs=w2d[:, jh * 1024 + dh * 512:
                                                    jh * 1024 + dh * 512 + 512],
                                            start=(jh == 0), stop=(jh == 7)),
                                         r=("mid", "w2d"), w=(bok,),
                                         tag="y2mm")
                                fsl = f_g[:, tile * 1024 + dh * 512:
                                          tile * 1024 + dh * 512 + 512]
                                if ee is None:
                                    P.op("dve", lambda e, fsl=fsl, bo=bo:
                                         e.tensor_copy(fsl, bo[:]),
                                         r=(bok,), w=(f"fg{tile}",), tag="fsh")
                                else:
                                    gcol = gat[:, tile * 8 + ee:
                                               tile * 8 + ee + 1]
                                    P.op("dve", lambda e, fsl=fsl, bo=bo,
                                         gcol=gcol: e.scalar_tensor_tensor(
                                            out=fsl, in0=bo[:], scalar=gcol,
                                            in1=fsl, op0=ALU.mult, op1=ALU.add),
                                         r=(bok, gatk, f"fg{tile}"),
                                         w=(f"fg{tile}",),
                                         tag="facc", early=True)

            def d_tail(g):
                for mt in range(8):
                    src = xst_d[g * GT + mt * 128:g * GT + (mt + 1) * 128, :]
                    P.dma("sp", lambda e, src=src: e.dma_start(
                            out=xrt[:], in_=src),
                          r=(f"xst{2 * g + mt // 4}",), w=("xrt",),
                          tag=f"ldxT{g}_{mt}")
                    ln(f_g[:, mt * 1024:(mt + 1) * 1024], (f"fg{mt}",),
                       [(lnscr2[:], "lnscr2")])
                    P.op("dve", lambda e: e.tensor_tensor(
                            out=xrt[:], in0=xrt[:], in1=lnscr2[:], op=ALU.add),
                         r=("lnscr2",), w=("xrt",), tag="xaddD")
                    dst = out_d[g * GT + mt * 128:g * GT + (mt + 1) * 128, :]
                    P.dma("act", lambda e, dst=dst: e.dma_start(
                            out=dst, in_=xrt[:]),
                          r=("xrt",), w=("out_dram",), tag=f"stD{g}_{mt}")

            d_li_stats(0)
            d_li_applyall(0)
            d_li_stats(1)
            d_li_applyall(1)
            for g in range(NG):
                if g + 2 < NG:
                    d_li_stats(g + 2)
                d_experts(g)
                if g + 2 < NG:
                    d_li_applyall(g + 2)
                d_tail(g)
            P.flush("D")
            dctx.close()

        if "DUMP" in phases:
            for c in range(NCH):
                P.dma("sp", lambda e, c=c: e.dma_start(
                        out=out_d[c * CH:(c + 1) * CH, :],
                        in_=xst_d[c * CH:(c + 1) * CH, :]),
                      r=(f"xst{c}",), w=("out_dram",), tag=f"dump{c}")
            P.flush("DUMP")

        P.op("sp", lambda e: e.nop(),
             r=("out_dram", "dbg_dram") if (DBG or DBG2) else ("out_dram",),
             tag="fence")
        P.flush("FIN")

    return nc


def kernel(**inputs):
    common = _prep_common(inputs)
    x_full = np.ascontiguousarray(np.asarray(inputs["L0_attn_x"], dtype=np.float32))
    nc = build_nc()
    in_maps = []
    for core in range(8):
        m = dict(common)
        m["x0"] = np.ascontiguousarray(x_full[core])
        in_maps.append(m)
    res = run_bass_kernel_spmd(nc, in_maps, core_ids=list(range(8)))
    return np.stack([np.asarray(r["out"], dtype=np.float32)
                     for r in res.results], axis=0)


# version 14 (final)
# speedup vs baseline: 1.2248x; 1.0066x over previous
import numpy as np
import ml_dtypes
from contextlib import ExitStack

import concourse.bass as bass
import concourse.mybir as mybir
from concourse.bass_utils import run_bass_kernel_spmd
from concourse.alu_op_type import AluOpType as ALU

F32 = mybir.dt.float32
BF16 = mybir.dt.bfloat16
AF = mybir.ActivationFunctionType
AX = mybir.AxisListType.X
BF16NP = ml_dtypes.bfloat16

S = 4096
DM = 1024
NCH = 8
CH = 512
MT = 4
NG = 4
GT = 1024
PIF = float(np.pi)
EPS = 1e-5
DBG = False
DBG2 = False


class _Op:
    __slots__ = ("i", "eng", "fn", "r", "w", "dma", "dmai", "inc", "incv",
                 "deps", "tag", "flushed", "fend", "waits", "swb", "early")

    def __init__(self, i, eng, fn, r, w, dma, dmai, tag, swb=False, early=False):
        self.i, self.eng, self.fn = i, eng, fn
        self.r, self.w = tuple(r), tuple(w)
        self.dma, self.dmai = dma, dmai
        self.inc, self.incv = False, None
        self.deps, self.tag = [], tag
        self.flushed, self.fend, self.waits = False, None, []
        self.swb, self.early = swb, early


class Prog:
    CENG = ("pe", "act", "dve", "pool", "sp")
    QENG = ("sp", "act", "pool")

    NLANE = 8

    def __init__(self, nc, ctx):
        self.nc = nc
        self.ops = []
        self.pending = []
        self.last_w = {}
        self.readers = {}
        self.inc_cnt = {e: 0 for e in self.CENG}
        self.dma_cnt = {e: 0 for e in self.QENG}
        self.lane_cnt = {e: [0] * self.NLANE for e in self.QENG}
        self.sems = {e: ctx.enter_context(nc.semaphore(f"s_{e}")) for e in self.CENG}
        self.dsems = {e: [ctx.enter_context(nc.semaphore(f"d_{e}{i}"))
                          for i in range(self.NLANE)] for e in self.QENG}
        self.waited = {e: {} for e in self.CENG}
        self.nflush = 0
        self.barrier_req = {}

    def _track(self, o):
        deps = set()
        for k in o.r:
            j = self.last_w.get(k)
            if j is not None:
                deps.add(j)
            self.readers.setdefault(k, []).append(o.i)
        for k in o.w:
            j = self.last_w.get(k)
            if j is not None:
                deps.add(j)
            for rj in self.readers.get(k, ()):
                deps.add(rj)
            self.last_w[k] = o.i
            self.readers[k] = []
        deps.discard(o.i)
        o.deps = sorted(deps)

    def op(self, eng, fn, r=(), w=(), tag="", swb=False, early=False):
        o = _Op(len(self.ops), eng, fn, r, w, False, None, tag, swb, early)
        self._track(o)
        self.ops.append(o)
        self.pending.append(o)
        return o

    def dma(self, eng, fn, r=(), w=(), tag=""):
        assert eng in self.QENG
        lane = self.dma_cnt[eng] % self.NLANE
        self.dma_cnt[eng] += 1
        self.lane_cnt[eng][lane] += 1
        o = _Op(len(self.ops), eng, fn, r, w, True,
                (lane, self.lane_cnt[eng][lane]), tag)
        self._track(o)
        self.ops.append(o)
        self.pending.append(o)
        return o

    def barrier(self):
        assert not self.pending
        snap = {}
        for e2 in self.CENG:
            if self.inc_cnt[e2]:
                snap[("e", e2)] = self.inc_cnt[e2]
        for q in self.QENG:
            for lane in range(self.NLANE):
                if self.lane_cnt[q][lane]:
                    snap[("d", q, lane)] = 16 * self.lane_cnt[q][lane]
        for e in self.CENG:
            self.barrier_req[e] = dict(snap)

    def flush(self, name):
        for o in self.pending:
            for d in o.deps:
                dop = self.ops[d]
                if dop.dma or dop.flushed:
                    continue
                if dop.eng == o.eng and (dop.eng in ("pe", "sp")):
                    continue
                dop.inc = True
        for o in self.pending:
            if o.dma:
                continue
            if o.inc:
                self.inc_cnt[o.eng] += 1
                o.incv = self.inc_cnt[o.eng]
        last = {}
        for o in self.pending:
            if not o.dma:
                last[o.eng] = o
        for o in last.values():
            if o.incv is None:
                o.inc = True
                self.inc_cnt[o.eng] += 1
                o.incv = self.inc_cnt[o.eng]
        first_seen = set()
        for o in self.pending:
            wd = self.waited[o.eng]
            ws = {}
            if o.eng not in first_seen:
                first_seen.add(o.eng)
                breq = self.barrier_req.pop(o.eng, None)
                if breq:
                    for sem, val in breq.items():
                        if val > ws.get(sem, 0):
                            ws[sem] = val
            for d in o.deps:
                dop = self.ops[d]
                if dop.dma:
                    lane, cnt = dop.dmai
                    sem, val = ("d", dop.eng, lane), 16 * cnt
                elif dop.eng == o.eng and (dop.eng in ("pe", "sp")):
                    continue
                elif dop.flushed:
                    sem, val = ("e", dop.eng), dop.fend
                else:
                    assert dop.incv is not None, (dop.tag, o.tag)
                    sem, val = ("e", dop.eng), dop.incv
                if val > ws.get(sem, 0):
                    ws[sem] = val
            o.waits = [(sem, val) for sem, val in sorted(ws.items())
                       if wd.get(sem, 0) < val]
            for sem, val in o.waits:
                wd[sem] = val
        for o in self.pending:
            o.flushed = True
            o.fend = self.inc_cnt[o.eng] if not o.dma else None
        pend = self.pending
        self.pending = []
        self.nflush += 1
        secmap = {"sp": "sync", "pe": "tensor", "act": "scalar",
                  "dve": "vector", "pool": "gpsimd"}
        with self.nc.Block(f"f{self.nflush}_{name}") as blk:
            for engname in self.CENG:
                ops_e = [o for o in pend if o.eng == engname]
                if not ops_e:
                    continue

                def body(eng, ops_e=ops_e, engname=engname):
                    for o in ops_e:
                        for key, val in o.waits:
                            if key[0] == "d":
                                sem = self.dsems[key[1]][key[2]]
                            else:
                                sem = self.sems[key[1]]
                            eng.wait_ge(sem, val)
                        inst = o.fn(eng)
                        if o.dma:
                            inst.then_inc(self.dsems[o.eng][o.dmai[0]], 16)
                        elif o.inc:
                            assert inst is not None, o.tag
                            inst.then_inc(self.sems[o.eng], 1)

                getattr(blk, secmap[engname])(body)


def _prep_common(I):
    a = lambda x: np.ascontiguousarray(np.asarray(x, dtype=np.float32))
    c = {}
    c["wqT"] = a(np.asarray(I["L0_attn_Wq"]).T)
    c["wkT"] = a(np.asarray(I["L0_attn_Wk"]).T)
    c["wvT"] = a(np.asarray(I["L0_attn_Wv"]).T)
    c["woT"] = a(np.asarray(I["L0_attn_Wo"]).T)
    c["bq_c"] = a(np.asarray(I["L0_attn_bq"]).reshape(8, 128).T)
    c["bk_c"] = a(np.asarray(I["L0_attn_bk"]).reshape(8, 128).T)
    c["bv_row"] = a(np.asarray(I["L0_attn_bv"])[None, :])
    c["bo_row"] = a(np.asarray(I["L0_attn_bo"])[None, :])
    c["a_rt"] = a(np.transpose(np.asarray(I["L0_ffn_G1_R"])[0], (1, 2, 0)))
    c["a_it"] = a(np.transpose(np.asarray(I["L0_ffn_G1_I"])[0], (1, 2, 0)))
    c["b_rt"] = a(np.transpose(np.asarray(I["L0_ffn_G2_R"])[:, :, 0, :], (0, 2, 1)))
    c["b_it"] = a(np.transpose(np.asarray(I["L0_ffn_G2_I"])[:, :, 0, :], (0, 2, 1)))
    c["brl_row"] = a(np.asarray(I["L0_ffn_bias_real"])[None, :])
    c["bil_row"] = a(np.asarray(I["L0_ffn_bias_imag"])[None, :])
    c["nulog_row"] = a(np.asarray(I["L1_attn_nu_log"])[None, :])
    c["thetalog_row"] = a(np.asarray(I["L1_attn_theta_log"])[None, :])
    c["breT"] = a(np.asarray(I["L1_attn_B_re"]).T)
    c["bimT"] = a(np.asarray(I["L1_attn_B_im"]).T)
    c["creT"] = a(np.asarray(I["L1_attn_C_re"]).T)
    c["cimT"] = a(np.asarray(I["L1_attn_C_im"]).T)
    c["dskip_row"] = a(np.asarray(I["L1_attn_D_skip"])[None, :])
    c["rw"] = a(I["L1_ffn_router_w"])
    c["ebias_row"] = a(np.asarray(I["L1_ffn_expert_bias"])[None, :])
    c["w1"] = a(I["L1_ffn_w1"])
    c["w2"] = a(I["L1_ffn_w2"])
    c["w3"] = a(I["L1_ffn_w3"])
    c["sw1"] = a(I["L1_ffn_sw1"])
    c["sw2"] = a(I["L1_ffn_sw2"])
    c["sw3"] = a(I["L1_ffn_sw3"])
    c["id128"] = np.eye(128, dtype=BF16NP)
    c["id64d"] = np.vstack([np.eye(64), np.eye(64)]).astype(BF16NP)
    c["ut128"] = np.triu(np.ones((128, 128), dtype=np.float32))
    c["ut128b"] = np.triu(np.ones((128, 128), dtype=np.float32)).astype(BF16NP)
    c["ones_bf"] = np.ones((128, 1), dtype=BF16NP)
    c["onesrow_f"] = np.ones((1, 128), dtype=np.float32)
    c["onesrow_bf"] = np.ones((1, 128), dtype=BF16NP)
    c["iota1_row"] = np.arange(1, 129, dtype=np.float32)[None, :]
    e127 = np.zeros((128, 1), dtype=np.float32)
    e127[127, 0] = 1.0
    c["e127_f"] = e127
    c["e127_bf"] = e127.astype(BF16NP)
    c["eps_col"] = np.full((128, 1), EPS, dtype=np.float32)
    ecq = np.zeros((32, 512), dtype=np.float32)
    for q in range(4):
        for p in range(128):
            ecq[8 * q + p // 16, q * 128 + p] = 1.0
    c["ecq"] = ecq
    ecg = np.zeros((16, 128), dtype=np.float32)
    for p in range(128):
        ecg[p % 16, p] = 1.0
    c["ecg"] = ecg
    return c


_IN_SPECS = {
    "x0": ([S, DM], F32),
    "wqT": ([DM, DM], F32), "wkT": ([DM, DM], F32),
    "wvT": ([DM, DM], F32), "woT": ([DM, DM], F32),
    "bq_c": ([128, 8], F32), "bk_c": ([128, 8], F32),
    "bv_row": ([1, DM], F32), "bo_row": ([1, DM], F32),
    "a_rt": ([4, 32, 32], F32), "a_it": ([4, 32, 32], F32),
    "b_rt": ([4, 16, 16], F32), "b_it": ([4, 16, 16], F32),
    "brl_row": ([1, 512], F32), "bil_row": ([1, 512], F32),
    "nulog_row": ([1, DM], F32), "thetalog_row": ([1, DM], F32),
    "breT": ([DM, DM], F32), "bimT": ([DM, DM], F32),
    "creT": ([DM, DM], F32), "cimT": ([DM, DM], F32),
    "dskip_row": ([1, DM], F32),
    "rw": ([DM, 8], F32), "ebias_row": ([1, 8], F32),
    "w1": ([8, DM, DM], F32), "w2": ([8, DM, DM], F32), "w3": ([8, DM, DM], F32),
    "sw1": ([DM, DM], F32), "sw2": ([DM, DM], F32), "sw3": ([DM, DM], F32),
    "id128": ([128, 128], BF16), "id64d": ([128, 64], BF16),
    "ut128": ([128, 128], F32), "ut128b": ([128, 128], BF16),
    "ones_bf": ([128, 1], BF16),
    "onesrow_f": ([1, 128], F32), "onesrow_bf": ([1, 128], BF16),
    "iota1_row": ([1, 128], F32),
    "e127_f": ([128, 1], F32), "e127_bf": ([128, 1], BF16),
    "eps_col": ([128, 1], F32),
    "ecq": ([32, 512], F32), "ecg": ([16, 128], F32),
}


def build_nc(phases=("P0", "AB", "C", "D")):
    nc = bass.Bass("TRN2", target_bir_lowering=False)
    DI = {k: nc.dram_tensor(k, shp, dt, kind="ExternalInput").ap()
          for k, (shp, dt) in _IN_SPECS.items()}
    out_d = nc.dram_tensor("out", [S, DM], F32, kind="ExternalOutput").ap()
    xst_d = nc.dram_tensor("xst", [S, DM], F32, kind="Internal").ap()
    dbg_d = (nc.dram_tensor("dbg", [128, 16384], F32, kind="ExternalOutput").ap()
             if (DBG or DBG2) else None)

    ctx = ExitStack()
    with ctx:
        P = Prog(nc, ctx)
        sb = lambda name, shp, dt: ctx.enter_context(nc.sbuf_tensor(name, shp, dt))
        ps = lambda name, shp, dt: ctx.enter_context(nc.psum_tensor(name, shp, dt))

        pf = [ps(f"pf{i}", [128, 512], F32) for i in range(6)]
        pb = [ps(f"pb{i}", [128, 1024], BF16) for i in range(2)]

        id128 = sb("t_id128", [128, 128], BF16)
        id64d = sb("t_id64d", [128, 64], BF16)
        ut128 = sb("t_ut128", [128, 128], F32)
        ut128b = sb("t_ut128b", [128, 128], BF16)
        ones_bf = sb("t_ones_bf", [128, 1], BF16)
        onesrow_f = sb("t_onesrow_f", [1, 128], F32)
        onesrow_bf = sb("t_onesrow_bf", [1, 128], BF16)
        iota1_row = sb("t_iota1_row", [1, 128], F32)
        e127_f = sb("t_e127_f", [128, 1], F32)
        e127_bf = sb("t_e127_bf", [128, 1], BF16)
        eps_col = sb("t_eps_col", [128, 1], F32)
        lnst = sb("lnst", [128, 12], F32)
        lnmv = sb("lnmv", [128, 2], F32)
        lnlv = sb("lnlv", [128, 1], F32)
        lnrstd = sb("lnrstd", [128, 1], F32)
        lnnmr = sb("lnnmr", [128, 1], F32)
        lnscr2 = sb("lnscr2", [128, 1024], F32)
        dbgs = sb("dbgs", [128, 80], F32) if DBG else None
        mtb = [sb(f"mtb{i}", [128, 128], BF16) for i in range(2)]

        def dbgdump(col, ap, keys, w, always=False):
            if not (DBG or (DBG2 and always)):
                return
            P.dma("pool", lambda e, ap=ap, col=col, w=w: e.dma_start(
                    out=dbg_d[:, col:col + w], in_=ap,
                    allow_slow_non_contiguous=True),
                  r=tuple(keys), w=("dbg_dram",), tag=f"dbg{col}")

        def ldc(name, tile, cast=False):
            q = "pool" if cast else "sp"
            P.dma(q, lambda e, t=tile, d=DI[name]: e.dma_start(out=t[:], in_=d),
                  w=(name,), tag=f"ld_{name}")

        def ln(x_ap, xkeys, outs):
            xkeys = tuple(xkeys)
            P.op("dve", lambda e: e.bn_stats(lnst[:, 0:6], x_ap[:, 0:512]),
                 r=xkeys, w=("lnst",), tag="ln_s0", swb=True)
            P.op("dve", lambda e: e.bn_stats(lnst[:, 6:12], x_ap[:, 512:1024]),
                 r=xkeys, w=("lnst",), tag="ln_s1", swb=True)
            P.op("dve", lambda e: e.bn_aggr(lnmv[:], lnst[:]),
                 r=("lnst",), w=("lnmv",), tag="ln_ag", swb=True)
            P.op("act", lambda e: e.activation(out=lnlv[:], in_=lnmv[:, 1:2],
                                               func=AF.Ln, bias=eps_col[:],
                                               scale=1.0),
                 r=("lnmv", "epsc"), w=("lnlv",), tag="ln_lnv")
            P.op("act", lambda e: e.activation(out=lnrstd[:], in_=lnlv[:],
                                               func=AF.Exp, scale=-0.5),
                 r=("lnlv",), w=("lnrstd",), tag="ln_rstd")
            P.op("dve", lambda e: e.tensor_scalar(
                    out=lnnmr[:], in0=lnmv[:, 0:1], scalar1=lnrstd[:],
                    scalar2=-1.0, op0=ALU.mult, op1=ALU.mult),
                 r=("lnmv", "lnrstd"), w=("lnnmr",), tag="ln_nmr", early=True)
            for (o_ap, o_key) in outs:
                P.op("act", lambda e, o_ap=o_ap: e.activation(
                        out=o_ap, in_=x_ap, func=AF.Identity,
                        scale=lnrstd[:], bias=lnnmr[:]),
                     r=xkeys + ("lnrstd", "lnnmr"), w=(o_key,), tag="ln_apply")

        def ln_stats(x_ap, xkeys, stb, stbk, sm):
            xkeys = tuple(xkeys)
            P.op("dve", lambda e: e.bn_stats(lnst[:, 0:6], x_ap[:, 0:512]),
                 r=xkeys, w=("lnst",), tag="lns_s0")
            P.op("dve", lambda e: e.bn_stats(lnst[:, 6:12], x_ap[:, 512:1024]),
                 r=xkeys, w=("lnst",), tag="lns_s1")
            P.op("dve", lambda e: e.bn_aggr(lnmv[:], lnst[:]),
                 r=("lnst",), w=("lnmv",), tag="lns_ag")
            P.op("act", lambda e: e.activation(out=lnlv[:], in_=lnmv[:, 1:2],
                                               func=AF.Ln, bias=eps_col[:],
                                               scale=1.0),
                 r=("lnmv", "epsc"), w=("lnlv",), tag="lns_lnv")
            P.op("act", lambda e: e.activation(
                    out=stb[:, 2 * sm:2 * sm + 1], in_=lnlv[:],
                    func=AF.Exp, scale=-0.5),
                 r=("lnlv",), w=(stbk,), tag="lns_rstd")
            P.op("dve", lambda e: e.tensor_scalar(
                    out=stb[:, 2 * sm + 1:2 * sm + 2], in0=lnmv[:, 0:1],
                    scalar1=stb[:, 2 * sm:2 * sm + 1],
                    scalar2=-1.0, op0=ALU.mult, op1=ALU.mult),
                 r=("lnmv", stbk), w=(stbk,), tag="lns_nmr")

        def ln_apply(x_ap, xkeys, o_ap, o_key, stb, stbk, sm):
            P.op("act", lambda e: e.activation(
                    out=o_ap, in_=x_ap, func=AF.Identity,
                    scale=stb[:, 2 * sm:2 * sm + 1],
                    bias=stb[:, 2 * sm + 1:2 * sm + 2]),
                 r=tuple(xkeys) + (stbk,), w=(o_key,), tag="lns_apply")

        def transpose_pack(src_fn, srckeys, bank, bankkey, n=8):
            for j in range(n):
                P.op("pe", lambda e, j=j: e.matmul(
                        bank[:, j * 128:(j + 1) * 128], lhsT=src_fn(j),
                        rhs=id128[:], is_transpose=True,
                        start=(j == 0), stop=(j == n - 1)),
                     r=tuple(srckeys) + ("id128",), w=(bankkey,), tag="tpack")

        def repl_row(row_ap, rowkey, dst, dstkey, width, dve=False):
            nh = (width + 511) // 512
            for h in range(nh):
                wgt = min(512, width - h * 512)
                bank = pf[4 + (h % 2)]
                P.op("pe", lambda e, h=h, wgt=wgt, bank=bank: e.matmul(
                        bank[:, 0:wgt], lhsT=onesrow_f[:],
                        rhs=row_ap[0:1, h * 512:h * 512 + wgt],
                        start=True, stop=True),
                     r=(rowkey, "onesrow_f"), w=(f"PF{4 + h % 2}",), tag="repl_mm")
                eng = "dve" if dve else "act"
                if dve:
                    P.op("dve", lambda e, h=h, wgt=wgt, bank=bank: e.tensor_copy(
                            dst[:, h * 512:h * 512 + wgt], bank[:, 0:wgt]),
                         r=(f"PF{4 + h % 2}",), w=(dstkey,), tag="repl_cp")
                else:
                    P.op("act", lambda e, h=h, wgt=wgt, bank=bank: e.copy(
                            out=dst[:, h * 512:h * 512 + wgt], in_=bank[:, 0:wgt]),
                         r=(f"PF{4 + h % 2}",), w=(dstkey,), tag="repl_cp")

        abctx = ExitStack()
        absb = lambda name, shp, dt: abctx.enter_context(
            nc.sbuf_tensor(name, shp, dt))
        wq_sb = absb("wq_sb", [128, 8192], BF16)
        wk_sb = absb("wk_sb", [128, 8192], BF16)
        wv_sb = absb("wv_sb", [128, 8192], BF16)
        wo_sb = absb("wo_sb", [128, 8192], BF16)
        bqc_sb = absb("bqc_sb", [128, 8], F32)
        bkc_sb = absb("bkc_sb", [128, 8], F32)
        bvb = absb("bvb", [1, 1024], BF16)
        bob = absb("bob", [1, 1024], BF16)
        brlb = absb("brlb", [1, 512], BF16)
        bilb = absb("bilb", [1, 512], BF16)
        wrT = absb("wrT", [128, 2048], BF16)
        wiT = absb("wiT", [128, 2048], BF16)
        wiTn = absb("wiTn", [128, 2048], BF16)

        if "P0" in phases:
            p0ctx = ExitStack()
            p0sb = lambda name, shp, dt: p0ctx.enter_context(
                nc.sbuf_tensor(name, shp, dt))
            for nm, t in (("id128", id128), ("id64d", id64d), ("ut128", ut128),
                          ("ut128b", ut128b),
                          ("ones_bf", ones_bf), ("onesrow_f", onesrow_f),
                          ("onesrow_bf", onesrow_bf),
                          ("iota1_row", iota1_row), ("e127_f", e127_f),
                          ("e127_bf", e127_bf)):
                ldc(nm, t)
            P.dma("sp", lambda e: e.dma_start(out=eps_col[:], in_=DI["eps_col"]),
                  w=("epsc",), tag="ld_eps")
            for nm, t in (("wqT", wq_sb), ("wkT", wk_sb), ("wvT", wv_sb),
                          ("woT", wo_sb)):
                P.dma("pool", lambda e, t=t, d=DI[nm]: e.dma_start(
                        out=t[:].rearrange("p (j o) -> p j o", j=8),
                        in_=d.rearrange("(j p) o -> p j o", p=128)),
                      w=(nm,), tag=f"ld_{nm}")
            ldc("bq_c", bqc_sb)
            ldc("bk_c", bkc_sb)
            ldc("bv_row", bvb, cast=True)
            ldc("bo_row", bob, cast=True)
            ldc("brl_row", brlb, cast=True)
            ldc("bil_row", bilb, cast=True)

            art_sb = p0sb("art_sb", [32, 128], F32)
            brt_sb = p0sb("brt_sb", [16, 64], F32)
            ait_sb = p0sb("ait_sb", [32, 128], F32)
            bit_sb = p0sb("bit_sb", [16, 64], F32)
            ecq_sb = p0sb("ecq_sb", [32, 512], F32)
            ecg_sb = p0sb("ecg_sb", [16, 128], F32)
            for nm, t, pat in (("a_rt", art_sb, "c d b -> d c b"),
                               ("a_it", ait_sb, "c d b -> d c b"),
                               ("b_rt", brt_sb, "c g e -> g c e"),
                               ("b_it", bit_sb, "c g e -> g c e")):
                P.dma("sp", lambda e, t=t, d=DI[nm], pat=pat, nm=nm: e.dma_start(
                        out=t[:].rearrange("p (c x) -> p c x", c=4),
                        in_=d.rearrange(pat)),
                      w=(nm,), tag=f"ld_{nm}")
            ldc("ecq", ecq_sb)
            ldc("ecg", ecg_sb)
            wtmpA = p0sb("wtmpA", [128, 32], F32)
            wtmpB = p0sb("wtmpB", [128, 16], F32)
            wtmpM = p0sb("wtmpM", [128, 512], F32)
            wacc = p0sb("wacc", [128, 512], F32)

            def wbuild(a_sb, b_sb, akey, bkey, dstT, dstTn, dkey, dnkey):
                for q in range(4):
                    for cc in range(4):
                        bA = pf[2 * (cc % 2)]
                        bAk = f"PF{2 * (cc % 2)}"
                        bB = pf[1 + 2 * (cc % 2)]
                        bBk = f"PF{1 + 2 * (cc % 2)}"
                        P.op("pe", lambda e, q=q, cc=cc, bA=bA: e.matmul(
                                bA[:, 0:32], lhsT=ecq_sb[:, q * 128:(q + 1) * 128],
                                rhs=a_sb[:, cc * 32:(cc + 1) * 32],
                                start=True, stop=True),
                             r=(akey, "ecq"), w=(bAk), tag="wb_A")
                        P.op("pe", lambda e, cc=cc, bB=bB: e.matmul(
                                bB[:, 0:16], lhsT=ecg_sb[:],
                                rhs=b_sb[:, cc * 16:(cc + 1) * 16],
                                start=True, stop=True),
                             r=(bkey, "ecg"), w=(bBk), tag="wb_B")
                        P.op("act", lambda e, bA=bA: e.copy(
                                out=wtmpA[:], in_=bA[:, 0:32]),
                             r=(bAk), w=("wb_A"), tag="wb_Aev")
                        P.op("dve", lambda e, bB=bB: e.tensor_copy(
                                wtmpB[:], bB[:, 0:16]),
                             r=(bBk), w=("wb_B"), tag="wb_Bev")
                        avb = wtmpA[:].rearrange("p (b one) -> p b one",
                                                 one=1).broadcast_to([128, 32, 16])
                        bvb = wtmpB[:].rearrange("p (one e) -> p one e",
                                                 one=1).broadcast_to([128, 32, 16])
                        if cc == 0:
                            ov = wacc[:].rearrange("p (b e) -> p b e", e=16)
                            P.op("dve", lambda e, avb=avb, bvb=bvb, ov=ov:
                                 e.tensor_tensor(out=ov, in0=avb, in1=bvb,
                                                 op=ALU.mult),
                                 r=("wb_A", "wb_B"), w=("wb_acc"),
                                 tag="wb_mul")
                        else:
                            mv = wtmpM[:].rearrange("p (b e) -> p b e", e=16)
                            P.op("dve", lambda e, avb=avb, bvb=bvb, mv=mv:
                                 e.tensor_tensor(out=mv, in0=avb, in1=bvb,
                                                 op=ALU.mult),
                                 r=("wb_A", "wb_B"), w=("wb_M"),
                                 tag="wb_mul")
                            P.op("pool", lambda e: e.tensor_tensor(
                                    out=wacc[:], in0=wacc[:], in1=wtmpM[:],
                                    op=ALU.add),
                                 r=("wb_M"), w=("wb_acc"), tag="wb_add")
                    P.op("dve", lambda e, q=q: e.tensor_copy(
                            dstT[:, q * 512:(q + 1) * 512], wacc[:]),
                         r=("wb_acc"), w=(dkey), tag="wb_cast")
                    if dstTn is not None:
                        P.op("act", lambda e, q=q: e.mul(
                                out=dstTn[:, q * 512:(q + 1) * 512], in_=wacc[:],
                                mul=-1.0),
                             r=("wb_acc"), w=(dnkey), tag="wb_castn")

            wbuild(art_sb, brt_sb, "a_rt", "b_rt", wrT, None, "wrT", None)
            wbuild(ait_sb, bit_sb, "a_it", "b_it", wiT, wiTn, "wiT", "wiTn")
            dbgdump(0, wrT[:], ("wrT",), 2048, always=True)
            dbgdump(2048, wiTn[:], ("wiTn",), 2048, always=True)
            P.flush("P0")
            p0ctx.close()
            P.barrier()

        if "AB" in phases:
            x_sb = [absb(f"xA{i}", [128, 4096], F32) for i in range(2)]
            nbf2 = [absb(f"nbf{i}", [128, 1024], BF16) for i in range(2)]
            stAB = absb("stAB", [128, 8], F32)
            n1T = absb("n1T", [128, 4096], BF16)
            q_sb = absb("q_sb", [128, 4096], BF16)
            k_sb = absb("k_sb", [128, 4096], BF16)
            v_sb = absb("v_sb", [128, 4096], BF16)
            attn_bf = [absb(f"attnbf{i}", [128, 1024], BF16) for i in range(4)]
            attnT = absb("attnT", [128, 1024], BF16)
            y_sb = absb("yA", [128, 1024], F32)
            n3T = absb("n3T", [128, 1024], BF16)
            f_sb = absb("fA", [128, 1024], F32)
            S_sb = absb("S_sb", [128, 520], F32)
            S_bf = absb("S_bf", [128, 520], BF16)
            rden = absb("rden", [128, 1], F32)
            ktp_sb = absb("ktp_sb", [128, 128], BF16)

            P.op("dve", lambda e: e.memset(S_sb[:], 0.0), w=("S_sb",), tag="S0",
                 swb=True)
            P.op("dve", lambda e: e.memset(S_bf[:], 0.0), w=("S_bf",), tag="Sbf0",
                 swb=True)

            def ab_attn_stage(c, m):
                par = c % 2
                apar = m % 4
                abf = attn_bf[apar]
                for jp in range(8):
                    for pi in range(2):
                        b0 = pi * 64
                        h = 2 * jp + pi
                        qh = q_sb[b0:b0 + 64,
                                  jp * 512 + m * 128:jp * 512 + m * 128 + 128]
                        kh = k_sb[b0:b0 + 64,
                                  jp * 512 + m * 128:jp * 512 + m * 128 + 128]
                        vh = v_sb[:, m * 1024 + h * 64:m * 1024 + h * 64 + 64]
                        sc_bank = pf[pi]
                        sck = f"PF{pi}"
                        nd_bank = pf[2 + pi]
                        ndk = f"PF{2 + pi}"
                        P.op("pe", lambda e, kh=kh, qh=qh, sc_bank=sc_bank:
                             e.matmul(sc_bank[:, 0:128], lhsT=kh, rhs=qh,
                                      start=True, stop=True),
                             r=("q_sb", "k_sb"), w=(sck,), tag="scT")
                        P.op("dve", lambda e, sc_bank=sc_bank, pi=pi:
                             e.tensor_tensor(out=mtb[pi][:],
                                             in0=sc_bank[:, 0:128],
                                             in1=ut128[:], op=ALU.mult),
                             r=(sck, "ut128"), w=(f"mtb{pi}",), tag="mask")
                        P.op("pe", lambda e, pi=pi, vh=vh, nd_bank=nd_bank:
                             e.matmul(nd_bank[:, 0:64], lhsT=mtb[pi][:],
                                      rhs=vh, start=True, stop=False),
                             r=(f"mtb{pi}", "v_sb"), w=(ndk,), tag="num1")
                        P.op("pe", lambda e, pi=pi, nd_bank=nd_bank:
                             e.matmul(nd_bank[:, 64:65], lhsT=mtb[pi][:],
                                      rhs=ones_bf[:], start=False, stop=False),
                             r=(f"mtb{pi}", "ones_bf"), w=(ndk,), tag="den1")
                        P.op("pe", lambda e, qh=qh, jp=jp, b0=b0,
                             nd_bank=nd_bank: e.matmul(
                                nd_bank[:, 0:64], lhsT=qh,
                                rhs=S_bf[b0:b0 + 64, jp * 65:jp * 65 + 64],
                                start=False, stop=False),
                             r=("q_sb", "S_bf"), w=(ndk,), tag="num2")
                        P.op("pe", lambda e, qh=qh, jp=jp, b0=b0,
                             nd_bank=nd_bank: e.matmul(
                                nd_bank[:, 64:65], lhsT=qh,
                                rhs=S_bf[b0:b0 + 64, jp * 65 + 64:jp * 65 + 65],
                                start=False, stop=True),
                             r=("q_sb", "S_bf"), w=(ndk,), tag="den2")
                        P.op("pe", lambda e, kh=kh, b0=b0, pi=pi: e.matmul(
                                pb[1][:, pi * 64:pi * 64 + 64], lhsT=kh,
                                rhs=id64d[b0:b0 + 64, :], is_transpose=True,
                                start=(pi == 0), stop=(pi == 1)),
                             r=("k_sb", "id64d"), w=("PB1",), tag="kT")
                        P.op("dve", lambda e, nd_bank=nd_bank: e.reciprocal(
                                out=rden[:], in_=nd_bank[:, 64:65]),
                             r=(ndk,), w=("rden",), tag="rden", swb=True)
                        P.op("dve", lambda e, abf=abf, h=h, nd_bank=nd_bank,
                             apar=apar: e.tensor_scalar(
                                out=abf[:, h * 64:h * 64 + 64],
                                in0=nd_bank[:, 0:64], scalar1=rden[:],
                                scalar2=None, op0=ALU.mult),
                             r=(ndk, "rden"), w=(f"attnbf{apar}",), tag="attno",
                             early=True)
                    P.op("act", lambda e: e.copy(out=ktp_sb[:],
                                                 in_=pb[1][:, 0:128]),
                         r=("PB1",), w=("ktp",), tag="ktp_ev")
                    for pi in range(2):
                        b0 = pi * 64
                        h = 2 * jp + pi
                        vh = v_sb[:, m * 1024 + h * 64:m * 1024 + h * 64 + 64]
                        P.op("pe", lambda e, pi=pi, b0=b0, vh=vh: e.matmul(
                                pf[4][b0:b0 + 64, 0:64],
                                lhsT=ktp_sb[:, pi * 64:pi * 64 + 64], rhs=vh,
                                start=True, stop=False,
                                tile_position=(0, b0)),
                             r=("ktp", "v_sb"), w=("PF4",), tag="dS")
                        P.op("pe", lambda e, pi=pi, b0=b0: e.matmul(
                                pf[4][b0:b0 + 64, 64:65],
                                lhsT=ktp_sb[:, pi * 64:pi * 64 + 64],
                                rhs=ones_bf[:], start=False, stop=(pi == 1),
                                tile_position=(0, b0)),
                             r=("ktp", "ones_bf"), w=("PF4",), tag="dz")
                    P.op("dve", lambda e, jp=jp: e.tensor_tensor(
                            out=S_bf[:, jp * 65:(jp + 1) * 65],
                            in0=S_sb[:, jp * 65:(jp + 1) * 65],
                            in1=pf[4][:, 0:65], op=ALU.add),
                         r=("S_sb", "PF4"), w=("S_bf",), tag="Snew")
                    P.op("dve", lambda e, jp=jp: e.tensor_tensor(
                            out=S_sb[:, jp * 65:(jp + 1) * 65],
                            in0=S_sb[:, jp * 65:(jp + 1) * 65],
                            in1=pf[4][:, 0:65], op=ALU.add),
                         r=("S_sb", "PF4"), w=("S_sb",), tag="Supd")

            def ab_post_stage(c, m):
                par = c % 2
                xk = f"xA{par}"
                xt = x_sb[par]
                apar = m % 4
                abf = attn_bf[apar]
                xv = xt[:, m * 1024:(m + 1) * 1024]
                transpose_pack(lambda j, abf=abf: abf[:, j * 128:(j + 1) * 128],
                               (f"attnbf{apar}",), pb[0], "PB0")
                P.op("act", lambda e: e.copy(out=attnT[:], in_=pb[0][:]),
                     r=("PB0",), w=("attnT",), tag="attnT_ev")
                for hf in range(2):
                    for j in range(8):
                        P.op("pe", lambda e, j=j, hf=hf:
                             e.matmul(pf[5][:],
                                      lhsT=attnT[:, j * 128:(j + 1) * 128],
                                      rhs=wo_sb[:, j * 1024 + hf * 512:
                                                j * 1024 + hf * 512 + 512],
                                      start=(j == 0), stop=False),
                             r=("attnT", "woT"), w=("PF5",), tag="omm")
                    P.op("pe", lambda e, hf=hf: e.matmul(
                            pf[5][:], lhsT=onesrow_bf[:],
                            rhs=bob[0:1, hf * 512:hf * 512 + 512],
                            start=False, stop=True),
                         r=("bo_row", "onesrow_bf"), w=("PF5",), tag="obmm")
                    P.op("act", lambda e, hf=hf: e.copy(
                            out=y_sb[:, hf * 512:(hf + 1) * 512], in_=pf[5][:]),
                         r=("PF5",), w=("yA",), tag="oev")
                ln(y_sb[:], ("yA",), [(lnscr2[:], "lnscr2")])
                P.op("dve", lambda e, xv=xv: e.tensor_tensor(
                        out=xv, in0=xv, in1=lnscr2[:], op=ALU.add),
                     r=("lnscr2",), w=(xk,), tag="xadd1")
                nb = nbf2[m % 2]
                nbk = f"nbf{m % 2}"
                ln(xv, (xk,), [(nb[:], nbk)])
                transpose_pack(lambda j, nb=nb: nb[:, j * 128:(j + 1) * 128],
                               (nbk,), pb[0], "PB0")
                P.op("act", lambda e: e.copy(out=n3T[:], in_=pb[0][:]),
                     r=("PB0",), w=("n3T",), tag="n3T_ev")
                for half, w_lo, wk_lo, w_hi, wk_hi, bbr, brk in (
                        (0, wrT, "wrT", wiTn, "wiTn", brlb, "brl_row"),
                        (1, wiT, "wiT", wrT, "wrT", bilb, "bil_row")):
                    for j in range(8):
                        wsb = w_lo if j < 4 else w_hi
                        wky = wk_lo if j < 4 else wk_hi
                        qq = j if j < 4 else j - 4
                        P.op("pe", lambda e, j=j, qq=qq, wsb=wsb:
                             e.matmul(pf[5][:],
                                      lhsT=n3T[:, j * 128:(j + 1) * 128],
                                      rhs=wsb[:, qq * 512:(qq + 1) * 512],
                                      start=(j == 0), stop=False),
                             r=("n3T", wky), w=("PF5",), tag="bmm")
                    P.op("pe", lambda e, bbr=bbr: e.matmul(
                            pf[5][:], lhsT=onesrow_bf[:], rhs=bbr[0:1, :],
                            start=False, stop=True),
                         r=(brk, "onesrow_bf"), w=("PF5",), tag="bbmm")
                    P.op("act", lambda e, half=half: e.copy(
                            out=f_sb[:, half * 512:(half + 1) * 512],
                            in_=pf[5][:]),
                         r=("PF5",), w=("fA",), tag="bev")
                ln(f_sb[:], ("fA",), [(lnscr2[:], "lnscr2")])
                P.op("dve", lambda e, xv=xv: e.tensor_tensor(
                        out=xv, in0=xv, in1=lnscr2[:], op=ALU.add),
                     r=("lnscr2",), w=(xk,), tag="xadd2")

            def ab_store(c):
                par = c % 2
                xk = f"xA{par}"
                xt = x_sb[par]
                dst = xst_d[c * CH:(c + 1) * CH, :].rearrange(
                    "(m p) d -> p m d", p=128)
                P.dma("act", lambda e, xt=xt, dst=dst: e.dma_start(
                        out=dst, in_=xt[:].rearrange("p (m d) -> p m d", m=MT)),
                      r=(xk,), w=(f"xst{c}",), tag=f"stxA{c}")

            for c in range(NCH):
                par = c % 2
                xk = f"xA{par}"
                xt = x_sb[par]
                src = DI["x0"][c * CH:(c + 1) * CH, :].rearrange(
                    "(m p) d -> p m d", p=128)
                P.dma("sp", lambda e, xt=xt, src=src: e.dma_start(
                        out=xt[:].rearrange("p (m d) -> p m d", m=MT), in_=src),
                      r=("x0_dram",), w=(xk,), tag=f"ldxA{c}")
                for m in range(MT):
                    xv = xt[:, m * 1024:(m + 1) * 1024]
                    ln_stats(xv, (xk,), stAB, "stAB", m)
                for m in range(MT):
                    xv = xt[:, m * 1024:(m + 1) * 1024]
                    nb = nbf2[m % 2]
                    nbk = f"nbf{m % 2}"
                    ln_apply(xv, (xk,), nb[:], nbk, stAB, "stAB", m)
                    tb = pb[m % 2]
                    transpose_pack(lambda j, nb=nb: nb[:, j * 128:(j + 1) * 128],
                                   (nbk,), tb, f"PB{m % 2}")
                    dv = n1T[:].rearrange("p (j t) -> p j t", j=8)[
                        :, :, m * 128:(m + 1) * 128]
                    sv = tb[:].rearrange("p (j t) -> p j t", j=8)
                    P.op("act", lambda e, dv=dv, sv=sv: e.copy(out=dv, in_=sv),
                         r=(f"PB{m % 2}",), w=("n1T",), tag="n1T_ev")
                for nm, wsb, wkey, bsb, bkey, dst, dkey in (
                        ("q", wq_sb, "wqT", bqc_sb, "bq_c", q_sb, "q_sb"),
                        ("k", wk_sb, "wkT", bkc_sb, "bk_c", k_sb, "k_sb")):
                    for jo in range(8):
                        bank = pf[jo % 2]
                        bk_key = f"PF{jo % 2}"
                        for ji in range(8):
                            P.op("pe", lambda e, jo=jo, ji=ji, wsb=wsb, bank=bank:
                                 e.matmul(bank[:],
                                          lhsT=wsb[:, ji * 1024 + jo * 128:
                                                   ji * 1024 + jo * 128 + 128],
                                          rhs=n1T[:, ji * 512:(ji + 1) * 512],
                                          start=(ji == 0), stop=(ji == 7)),
                                 r=(wkey, "n1T"), w=(bk_key,), tag=f"{nm}mm")
                        P.op("act", lambda e, jo=jo, bsb=bsb, dst=dst, bank=bank:
                             e.activation(out=dst[:, jo * 512:(jo + 1) * 512],
                                          in_=bank[:], func=AF.Exp,
                                          bias=bsb[:, jo:jo + 1], scale=1.0),
                             r=(bk_key, bkey), w=(dkey,), tag=f"{nm}exp")
                for m in range(MT):
                    for hf in range(2):
                        bank = pf[2 + hf]
                        bk_key = f"PF{2 + hf}"
                        for ji in range(8):
                            P.op("pe", lambda e, m=m, hf=hf, ji=ji, bank=bank:
                                 e.matmul(bank[:],
                                          lhsT=n1T[:, ji * 512 + m * 128:
                                                   ji * 512 + m * 128 + 128],
                                          rhs=wv_sb[:, ji * 1024 + hf * 512:
                                                    ji * 1024 + hf * 512 + 512],
                                          start=(ji == 0), stop=False),
                                 r=("wvT", "n1T"), w=(bk_key,), tag="vmm")
                        P.op("pe", lambda e, hf=hf, bank=bank: e.matmul(
                                bank[:], lhsT=onesrow_bf[:],
                                rhs=bvb[0:1, hf * 512:hf * 512 + 512],
                                start=False, stop=True),
                             r=("bv_row", "onesrow_bf"), w=(bk_key,), tag="vbmm")
                        P.op("act", lambda e, m=m, hf=hf, bank=bank: e.copy(
                                out=v_sb[:, m * 1024 + hf * 512:
                                         m * 1024 + hf * 512 + 512],
                                in_=bank[:]),
                             r=(bk_key,), w=("v_sb",), tag="vev")
                ab_attn_stage(c, 0)
                if c > 0:
                    ab_post_stage(c - 1, 2)
                ab_attn_stage(c, 1)
                if c > 0:
                    ab_post_stage(c - 1, 3)
                    ab_store(c - 1)
                ab_attn_stage(c, 2)
                ab_post_stage(c, 0)
                ab_attn_stage(c, 3)
                ab_post_stage(c, 1)
            ab_post_stage(NCH - 1, 2)
            ab_post_stage(NCH - 1, 3)
            ab_store(NCH - 1)
            P.flush("AB")
            abctx.close()
            P.barrier()

        if "C" in phases:
            cctx = ExitStack()
            csb = lambda name, shp, dt: cctx.enter_context(
                nc.sbuf_tensor(name, shp, dt))
            bre_sb = csb("bre_sb", [128, 8192], BF16)
            bim_sb = csb("bim_sb", [128, 8192], BF16)
            cre_sb = csb("cre_sb", [128, 8192], BF16)
            cim_sb = csb("cim_sb", [128, 8192], BF16)
            lp_re = csb("lp_re", [128, 1024], F32)
            lp_im = csb("lp_im", [128, 1024], F32)
            li_re = csb("li_re", [128, 1024], F32)
            li_imN = csb("li_imN", [128, 1024], F32)
            drep = csb("drep", [128, 1024], F32)

            pctx = ExitStack()
            psb = lambda name, shp, dt: pctx.enter_context(
                nc.sbuf_tensor(name, shp, dt))
            nulog_sb = psb("nulog_sb", [1, 1024], F32)
            thlog_sb = psb("thlog_sb", [1, 1024], F32)
            dskip_sb = psb("dskip_sb", [1, 1024], F32)
            nu_row = psb("nu_row", [1, 1024], F32)
            th_row = psb("th_row", [1, 1024], F32)
            g_row = psb("g_row", [1, 1024], F32)
            lpm_sb = psb("lpm_sb", [128, 1024], F32)
            lim_sb = psb("lim_sb", [128, 1024], F32)
            sin_sb = psb("sin_sb", [128, 1024], F32)
            cos_sb = psb("cos_sb", [128, 1024], F32)
            cscrA = psb("cscrA", [128, 512], F32)
            cscrB = psb("cscrB", [128, 512], F32)
            grep = psb("grep", [128, 1024], F32)
            bstage = psb("bstage", [128, 2048], F32)
            uscr = psb("uscr_prep", [128, 1024], F32)

            ldc("nulog_row", nulog_sb)
            ldc("thetalog_row", thlog_sb)
            ldc("dskip_row", dskip_sb)
            P.op("act", lambda e: e.activation(out=nu_row[:], in_=nulog_sb[:],
                                               func=AF.Exp),
                 r=("nulog_row",), w=("nu_row",), tag="nu")
            P.op("act", lambda e: e.activation(out=th_row[:], in_=thlog_sb[:],
                                               func=AF.Exp),
                 r=("thetalog_row",), w=("th_row",), tag="th")
            for h2 in range(2):
                c0, c1 = h2 * 512, (h2 + 1) * 512
                P.op("pe", lambda e, c0=c0, c1=c1: e.matmul(
                        pf[0][:], lhsT=iota1_row[:], rhs=nu_row[0:1, c0:c1],
                        start=True, stop=True),
                     r=("nu_row", "iota1_row"), w=("PF0",), tag="tpn")
                P.op("act", lambda e, c0=c0, c1=c1: e.activation(
                        out=lpm_sb[:, c0:c1], in_=pf[0][:], func=AF.Exp,
                        scale=-1.0),
                     r=("PF0",), w=("lpm",), tag="lpmag")
                P.op("act", lambda e, c0=c0, c1=c1: e.activation(
                        out=lim_sb[:, c0:c1], in_=pf[0][:], func=AF.Exp,
                        scale=1.0),
                     r=("PF0",), w=("lim",), tag="limag")
                P.op("pe", lambda e, c0=c0, c1=c1: e.matmul(
                        pf[1][:], lhsT=iota1_row[:], rhs=th_row[0:1, c0:c1],
                        start=True, stop=True),
                     r=("th_row", "iota1_row"), w=("PF1",), tag="tpt")
                P.op("dve", lambda e: e.tensor_scalar(
                        out=cscrA[:], in0=pf[1][:], scalar1=PIF, scalar2=None,
                        op0=ALU.is_gt),
                     r=("PF1",), w=("cscrA",), tag="phmask")
                P.op("dve", lambda e: e.scalar_tensor_tensor(
                        out=cscrB[:], in0=cscrA[:], scalar=-2.0 * PIF,
                        in1=pf[1][:], op0=ALU.mult, op1=ALU.add),
                     r=("cscrA", "PF1"), w=("cscrB",), tag="phred")
                P.op("act", lambda e, c0=c0, c1=c1: e.activation(
                        out=sin_sb[:, c0:c1], in_=cscrB[:], func=AF.Sin),
                     r=("cscrB",), w=("sin",), tag="sin")
                P.op("act", lambda e: e.activation(
                        out=cscrA[:], in_=cscrB[:], func=AF.Sin, scale=0.5),
                     r=("cscrB",), w=("cscrA",), tag="sinh")
                P.op("act", lambda e: e.activation(
                        out=cscrA[:], in_=cscrA[:], func=AF.Square),
                     r=("cscrA",), w=("cscrA",), tag="sq")
                P.op("dve", lambda e, c0=c0, c1=c1: e.tensor_scalar(
                        out=cos_sb[:, c0:c1], in0=cscrA[:], scalar1=-2.0,
                        scalar2=1.0, op0=ALU.mult, op1=ALU.add),
                     r=("cscrA",), w=("cos",), tag="cos")
            P.op("dve", lambda e: e.tensor_tensor(out=lp_re[:], in0=lpm_sb[:],
                                                  in1=cos_sb[:], op=ALU.mult),
                 r=("lpm", "cos"), w=("lp_re",), tag="lpre")
            P.op("dve", lambda e: e.tensor_tensor(out=lp_im[:], in0=lpm_sb[:],
                                                  in1=sin_sb[:], op=ALU.mult),
                 r=("lpm", "sin"), w=("lp_im",), tag="lpim")
            P.op("dve", lambda e: e.tensor_tensor(out=li_re[:], in0=lim_sb[:],
                                                  in1=cos_sb[:], op=ALU.mult),
                 r=("lim", "cos"), w=("li_re",), tag="lire")
            P.op("dve", lambda e: e.tensor_tensor(out=li_imN[:], in0=lim_sb[:],
                                                  in1=sin_sb[:], op=ALU.mult),
                 r=("lim", "sin"), w=("li_imN",), tag="liimN")
            dbgdump(10240, lp_re[:], ("lp_re",), 1024, always=True)
            dbgdump(11264, li_re[:], ("li_re",), 1024, always=True)
            dbgdump(12288, lp_im[:], ("lp_im",), 1024, always=True)
            dbgdump(13312, li_imN[:], ("li_imN",), 1024, always=True)
            dbgdump(14336, grep[:], ("grep",), 1024, always=True)
            dbgdump(15360, drep[:], ("drep",), 1024, always=True)
            dbgdump(4096, bre_sb[:, 0:2048], ("bre",), 2048, always=True)
            dbgdump(6144, cre_sb[:, 0:2048], ("cre",), 2048, always=True)
            if DBG2:
                P.dma("pool", lambda e: e.dma_start(out=dbg_d[0:1, 9216:10240],
                                                    in_=g_row[:]),
                      r=("g_row",), w=("dbg_dram",), tag="dbg_grow")
                P.dma("pool", lambda e: e.dma_start(out=dbg_d[1:2, 9216:10240],
                                                    in_=dskip_sb[:]),
                      r=("dskip_row",), w=("dbg_dram",), tag="dbg_drow")
            P.op("act", lambda e: e.activation(out=g_row[:], in_=nu_row[:],
                                               func=AF.Exp, scale=-2.0),
                 r=("nu_row",), w=("g_row",), tag="e2")
            P.op("dve", lambda e: e.tensor_scalar(out=g_row[:], in0=g_row[:],
                                                  scalar1=-1.0, scalar2=1.0 + EPS,
                                                  op0=ALU.mult, op1=ALU.add),
                 r=("g_row",), w=("g_row",), tag="g1m")
            P.op("act", lambda e: e.activation(out=g_row[:], in_=g_row[:],
                                               func=AF.Ln),
                 r=("g_row",), w=("g_row",), tag="glng")
            P.op("act", lambda e: e.activation(out=g_row[:], in_=g_row[:],
                                               func=AF.Exp, scale=0.5),
                 r=("g_row",), w=("g_row",), tag="gexp")
            repl_row(g_row[:], "g_row", grep, "grep", 1024)
            repl_row(dskip_sb[:], "dskip_row", drep, "drep", 1024)
            for wnm, dst, dkey in (("breT", bre_sb, "bre"), ("bimT", bim_sb, "bim")):
                for jq in range(4):
                    P.dma("sp", lambda e, d=DI[wnm], jq=jq: e.dma_start(
                            out=bstage[:].rearrange("p (j o) -> p j o", j=2),
                            in_=d[jq * 256:(jq + 1) * 256, :].rearrange(
                                "(j p) o -> p j o", p=128)),
                          w=("bstage",), tag=f"ld_{wnm}{jq}")
                    for j2 in range(2):
                        j = jq * 2 + j2
                        P.op("dve", lambda e, j=j, j2=j2, dst=dst: e.tensor_tensor(
                                out=dst[:, j * 1024:(j + 1) * 1024],
                                in0=bstage[:, j2 * 1024:(j2 + 1) * 1024],
                                in1=grep[:], op=ALU.mult),
                             r=("bstage", "grep"), w=(dkey,), tag=f"scale_{wnm}")
            for wnm, dst, dkey in (("creT", cre_sb, "cre"), ("cimT", cim_sb, "cim")):
                P.dma("pool", lambda e, d=DI[wnm], dst=dst: e.dma_start(
                        out=dst[:].rearrange("p (j o) -> p j o", j=8),
                        in_=d.rearrange("(j p) o -> p j o", p=128)),
                      w=(dkey,), tag=f"ld_{wnm}")
            P.flush("Cprep")
            pctx.close()
            P.barrier()

            xC = [csb(f"xC{i}", [128, 4096], F32) for i in range(2)]
            nbfC2 = [csb(f"nbfC{i}", [128, 1024], BF16) for i in range(2)]
            n1TC = [csb(f"n1TC{i}", [128, 4096], BF16) for i in range(2)]
            lnstC = [csb(f"lnstC{i}", [128, 8], F32) for i in range(2)]
            uA_bf = csb("uA_bf", [128, 2048], BF16)
            uB_bf = csb("uB_bf", [128, 2048], BF16)
            skscr = csb("skscr", [128, 1024], F32)
            sksc2 = csb("sksc2", [128, 1024], F32)
            ct1 = csb("ct1", [128, 512], F32)
            ct2 = csb("ct2", [128, 512], F32)
            hs_bf = [csb(f"hs_bf{i}", [128, 2048], BF16) for i in range(2)]
            hsT = csb("hsT", [128, 2048], BF16)
            carry = csb("carry", [1, 2048], BF16)
            yC = csb("yC", [128, 1024], F32)
            P.op("dve", lambda e: e.memset(carry[:], 0.0), w=("carry",),
                 tag="carry0", swb=True)
            P.op("dve", lambda e: e.memset(hs_bf[0][:], 0.0), w=("hs_bf0",),
                 tag="hs0a", swb=True)
            P.op("dve", lambda e: e.memset(hs_bf[1][:], 0.0), w=("hs_bf1",),
                 tag="hs0b", swb=True)

            def c_scan_stage(c, m):
                par = c % 2
                ntk = f"n1TC{par}"
                mpar = m % 2
                hk = f"hs_bf{mpar}"
                hsb = hs_bf[mpar]
                for ri, wsb, wky in ((0, bre_sb, "bre"), (1, bim_sb, "bim")):
                    for h2 in range(2):
                        bank = pf[2 * ri + h2]
                        bkk = f"PF{2 * ri + h2}"
                        for ji in range(8):
                            P.op("pe", lambda e, ji=ji, h2=h2, m=m, wsb=wsb,
                                 bank=bank, par=par: e.matmul(
                                    bank[:],
                                    lhsT=n1TC[par][:, m * 1024 + ji * 128:
                                                   m * 1024 + ji * 128 + 128],
                                    rhs=wsb[:, ji * 1024 + h2 * 512:
                                            ji * 1024 + h2 * 512 + 512],
                                    start=(ji == 0), stop=(ji == 7)),
                                 r=(ntk, wky), w=(bkk,), tag="bumm")
                for h2 in range(2):
                    c0, c1 = h2 * 512, (h2 + 1) * 512
                    bre_ps, bim_ps = pf[h2], pf[2 + h2]
                    kre, kim = f"PF{h2}", f"PF{2 + h2}"
                    P.op("dve", lambda e, c0=c0, c1=c1, bre_ps=bre_ps:
                         e.tensor_tensor(out=uA_bf[:, c0:c1],
                                         in0=li_re[:, c0:c1],
                                         in1=bre_ps[:], op=ALU.mult),
                         r=("li_re", kre), w=("uA_bf",), tag="u1")
                    P.op("dve", lambda e, c0=c0, c1=c1, bim_ps=bim_ps:
                         e.tensor_tensor(out=uB_bf[:, c0:c1],
                                         in0=li_imN[:, c0:c1],
                                         in1=bim_ps[:], op=ALU.mult),
                         r=("li_imN", kim), w=("uB_bf",), tag="u2")
                    P.op("dve", lambda e, c0=c0, c1=c1, bim_ps=bim_ps:
                         e.scalar_tensor_tensor(
                            out=uA_bf[:, 1024 + c0:1024 + c1], in0=bim_ps[:],
                            scalar=-1.0,
                            in1=li_re[:, c0:c1], op0=ALU.mult, op1=ALU.mult),
                         r=("li_re", kim), w=("uA_bf",), tag="u4")
                    P.op("dve", lambda e, c0=c0, c1=c1, bre_ps=bre_ps:
                         e.tensor_tensor(out=uB_bf[:, 1024 + c0:1024 + c1],
                                         in0=li_imN[:, c0:c1],
                                         in1=bre_ps[:], op=ALU.mult),
                         r=("li_imN", kre), w=("uB_bf",), tag="u5")
                for seg in range(4):
                    bank = pf[seg]
                    bkk = f"PF{seg}"
                    P.op("pe", lambda e, seg=seg, bank=bank: e.matmul(
                            bank[:], lhsT=ut128b[:],
                            rhs=uA_bf[:, seg * 512:(seg + 1) * 512],
                            start=True, stop=False),
                         r=("uA_bf", "ut128b"), w=(bkk,), tag="cumsA")
                    P.op("pe", lambda e, seg=seg, bank=bank: e.matmul(
                            bank[:], lhsT=ut128b[:],
                            rhs=uB_bf[:, seg * 512:(seg + 1) * 512],
                            start=False, stop=False),
                         r=("uB_bf", "ut128b"), w=(bkk,), tag="cumsB")
                    P.op("pe", lambda e, seg=seg, bank=bank: e.matmul(
                            bank[:], lhsT=onesrow_bf[:],
                            rhs=carry[0:1, seg * 512:(seg + 1) * 512],
                            start=False, stop=True),
                         r=("carry", "onesrow_bf"), w=(bkk,), tag="carrymm")
                for h2 in range(2):
                    c0, c1 = h2 * 512, (h2 + 1) * 512
                    sre, sim = pf[h2], pf[2 + h2]
                    kre, kim = f"PF{h2}", f"PF{2 + h2}"
                    P.op("dve", lambda e, c0=c0, c1=c1, sre=sre: e.tensor_tensor(
                            out=ct1[:], in0=lp_re[:, c0:c1], in1=sre[:],
                            op=ALU.mult),
                         r=("lp_re", kre), w=("ct1",), tag="hs1")
                    P.op("dve", lambda e, c0=c0, c1=c1, sim=sim: e.tensor_tensor(
                            out=ct2[:], in0=lp_im[:, c0:c1], in1=sim[:],
                            op=ALU.mult),
                         r=("lp_im", kim), w=("ct2",), tag="hs2")
                    P.op("pool", lambda e, c0=c0, c1=c1, hsb=hsb: e.tensor_tensor(
                            out=hsb[:, c0:c1], in0=ct1[:], in1=ct2[:],
                            op=ALU.add),
                         r=("ct1", "ct2"), w=(hk,), tag="hs3")
                    P.op("dve", lambda e, c0=c0, c1=c1, sim=sim: e.tensor_tensor(
                            out=ct1[:], in0=lp_re[:, c0:c1], in1=sim[:],
                            op=ALU.mult),
                         r=("lp_re", kim), w=("ct1",), tag="hs4")
                    P.op("dve", lambda e, c0=c0, c1=c1, sre=sre: e.scalar_tensor_tensor(
                            out=ct2[:], in0=sre[:], scalar=-1.0,
                            in1=lp_im[:, c0:c1], op0=ALU.mult, op1=ALU.mult),
                         r=("lp_im", kre), w=("ct2",), tag="hs5")
                    P.op("pool", lambda e, c0=c0, c1=c1, hsb=hsb: e.tensor_tensor(
                            out=hsb[:, 1024 + c0:1024 + c1], in0=ct1[:],
                            in1=ct2[:], op=ALU.add),
                         r=("ct1", "ct2"), w=(hk,), tag="hs6")
                    for seg in (h2, 2 + h2):
                        bank = pf[seg]
                        bkk = f"PF{seg}"
                        P.op("pe", lambda e, seg=seg, bank=bank, hsb=hsb: e.matmul(
                                bank[0:1, :], lhsT=e127_bf[:],
                                rhs=hsb[:, seg * 512:(seg + 1) * 512],
                                start=True, stop=True),
                             r=(hk, "e127_bf"), w=(bkk,), tag="cext")
                        P.op("act", lambda e, seg=seg, bank=bank: e.copy(
                                out=carry[0:1, seg * 512:(seg + 1) * 512],
                                in_=bank[0:1, :]),
                             r=(bkk,), w=("carry",), tag="cev")


            def c_post_stage(c, m):
                par = c % 2
                xk = f"xC{par}"
                xt = xC[par]
                mpar = m % 2
                hk = f"hs_bf{mpar}"
                hsb = hs_bf[mpar]
                xv = xt[:, m * 1024:(m + 1) * 1024]
                transpose_pack(lambda j, hsb=hsb: hsb[:, j * 128:(j + 1) * 128],
                               (hk,), pb[0], "PB0")
                P.op("act", lambda e: e.copy(out=hsT[:, 0:1024], in_=pb[0][:]),
                     r=("PB0",), w=("hsT",), tag="hsTev1")
                transpose_pack(lambda j, hsb=hsb: hsb[:, 1024 + j * 128:
                                                      1024 + (j + 1) * 128],
                               (hk,), pb[1], "PB1")
                P.op("act", lambda e: e.copy(out=hsT[:, 1024:2048],
                                             in_=pb[1][:]),
                     r=("PB1",), w=("hsT",), tag="hsTev2")
                for hf in range(2):
                    bank = pf[4 + hf]
                    bkk = f"PF{4 + hf}"
                    for jh in range(8):
                        P.op("pe", lambda e, jh=jh, hf=hf, bank=bank: e.matmul(
                                bank[:], lhsT=hsT[:, jh * 128:(jh + 1) * 128],
                                rhs=cre_sb[:, jh * 1024 + hf * 512:
                                           jh * 1024 + hf * 512 + 512],
                                start=(jh == 0), stop=False),
                             r=("hsT", "cre"), w=(bkk,), tag="cmmre")
                    for jh in range(8):
                        P.op("pe", lambda e, jh=jh, hf=hf, bank=bank: e.matmul(
                                bank[:],
                                lhsT=hsT[:, 1024 + jh * 128:
                                         1024 + (jh + 1) * 128],
                                rhs=cim_sb[:, jh * 1024 + hf * 512:
                                           jh * 1024 + hf * 512 + 512],
                                start=False, stop=(jh == 7)),
                             r=("hsT", "cim"), w=(bkk,), tag="cmmim")
                    P.op("act", lambda e, hf=hf, m=m, par=par, xv=xv:
                         e.activation(
                            out=skscr[:, hf * 512:(hf + 1) * 512],
                            in_=xv[:, hf * 512:(hf + 1) * 512],
                            func=AF.Identity,
                            scale=lnstC[par][:, 2 * m:2 * m + 1],
                            bias=lnstC[par][:, 2 * m + 1:2 * m + 2]),
                         r=(xk, f"lnstC{par}"), w=("skscr",), tag="skln")
                    P.op("dve", lambda e, hf=hf: e.tensor_tensor(
                            out=sksc2[:, hf * 512:(hf + 1) * 512],
                            in0=drep[:, hf * 512:(hf + 1) * 512],
                            in1=skscr[:, hf * 512:(hf + 1) * 512], op=ALU.mult),
                         r=("drep", "skscr"), w=("sksc2",), tag="skip")
                    P.op("dve", lambda e, hf=hf, bank=bank: e.tensor_tensor(
                            out=yC[:, hf * 512:(hf + 1) * 512], in0=bank[:],
                            in1=sksc2[:, hf * 512:(hf + 1) * 512], op=ALU.add),
                         r=(bkk, "sksc2"), w=("yC",), tag="cout")
                ln(yC[:], ("yC",), [(lnscr2[:], "lnscr2")])
                P.op("dve", lambda e, xv=xv: e.tensor_tensor(
                        out=xv, in0=xv, in1=lnscr2[:], op=ALU.add),
                     r=("lnscr2",), w=(xk,), tag="xaddC")

            def c_leadin(c):
                par = c % 2
                xk = f"xC{par}"
                xt = xC[par]
                ntk = f"n1TC{par}"
                src = xst_d[c * CH:(c + 1) * CH, :].rearrange(
                    "(m p) d -> p m d", p=128)
                P.dma("sp", lambda e, xt=xt, src=src: e.dma_start(
                        out=xt[:].rearrange("p (m d) -> p m d", m=MT), in_=src),
                      r=(f"xst{c}",), w=(xk,), tag=f"ldxC{c}")
                stk = f"lnstC{par}"
                for m in range(MT):
                    xv = xt[:, m * 1024:(m + 1) * 1024]
                    ln_stats(xv, (xk,), lnstC[par], stk, m)
                for m in range(MT):
                    xv = xt[:, m * 1024:(m + 1) * 1024]
                    nb = nbfC2[m % 2]
                    nbk = f"nbfC{m % 2}"
                    ln_apply(xv, (xk,), nb[:], nbk, lnstC[par], stk, m)
                    tb = pb[m % 2]
                    transpose_pack(lambda j, nb=nb: nb[:, j * 128:(j + 1) * 128],
                                   (nbk,), tb, f"PB{m % 2}")
                    P.op("act", lambda e, m=m, tb=tb, par=par: e.copy(
                            out=n1TC[par][:, m * 1024:(m + 1) * 1024],
                            in_=tb[:]),
                         r=(f"PB{m % 2}",), w=(ntk,), tag="n1Tev")

            def c_store(c):
                par = c % 2
                xk = f"xC{par}"
                xt = xC[par]
                dst = xst_d[c * CH:(c + 1) * CH, :].rearrange(
                    "(m p) d -> p m d", p=128)
                P.dma("act", lambda e, xt=xt, dst=dst: e.dma_start(
                        out=dst, in_=xt[:].rearrange("p (m d) -> p m d", m=MT)),
                      r=(xk,), w=(f"xst{c}",), tag=f"stxC{c}")

            c_leadin(0)
            for c in range(NCH):
                c_scan_stage(c, 0)
                if c > 0:
                    c_post_stage(c - 1, 3)
                    c_store(c - 1)
                c_scan_stage(c, 1)
                c_post_stage(c, 0)
                if c + 1 < NCH:
                    c_leadin(c + 1)
                c_scan_stage(c, 2)
                c_post_stage(c, 1)
                c_scan_stage(c, 3)
                c_post_stage(c, 2)
            c_post_stage(NCH - 1, 3)
            c_store(NCH - 1)
            P.flush("C")
            cctx.close()
            P.barrier()

        if "D" in phases:
            dctx = ExitStack()
            dsb = lambda name, shp, dt: dctx.enter_context(
                nc.sbuf_tensor(name, shp, dt))
            n3Tgs = [dsb(f"n3Tg{i}", [128, 8192], BF16) for i in range(2)]
            nbfD2 = [dsb(f"nbfD{i}", [128, 1024], BF16) for i in range(2)]
            stD = dsb("stD", [128, 16], F32)
            f_g = dsb("f_g", [128, 8192], F32)
            gatess = [dsb(f"gates{i}", [128, 64], F32) for i in range(2)]
            w1d = [dsb(f"w1d{i}", [128, 8192], BF16) for i in range(2)]
            w3d = [dsb(f"w3d{i}", [128, 8192], BF16) for i in range(2)]
            w2d = dsb("w2d", [128, 8192], BF16)
            rw_sb = dsb("rw_sb", [128, 64], BF16)
            ebrep = dsb("ebrep", [128, 8], F32)
            ebrow = dsb("ebrow", [1, 8], F32)
            xmt = [dsb(f"xmt{i}", [128, 1024], F32) for i in range(4)]
            xrt2 = [dsb(f"xrt{i}", [128, 1024], F32) for i in range(2)]
            y1s = dsb("y1s", [128, 1024], F32)
            mid = dsb("mid", [128, 4096], BF16)
            gz = dsb("gz", [128, 8], F32)
            gp = dsb("gp", [128, 8], F32)
            gt = dsb("gt", [128, 8], F32)
            gt2 = dsb("gt2", [128, 8], F32)
            gtop1 = dsb("gtop1", [128, 8], F32)
            gtop2 = dsb("gtop2", [128, 8], F32)
            gmx = dsb("gmx", [128, 1], F32)
            gnm = dsb("gnm", [128, 1], F32)

            P.dma("pool", lambda e: e.dma_start(
                    out=rw_sb[:].rearrange("p (j x) -> p j x", j=8),
                    in_=DI["rw"].rearrange("(j p) x -> p j x", p=128)),
                  w=("rw",), tag="ld_rw")
            ldc("ebias_row", ebrow)
            P.op("pe", lambda e: e.matmul(pf[5][:, 0:8], lhsT=onesrow_f[:],
                                          rhs=ebrow[:], start=True, stop=True),
                 r=("ebias_row", "onesrow_f"), w=("PF5",), tag="ebmm")
            P.op("act", lambda e: e.copy(out=ebrep[:], in_=pf[5][:, 0:8]),
                 r=("PF5",), w=("ebrep",), tag="ebcp")

            exlist = [("sh", "sw1", "sw2", "sw3", None)] + [
                (f"e{ee}", "w1", "w2", "w3", ee) for ee in range(8)]

            def d_li_stats(g):
                for mt in range(8):
                    xb = xmt[mt % 4]
                    xbk = f"xmt{mt % 4}"
                    src = xst_d[g * GT + mt * 128:g * GT + (mt + 1) * 128, :]
                    P.dma("sp", lambda e, xb=xb, src=src: e.dma_start(
                            out=xb[:], in_=src),
                          r=(f"xst{2 * g + mt // 4}",), w=(xbk,),
                          tag=f"ldxD{g}_{mt}")
                    ln_stats(xb[:], (xbk,), stD, "stD", mt)

            def d_li_applyall(g):
                gpar = g % 2
                n3 = n3Tgs[gpar]
                n3k = f"n3Tg{gpar}"
                gat = gatess[gpar]
                gatk = f"gates{gpar}"
                def d_li_apply(mt):
                    xb = xmt[mt % 4]
                    xbk = f"xmt{mt % 4}"
                    src = xst_d[g * GT + mt * 128:g * GT + (mt + 1) * 128, :]
                    P.dma("sp", lambda e, xb=xb, src=src: e.dma_start(
                            out=xb[:], in_=src),
                          r=(f"xst{2 * g + mt // 4}",), w=(xbk,),
                          tag=f"ldxD2_{g}_{mt}")
                    nb = nbfD2[mt % 2]
                    nbk = f"nbfD{mt % 2}"
                    ln_apply(xb[:], (xbk,), nb[:], nbk, stD, "stD", mt)
                    tb = pb[mt % 2]
                    transpose_pack(lambda j, nb=nb: nb[:, j * 128:(j + 1) * 128],
                                   (nbk,), tb, f"PB{mt % 2}")
                    dv = n3[:].rearrange("p (j t) -> p j t", j=8)[
                        :, :, mt * 128:(mt + 1) * 128]
                    sv = tb[:].rearrange("p (j t) -> p j t", j=8)
                    P.op("dve", lambda e, dv=dv, sv=sv: e.tensor_copy(dv, sv),
                         r=(f"PB{mt % 2}",), w=(n3k,), tag="n3Tgev")
                    for j in range(8):
                        P.op("pe", lambda e, j=j, mt=mt, n3=n3: e.matmul(
                                pf[4][:, 0:8],
                                lhsT=n3[:, j * 1024 + mt * 128:
                                        j * 1024 + mt * 128 + 128],
                                rhs=rw_sb[:, j * 8:(j + 1) * 8],
                                start=(j == 0), stop=(j == 7)),
                             r=(n3k, "rw"), w=("PF4",), tag="rmm")
                    P.op("dve", lambda e: e.tensor_reduce(
                            out=gmx[:], in_=pf[4][:, 0:8], axis=AX, op=ALU.max),
                         r=("PF4",), w=("gmx",), tag="gmx", swb=True)
                    P.op("dve", lambda e: e.tensor_scalar(
                            out=gnm[:], in0=gmx[:], scalar1=-1.0, scalar2=None,
                            op0=ALU.mult),
                         r=("gmx",), w=("gnm",), tag="gnm", early=True)
                    P.op("act", lambda e: e.activation(
                            out=gz[:], in_=pf[4][:, 0:8], func=AF.Exp,
                            bias=gnm[:]),
                         r=("PF4", "gnm"), w=("gz",), tag="gexp")
                    P.op("dve", lambda e: e.tensor_reduce(
                            out=gmx[:], in_=gz[:], axis=AX, op=ALU.add),
                         r=("gz",), w=("gmx",), tag="gsum", swb=True)
                    P.op("dve", lambda e: e.reciprocal(out=gnm[:], in_=gmx[:]),
                         r=("gmx",), w=("gnm",), tag="grs", swb=True)
                    P.op("dve", lambda e: e.tensor_scalar(
                            out=gp[:], in0=gz[:], scalar1=gnm[:], scalar2=None,
                            op0=ALU.mult),
                         r=("gz", "gnm"), w=("gp",), tag="gp", early=True)
                    P.op("dve", lambda e: e.tensor_tensor(
                            out=gt[:], in0=gp[:], in1=ebrep[:], op=ALU.add),
                         r=("gp", "ebrep"), w=("gt",), tag="gt")
                    P.op("dve", lambda e: e.tensor_reduce(
                            out=gmx[:], in_=gt[:], axis=AX, op=ALU.max),
                         r=("gt",), w=("gmx",), tag="gm1", swb=True)
                    P.op("dve", lambda e: e.tensor_scalar(
                            out=gtop1[:], in0=gt[:], scalar1=gmx[:], scalar2=None,
                            op0=ALU.is_ge),
                         r=("gt", "gmx"), w=("gtop1",), tag="gtop1", early=True)
                    P.op("dve", lambda e: e.scalar_tensor_tensor(
                            out=gt2[:], in0=gtop1[:], scalar=-1e9, in1=gt[:],
                            op0=ALU.mult, op1=ALU.add),
                         r=("gtop1", "gt"), w=("gt2",), tag="gt2")
                    P.op("dve", lambda e: e.tensor_reduce(
                            out=gmx[:], in_=gt2[:], axis=AX, op=ALU.max),
                         r=("gt2",), w=("gmx",), tag="gm2", swb=True)
                    P.op("dve", lambda e: e.tensor_scalar(
                            out=gtop2[:], in0=gt2[:], scalar1=gmx[:], scalar2=None,
                            op0=ALU.is_ge),
                         r=("gt2", "gmx"), w=("gtop2",), tag="gtop2", early=True)
                    P.op("dve", lambda e: e.tensor_tensor(
                            out=gtop1[:], in0=gtop1[:], in1=gtop2[:], op=ALU.add),
                         r=("gtop2",), w=("gtop1",), tag="gmask")
                    P.op("dve", lambda e, mt=mt, gat=gat: e.tensor_tensor(
                            out=gat[:, mt * 8:(mt + 1) * 8], in0=gp[:],
                            in1=gtop1[:], op=ALU.mult),
                         r=("gp", "gtop1"), w=(gatk,), tag="gates")

                for mt in range(8):
                    d_li_apply(mt)

            def d_experts(g):
                gpar = g % 2
                n3 = n3Tgs[gpar]
                n3k = f"n3Tg{gpar}"
                gat = gatess[gpar]
                gatk = f"gates{gpar}"
                for xi, (ename, k1, k2, k3, ee) in enumerate(exlist):
                    par = (g * 9 + xi) % 2
                    for knm, wt, wkey in ((k1, w1d[par], f"w1d{par}"),
                                          (k3, w3d[par], f"w3d{par}"),
                                          (k2, w2d, "w2d")):
                        d = DI[knm] if ee is None else DI[knm][ee]
                        P.dma("pool", lambda e, wt=wt, d=d: e.dma_start(
                                out=wt[:].rearrange("p (j o) -> p j o", j=8),
                                in_=d.rearrange("(j p) o -> p j o", p=128)),
                              w=(wkey,), tag=f"ldw_{ename}_{knm}")
                    for u in range(2):
                        for jh in range(8):
                            b1 = pf[jh % 2]
                            b1k = f"PF{jh % 2}"
                            b3 = pf[2 + jh % 2]
                            b3k = f"PF{2 + jh % 2}"
                            for jd in range(8):
                                P.op("pe", lambda e, jd=jd, jh=jh, u=u, par=par,
                                     b1=b1, n3=n3: e.matmul(
                                        b1[:],
                                        lhsT=w1d[par][:, jd * 1024 + jh * 128:
                                                      jd * 1024 + jh * 128 + 128],
                                        rhs=n3[:, jd * 1024 + u * 512:
                                               jd * 1024 + u * 512 + 512],
                                        start=(jd == 0), stop=(jd == 7)),
                                     r=(f"w1d{par}", n3k), w=(b1k,), tag="y1mm")
                            P.op("act", lambda e, jh=jh, b1=b1: e.activation(
                                    out=y1s[:, (jh % 2) * 512:(jh % 2) * 512 + 512],
                                    in_=b1[:], func=AF.Silu),
                                 r=(b1k,), w=("y1s",), tag="silu")
                            for jd in range(8):
                                P.op("pe", lambda e, jd=jd, jh=jh, u=u, par=par,
                                     b3=b3, n3=n3: e.matmul(
                                        b3[:],
                                        lhsT=w3d[par][:, jd * 1024 + jh * 128:
                                                      jd * 1024 + jh * 128 + 128],
                                        rhs=n3[:, jd * 1024 + u * 512:
                                               jd * 1024 + u * 512 + 512],
                                        start=(jd == 0), stop=(jd == 7)),
                                     r=(f"w3d{par}", n3k), w=(b3k,), tag="y3mm")
                            P.op("dve", lambda e, jh=jh, b3=b3: e.tensor_tensor(
                                    out=mid[:, jh * 512:(jh + 1) * 512],
                                    in0=b3[:],
                                    in1=y1s[:, (jh % 2) * 512:(jh % 2) * 512 + 512],
                                    op=ALU.mult),
                                 r=(b3k, "y1s"), w=("mid",), tag="mid")
                        for mt2 in range(4):
                            tile = u * 4 + mt2
                            for dh in range(2):
                                bo = pf[4 + dh]
                                bok = f"PF{4 + dh}"
                                for jh in range(8):
                                    P.op("pe", lambda e, jh=jh, mt2=mt2, dh=dh,
                                         bo=bo: e.matmul(
                                            bo[:],
                                            lhsT=mid[:, jh * 512 + mt2 * 128:
                                                     jh * 512 + mt2 * 128 + 128],
                                            rhs=w2d[:, jh * 1024 + dh * 512:
                                                    jh * 1024 + dh * 512 + 512],
                                            start=(jh == 0), stop=(jh == 7)),
                                         r=("mid", "w2d"), w=(bok,),
                                         tag="y2mm")
                                fsl = f_g[:, tile * 1024 + dh * 512:
                                          tile * 1024 + dh * 512 + 512]
                                if ee is None:
                                    P.op("dve", lambda e, fsl=fsl, bo=bo:
                                         e.tensor_copy(fsl, bo[:]),
                                         r=(bok,), w=("f_g",), tag="fsh")
                                else:
                                    gcol = gat[:, tile * 8 + ee:
                                               tile * 8 + ee + 1]
                                    P.op("dve", lambda e, fsl=fsl, bo=bo,
                                         gcol=gcol: e.scalar_tensor_tensor(
                                            out=fsl, in0=bo[:], scalar=gcol,
                                            in1=fsl, op0=ALU.mult, op1=ALU.add),
                                         r=(bok, gatk, "f_g"), w=("f_g",),
                                         tag="facc", early=True)

            def d_tail(g):
                for mt in range(8):
                    xr = xrt2[mt % 2]
                    xrk = f"xrt{mt % 2}"
                    src = xst_d[g * GT + mt * 128:g * GT + (mt + 1) * 128, :]
                    P.dma("sp", lambda e, xr=xr, src=src: e.dma_start(
                            out=xr[:], in_=src),
                          r=(f"xst{2 * g + mt // 4}",), w=(xrk,),
                          tag=f"ldxT{g}_{mt}")
                    ln(f_g[:, mt * 1024:(mt + 1) * 1024], ("f_g",),
                       [(lnscr2[:], "lnscr2")])
                    P.op("dve", lambda e, xr=xr: e.tensor_tensor(
                            out=xr[:], in0=xr[:], in1=lnscr2[:], op=ALU.add),
                         r=("lnscr2",), w=(xrk,), tag="xaddD")
                    dst = out_d[g * GT + mt * 128:g * GT + (mt + 1) * 128, :]
                    P.dma("act", lambda e, xr=xr, dst=dst: e.dma_start(
                            out=dst, in_=xr[:]),
                          r=(xrk,), w=("out_dram",), tag=f"stD{g}_{mt}")

            d_li_stats(0)
            d_li_applyall(0)
            d_li_stats(1)
            d_li_applyall(1)
            for g in range(NG):
                if g + 2 < NG:
                    d_li_stats(g + 2)
                d_experts(g)
                if g + 2 < NG:
                    d_li_applyall(g + 2)
                d_tail(g)
            P.flush("D")
            dctx.close()

        if "DUMP" in phases:
            for c in range(NCH):
                P.dma("sp", lambda e, c=c: e.dma_start(
                        out=out_d[c * CH:(c + 1) * CH, :],
                        in_=xst_d[c * CH:(c + 1) * CH, :]),
                      r=(f"xst{c}",), w=("out_dram",), tag=f"dump{c}")
            P.flush("DUMP")

        P.op("sp", lambda e: e.nop(),
             r=("out_dram", "dbg_dram") if (DBG or DBG2) else ("out_dram",),
             tag="fence")
        P.flush("FIN")

    return nc


def kernel(**inputs):
    common = _prep_common(inputs)
    x_full = np.ascontiguousarray(np.asarray(inputs["L0_attn_x"], dtype=np.float32))
    nc = build_nc()
    in_maps = []
    for core in range(8):
        m = dict(common)
        m["x0"] = np.ascontiguousarray(x_full[core])
        in_maps.append(m)
    res = run_bass_kernel_spmd(nc, in_maps, core_ids=list(range(8)))
    return np.stack([np.asarray(r["out"], dtype=np.float32)
                     for r in res.results], axis=0)
